# version 02
# baseline (speedup 1.0000x reference)
import numpy as np
import concourse.bass as bass
import concourse.mybir as mybir
from concourse.bass_utils import run_bass_kernel_spmd

F32 = mybir.dt.float32
F32R = mybir.dt.float32r
BF16 = mybir.dt.bfloat16
I32 = mybir.dt.int32
AF = mybir.ActivationFunctionType
ALU = mybir.AluOpType
AXX = mybir.AxisListType.X

H = 1024
HC = 8
T = 2048
KW = 16
U = T + KW - 1
UPAD = 17 * 128
EPS = 1e-5
NUT = 17
NTOK = 16
NUNIT = 4
MAGIC = 0x5f3759df

XTOK_RING = 5
XU_RING = 5
XN_RING = 5
GXN_RING = 6


def build_nc(debug=False):
    nc = bass.Bass("TRN2", target_bir_lowering=False)

    x_d = nc.dram_tensor("x_slice", [UPAD, H], F32, kind="ExternalInput").ap()
    w1_d = nc.dram_tensor("w1", [H, H], F32, kind="ExternalInput").ap()
    w2_d = nc.dram_tensor("w2", [H, H], F32, kind="ExternalInput").ap()
    vr_d = nc.dram_tensor("v_r", [8, 128], F32, kind="ExternalInput").ap()
    wrt_d = nc.dram_tensor("wrt", [H, 8], F32, kind="ExternalInput").ap()
    a_d = nc.dram_tensor("A", [8, 64, H], F32, kind="ExternalInput").ap()
    bw_d = nc.dram_tensor("Bw", [8, H, 64], F32, kind="ExternalInput").ap()
    c_d = nc.dram_tensor("c_row", [1, H], F32, kind="ExternalInput").ap()
    wmod_d = nc.dram_tensor("wmod", [H, 6 * H], F32, kind="ExternalInput").ap()
    mv_d = nc.dram_tensor("mvec", [UPAD, 1], F32, kind="ExternalInput").ap()
    xg_d = nc.dram_tensor("xg", [9, H], F32, kind="ExternalInput").ap()
    id_d = nc.dram_tensor("ident", [128, 128], F32, kind="ExternalInput").ap()
    ones_d = nc.dram_tensor("onescol", [1, 128], F32, kind="ExternalInput").ap()
    out_d = nc.dram_tensor("out", [T, H], F32, kind="ExternalOutput").ap()

    WSCW = 128
    WSCR = 15 + T + 16
    wsc_d = nc.dram_tensor("wsc", [WSCR, WSCW], BF16, kind="Internal").ap()
    rowsc_d = nc.dram_tensor("rowsc", [6, H], F32, kind="Internal").ap()
    modsc_d = nc.dram_tensor("modsc", [12, 512], F32, kind="Internal").ap()

    dbg = {}
    def dbg_t(nm, shp):
        if debug:
            dbg[nm] = nc.dram_tensor(nm, shp, F32, kind="ExternalOutput").ap()
        return dbg.get(nm)
    for nm, shp in [("d_mod", [128, 48]), ("d_fmp", [128, 48]), ("d_xn", [UPAD, H]),
                    ("d_gxn", [UPAD, H]), ("d_p", [128, HC * 528]), ("d_q", [128, HC * 512]),
                    ("d_sc", [128, 64]), ("d_w", [128, 64]), ("d_x2", [T, H]),
                    ("d_gates", [1, 8]), ("d_h2", [128, HC * 128]), ("d_mid", [128, 512]),
                    ("d_gdbg", [1, 48]), ("d_arg", [128, 1024]), ("d_th", [128, 1024]),
                    ("d_sv", [128, 4]), ("d_alh", [128, 4096]), ("d_bwt", [128, 4096]),
                    ("d_att", [128, 1024]), ("d_pcA", [64, 96]), ("d_pcX", [15, 96]),
                    ("d_pcB", [64, 96]), ("d_pcY", [15, 96]), ("d_gbc", [128, 8])]:
        dbg_t(nm, shp)

    S = lambda name, shape, dt=F32: nc.alloc_sbuf_tensor(name, shape, dt)
    xtok = [S(f"xtok{j}", [128, H]) for j in range(XTOK_RING)]
    xu = [S(f"xu{j}", [128, H]) for j in range(XU_RING)]
    xnr = [S(f"xn{i}", [128, H], BF16) for i in range(XN_RING)]
    gxn = [S(f"gxn{j}", [128, H], BF16) for j in range(GXN_RING)]
    hpad = S("hpad", [128, HC * 640], BF16)
    p_s = S("p_s", [128, HC * 528])
    q_s = S("q_s", [128, HC * 512])
    argb = [S(f"arg{i}", [128, 1024]) for i in range(2)]
    thb = [S(f"th{i}", [128, 1024], BF16) for i in range(2)]
    w1b = S("w1b", [128, HC * H], BF16)
    w2b = S("w2b", [128, HC * H], BF16)
    alh = S("alh", [128, 32 * 128], BF16)
    bwt = S("bwt", [128, 4 * H], BF16)
    midb = S("midb", [128, 512], BF16)
    xn2tm = S("xn2tm", [128, H], BF16)
    xn2fm = S("xn2fm", [128, HC * 128], BF16)
    h2fm = S("h2fm", [128, HC * 128], BF16)
    atmp = [S(f"atmp{i}", [64, H], BF16) for i in range(2)]
    bwtmp = [S(f"bwtmp{i}", [128, 64]) for i in range(2)]
    bwtmp2 = [S(f"bwtmp2{i}", [128, 64], BF16) for i in range(2)]

    modfm = S("modfm", [128, 48])
    opca = S("opca", [128, 8])
    opcb = S("opcb", [128, 8])
    gapr = S("gapr", [128, 8])
    gbpr = S("gbpr", [128, 8])
    fmpack = S("fmpack", [128, 48])
    rowa = S("rowa", [1, H])
    rowb = S("rowb", [1, H])
    padbc = S("padbc", [128, H], BF16)
    gcbc = S("gcbc", [128, H], BF16)
    gsbc = S("gsbc", [128, H], BF16)
    mcol = S("mcol", [128, 17])
    silfm_t = S("silfm_t", [128, 8], F32R)
    c2fm = S("c2fm", [128, 8])
    c1fm = S("c1fm", [128, 8])
    safm = S("safm", [128, 8])
    wmst_t = [S(f"wmst_t{i}", [128, 512], F32R) for i in range(2)]
    vb16 = S("vb16", [128, 8], BF16)
    vf32 = S("vf32", [128, 8])
    wrts = S("wrts", [128, 64])
    sqd = S("sqd", [128, H], BF16)
    statb = S("statb", [128, 28])
    st2 = S("st2", [128, 32])
    nsc = S("nsc", [128, 44])
    negmax = S("negmax", [128, 4])
    sumexp = S("sumexp", [128, 4])
    rsum = S("rsum", [128, 4])
    expw = S("expw", [128, 64])
    wtm = S("wtm", [128, 64], BF16)
    pcA = [S(f"pcA{i}", [64, 96], BF16) for i in range(2)]
    pcB = [S(f"pcB{i}", [128, 96], BF16) for i in range(2)]
    pcX = [S(f"pcX{i}", [79, 96], BF16) for i in range(2)]
    pcY = [S(f"pcY{i}", [15, 96], BF16) for i in range(2)]

    xg_s = S("xg_s", [9, H])
    hpadgT = argb[1][0:16, 0:HC * 128]
    vtmp = argb[1][32:40, 0:128]
    gs1 = S("gs1", [9, 8])
    hpadg = S("hpadg", [128, 136])
    projg = S("projg", [128, 128])
    qgev = S("qgev", [128, 8])
    argg = S("argg", [128, 128])
    thg = S("thg", [128, 128])
    scg = S("scg", [16, 1])
    scrow = S("scrow", [1, 16])
    wgrow = S("wgrow", [1, 16])
    wgT = S("wgT", [16, 1])
    mrow = argb[0][0:1, 0:H]
    attng = mrow; x2g = mrow; h2g = mrow
    gls = S("gls", [1, 48])
    h2gfm = S("h2gfm", [128, 8])
    gatesrow = S("gatesrow", [1, 8])
    gbc = S("gbc", [128, 8])
    sv = S("sv", [128, 4])
    idn = S("idn", [128, 128])
    idnb = S("idnb", [128, 128], BF16)
    onesc = S("onesc", [1, 128])
    zrow = S("zrow", [128, 128], BF16)

    wstA = p_s[:, 0:1024]
    wstB = p_s[:, 1024:2048]

    modbuf = bass.AP(q_s, 0, [[HC * 512, 1], [1, 512]])

    pmm = [nc.alloc_psum_tensor(f"pmm{i}", [128, 512], F32) for i in range(4)]
    psc = nc.alloc_psum_tensor("psc", [128, 64], F32)
    pms = [nc.alloc_psum_tensor(f"pms{i}", [128, 512], F32) for i in range(2)]
    pmini = nc.alloc_psum_tensor("pmini", [128, 512], F32)

    ENGS = ("sync", "gpsimd", "tensor", "vector", "scalar")
    prog = {e: [] for e in ENGS}
    semnames = ["ld", "ldg", "xb", "pe", "ve", "ac", "st", "wv"]
    cnt = {s: 0 for s in semnames}
    SEM = {}

    import inspect
    tags = {e: [] for e in ENGS}

    class Eng:
        def __init__(self, name):
            self.name = name
        def emit(self, fn, sem=None, n=1):
            tags[self.name].append(inspect.stack()[1].lineno)
            if sem is None:
                prog[self.name].append(fn)
                return None
            cnt[sem] += n
            v = cnt[sem]
            prog[self.name].append(lambda e, f=fn, s=sem, k=n: f(e).then_inc(SEM[s], k))
            return v
        def wait(self, sem, val):
            tags[self.name].append(-1)
            assert val is not None and val <= cnt[sem], \
                f"[{self.name}] wait {sem}>={val} emitted={cnt[sem]}"
            prog[self.name].append(lambda e, s=sem, v=val: e.wait_ge(SEM[s], v))

    sy, gp, te, ve, ac = (Eng(n) for n in ENGS)

    def vf(fn):
        v = ve.emit(fn, "ve")
        ve.wait("ve", v)
        return v

    def dma(E_, out, in_, sem="ld", **kw):
        return E_.emit(lambda e, o=out, i=in_, kww=dict(kw): e.dma_start(out=o, in_=i, **kww),
                       sem, 16)

    def dmat(E_, out, in_, sem="xb"):
        return E_.emit(lambda e, o=out, i=in_: e.dma_start(out=o, in_=i, transpose=True),
                       sem, 16)

    RAP = bass.AP

    def newton_rsqrt(dst, src, ncols, emit_sem=None, rows=128):
        s0 = nsc[0:rows, 0:ncols]
        s1 = nsc[0:rows, 22:22 + ncols]
        vf(lambda e: e.tensor_single_scalar(out=s1, in_=src, scalar=1.0, op=ALU.add))
        vf(lambda e: e.reciprocal(out=s0, in_=s1))
        for _ in range(6):
            vf(lambda e: e.tensor_tensor(out=s1, in0=s0, in1=s0, op=ALU.mult))
            vf(lambda e: e.tensor_tensor(out=s1, in0=s1, in1=src, op=ALU.mult))
            vf(lambda e: e.tensor_scalar(out=s1, in0=s1, scalar1=-0.5, scalar2=1.5,
                                         op0=ALU.mult, op1=ALU.add))
            vf(lambda e: e.tensor_tensor(out=s0, in0=s0, in1=s1, op=ALU.mult))
        v = vf(lambda e: e.tensor_copy(out=dst, in_=s0))
        if emit_sem is None:
            return v
        cnt[emit_sem] += 1
        vv = cnt[emit_sem]
        prog["vector"].append(lambda e, s=emit_sem: e.tensor_copy(
            out=nsc[0:1, 40:41], in_=nsc[0:1, 40:41]).then_inc(SEM[s], 1))
        return vv

    bank_free = {}
    def pe_bank(nm):
        if nm in bank_free:
            te.wait(bank_free[nm][0], bank_free[nm][1])
    def set_bank_free(nm, sem, val):
        bank_free[nm] = (sem, val)

    dma(sy, idn[:], id_d)
    dma(sy, onesc[:], ones_d)
    dma(sy, mcol[:], RAP(mv_d.tensor, 0, [[1, 128], [128, 17]]),
        allow_slow_non_contiguous=True)
    dma(sy, xg_s[:], xg_d)
    dma(sy, rowa[:], c_d)
    dma(sy, vtmp[:], vr_d)
    ld_small = dma(sy, RAP(wrts, 0, [[64, 128], [8, 8], [1, 8]]),
                   RAP(wrt_d.tensor, 0, [[8, 128], [1024, 8], [1, 8]]),
                   allow_slow_non_contiguous=True)

    z_ve = ve.emit(lambda e: e.memset(zrow[:], 0.0), "ve")
    gp.wait("ve", z_ve)
    dma(gp, idnb[:], id_d, sem="ldg")
    zw_last = None
    for zi in range(0, WSCR, 128):
        rows = min(128, WSCR - zi)
        zw_last = dma(gp, wsc_d[zi:zi + rows, :], zrow[0:rows, :], sem="ldg")

    ac.wait("ld", ld_small)
    sil_a = ac.emit(lambda e: e.activation(out=rowa[:], in_=rowa[:], func=AF.Silu), "ac")
    te.wait("ac", sil_a)
    t_sil = None
    for c in range(8):
        t_sil = te.emit(lambda e, c=c: e.transpose(
            pmini[0:128, c:c + 1], rowa[0:1, 128 * c:128 * (c + 1)], idn[0:1, 0:1]),
            "pe" if c == 7 else None)
    ve.wait("pe", t_sil)
    silfm = silfm_t[:]
    ve.emit(lambda e: e.tensor_copy(out=vf32[:], in_=pmini[0:128, 0:8]))
    v_silr = ve.emit(lambda e: e.tensor_copy(out=silfm, in_=pmini[0:128, 0:8]), "ve")
    set_bank_free("pmini", "ve", v_silr)

    te.wait("ve", v_silr)
    wm_ld = {}
    wm_pe = {}
    st_mod = {}
    flat = [(g, c) for g in range(12) for c in range(8)]
    for i, (g, c) in enumerate(flat):
        if i >= 2:
            gp.wait("pe", wm_pe[i - 2])
        wm_ld[i] = dma(gp, wmst_t[i % 2][:],
                       wmod_d[128 * c:128 * (c + 1), 512 * g:512 * (g + 1)], sem="ldg")
        bank = f"pms{g % 2}"
        if c == 0:
            pe_bank(bank)
        te.wait("ldg", wm_ld[i])
        wm_pe[i] = te.emit(lambda e, g=g, c=c, i=i: e.matmul(
            pms[g % 2][0:1, :], silfm[:, c:c + 1], wmst_t[i % 2][:],
            start=(c == 0), stop=(c == 7)), "pe")
        if c == 7:
            ve.wait("pe", wm_pe[i])
            if g > 0:
                ve.wait("ldg", st_mod[g - 1])
            v_m = ve.emit(lambda e, g=g: e.tensor_copy(out=modbuf, in_=pms[g % 2][0:1, :]), "ve")
            set_bank_free(bank, "ve", v_m)
            gp.wait("ve", v_m)
            st_mod[g] = dma(gp, modsc_d[g:g + 1, :], modbuf, sem="ldg")

    sy.wait("ldg", st_mod[11])
    ld_modfm = dma(sy, RAP(modfm, 0, [[48, 128], [8, 6], [1, 8]]),
                   RAP(modsc_d.tensor, 0, [[1, 128], [1024, 6], [128, 8]]),
                   allow_slow_non_contiguous=True)

    ve.wait("ld", ld_modfm)
    vf(lambda e: e.tensor_single_scalar(out=opca[:], in_=modfm[:, 8:16], scalar=1.0, op=ALU.add))
    vf(lambda e: e.tensor_single_scalar(out=opcb[:], in_=modfm[:, 32:40], scalar=1.0, op=ALU.add))

    def clamp_fm(dstt, srcsl):
        vf(lambda e: e.tensor_scalar(out=nsc[:, 0:8], in0=srcsl, scalar1=0.0, scalar2=2.0,
                                     op0=ALU.is_ge, op1=ALU.mult))
        vf(lambda e: e.tensor_single_scalar(out=nsc[:, 0:8], in_=nsc[:, 0:8],
                                            scalar=1.0, op=ALU.subtract))
        vf(lambda e: e.tensor_single_scalar(out=nsc[:, 8:16], in_=srcsl, scalar=-1.0,
                                            op=ALU.mult))
        vf(lambda e: e.tensor_tensor(out=nsc[:, 8:16], in0=nsc[:, 8:16], in1=srcsl,
                                     op=ALU.max))
        vf(lambda e: e.tensor_single_scalar(out=nsc[:, 8:16], in_=nsc[:, 8:16],
                                            scalar=0.25, op=ALU.max))
        vf(lambda e: e.tensor_tensor(out=dstt, in0=nsc[:, 0:8], in1=nsc[:, 8:16],
                                     op=ALU.mult))
    clamp_fm(gapr[:], modfm[:, 16:24])
    clamp_fm(gbpr[:], modfm[:, 40:48])
    vf(lambda e: e.reciprocal(out=nsc[:, 0:8], in_=opca[:]))
    vf(lambda e: e.tensor_tensor(out=fmpack[:, 0:8], in0=modfm[:, 0:8], in1=nsc[:, 0:8],
                                 op=ALU.mult))
    vf(lambda e: e.tensor_single_scalar(out=fmpack[:, 0:8], in_=fmpack[:, 0:8],
                                        scalar=-1.0, op=ALU.mult))
    vf(lambda e: e.tensor_tensor(out=fmpack[:, 8:16], in0=gapr[:], in1=opca[:], op=ALU.mult))
    vf(lambda e: e.tensor_tensor(out=fmpack[:, 16:24], in0=gapr[:], in1=modfm[:, 0:8],
                                 op=ALU.mult))
    vf(lambda e: e.tensor_copy(out=fmpack[:, 24:32], in_=gapr[:]))
    vf(lambda e: e.tensor_copy(out=fmpack[:, 32:40], in_=opcb[:]))
    v_pack = ve.emit(lambda e: e.tensor_copy(out=fmpack[:, 40:48], in_=modfm[:, 24:32]), "ve")

    te.wait("ve", v_pack)
    pe_bank("pmini")
    t_pack = te.emit(lambda e: e.transpose(pmini[0:48, 0:128], fmpack[:], idn[:]), "pe")
    ve.wait("pe", t_pack)
    v_packs = ve.emit(lambda e: e.tensor_copy(out=argb[0][0:48, 0:128], in_=pmini[0:48, 0:128]), "ve")
    set_bank_free("pmini", "ve", v_packs)
    gp.wait("ve", v_packs)
    st_pack = dma(gp, RAP(rowsc_d.tensor, 0, [[128, 48], [1, 128]]), argb[0][0:48, 0:128], sem="ldg")
    sy.wait("ldg", st_pack)

    def bcast_row(dst_sb, row_idx):
        ldr = dma(sy, rowa[:], rowsc_d[row_idx:row_idx + 1, :])
        last = None
        for hh in range(2):
            bank = f"pms{hh}"
            pe_bank(bank)
            te.wait("ld", ldr)
            mmv = te.emit(lambda e, hh=hh: e.matmul(
                pms[hh][:, :], onesc[:], rowa[0:1, 512 * hh:512 * (hh + 1)],
                start=True, stop=True), "pe")
            ve.wait("pe", mmv)
            last = ve.emit(lambda e, hh=hh, d=dst_sb: e.tensor_copy(
                out=d[:, 512 * hh:512 * (hh + 1)], in_=pms[hh][:, :]), "ve")
            set_bank_free(bank, "ve", last)
        sy.wait("pe", cnt["pe"])
        return last
    bcast_row(padbc, 0)
    bcast_row(gcbc, 1)
    v_bcast = bcast_row(gsbc, 2)

    pe_bank("pmini")
    t_v = te.emit(lambda e: e.transpose(pmini[0:128, 0:8], vtmp, idn[32:40, 32:40], tile_position=(32, 0)), "pe")
    ve.wait("pe", t_v)
    ve.emit(lambda e: e.tensor_copy(out=vb16[:], in_=pmini[0:128, 0:8]))
    v_v = ve.emit(lambda e: e.tensor_copy(out=vf32[:], in_=pmini[0:128, 0:8]), "ve")
    set_bank_free("pmini", "ve", v_v)

    if debug:
        gp.wait("ve", v_v)
        dma(gp, dbg["d_mod"], modfm[:], sem="st")
        dma(gp, dbg["d_fmp"], fmpack[:], sem="st")

    vf(lambda e: e.reduce_sum(out=gs1[:, 0:1], in_=xg_s[:], axis=AXX))
    ac.wait("ld", ld_small)
    a_msq = ac.emit(lambda e: e.activation(out=hpadgT[0:9, :], in_=xg_s[:], func=AF.Square,
                                           accum_out=gs1[:, 1:2]), "ac")
    ve.wait("ac", a_msq)
    vf(lambda e: e.tensor_single_scalar(out=gs1[:, 2:3], in_=gs1[:, 0:1],
                                        scalar=1.0 / H, op=ALU.mult))
    vf(lambda e: e.tensor_tensor(out=gs1[:, 3:4], in0=gs1[:, 2:3], in1=gs1[:, 2:3],
                                 op=ALU.mult))
    vf(lambda e: e.tensor_single_scalar(out=gs1[:, 4:5], in_=gs1[:, 1:2],
                                        scalar=1.0 / H, op=ALU.mult))
    vf(lambda e: e.tensor_tensor(out=gs1[:, 4:5], in0=gs1[:, 4:5], in1=gs1[:, 3:4],
                                 op=ALU.subtract))
    vf(lambda e: e.tensor_single_scalar(out=gs1[:, 4:5], in_=gs1[:, 4:5],
                                        scalar=EPS, op=ALU.add))
    newton_rsqrt(gs1[:, 5:6], gs1[:, 4:5], 1, rows=9)
    xngv = argb[0][64:73, 0:H]
    v_mln = ve.emit(lambda e: e.tensor_scalar(out=xngv, in0=xg_s[:],
                                              scalar1=gs1[:, 2:3], scalar2=gs1[:, 5:6],
                                              op0=ALU.subtract, op1=ALU.mult), "ve")
    te.wait("ve", v_mln)
    pe_bank("pmini")
    t_xng = None
    for c in range(8):
        t_xng = te.emit(lambda e, c=c: e.transpose(
            pmini[0:128, 9 * c:9 * (c + 1)], argb[0][64:73, 128 * c:128 * (c + 1)],
            idn[64:73, 64:73], tile_position=(64, 0)),
            "pe" if c == 7 else None)
    ve.wait("pe", t_xng)
    vf(lambda e: e.memset(hpadg[:], 0.0))
    for c in range(8):
        ve.emit(lambda e, c=c: e.tensor_scalar(
            out=hpadg[:, 17 * c + 7:17 * c + 16], in0=pmini[0:128, 9 * c:9 * (c + 1)],
            scalar1=opca[:, c:c + 1], scalar2=modfm[:, c:c + 1],
            op0=ALU.mult, op1=ALU.add))
    v_hpg = ve.emit(lambda e: e.tensor_copy(
        out=RAP(hpadg, 16, [[136, 128], [17, 8]]), in_=modfm[:, 0:8]), "ve")
    set_bank_free("pmini", "ve", v_hpg)

    pe_bank("pmini")
    te.wait("ve", v_hpg)
    wv_marks = {}
    for idx in range(16):
        wi, c = idx // 8, idx % 8
        wd = w2_d if wi == 0 else w1_d
        stg = wstA if idx % 2 == 0 else wstB
        wb = w2b if wi == 0 else w1b
        if idx >= 2:
            gp.wait("wv", wv_marks[idx - 2])
        ldw = dma(gp, stg, wd[128 * c:128 * (c + 1), :], sem="ldg")
        ve.wait("ldg", ldw)
        v_cast = ve.emit(lambda e, wb=wb, c=c, stg=stg: e.tensor_scalar(
            out=wb[:, 1024 * c:1024 * (c + 1)], in0=stg,
            scalar1=opca[:, c:c + 1], scalar2=None, op0=ALU.mult), "ve")
        te.wait("ve", v_cast)
        for m in range(8):
            if wi == 0:
                te.emit(lambda e, c=c, m=m, stg=stg: e.matmul(
                    pmini[0:128, 17 * m:17 * m + 17],
                    stg[:, 128 * m:128 * (m + 1)],
                    hpadg[:, 17 * c:17 * c + 17],
                    start=(c == 0), stop=(c == 7)))
            else:
                te.emit(lambda e, c=c, m=m, stg=stg: e.matmul(
                    RAP(pmini, 140 + 16 * m, [[512, 128], [1, 2]]),
                    stg[:, 128 * m:128 * (m + 1)],
                    RAP(hpadg, 17 * c + 7, [[136, 128], [9, 2]]),
                    start=(c == 0), stop=(c == 7)))
        wv_marks[idx] = te.emit(lambda e: e.matmul(
            pmini[0:1, 500:501], idn[0:1, 0:1], idn[0:1, 0:1], start=False, stop=False, skip_group_check=True), "wv")
    qg_done = te.emit(lambda e: e.matmul(
        pmini[0:1, 502:503], idn[0:1, 0:1], idn[0:1, 0:1], start=False, stop=False, skip_group_check=True), "pe")

    ve.wait("pe", qg_done)
    vf(lambda e: e.tensor_copy(
        out=RAP(projg, 0, [[128, 128], [16, 8], [1, 16]]),
        in_=RAP(pmini, 0, [[512, 128], [17, 8], [1, 16]])))
    vf(lambda e: e.tensor_copy(
        out=c2fm[:], in_=RAP(pmini, 16, [[512, 128], [17, 8], [1, 1]])))
    vf(lambda e: e.tensor_copy(
        out=c1fm[:], in_=RAP(pmini, 141, [[512, 128], [16, 8], [1, 1]])))
    v_pj = vf(lambda e: e.tensor_copy(
        out=qgev[:], in_=RAP(pmini, 140, [[512, 128], [16, 8], [1, 1]])))
    set_bank_free("pmini", "ve", v_pj)

    v_argg = None
    for c in range(8):
        v_argg = vf(lambda e, c=c: e.tensor_scalar(
            out=argg[:, 16 * c:16 * (c + 1)], in0=projg[:, 16 * c:16 * (c + 1)],
            scalar1=qgev[:, c:c + 1], scalar2=None, op0=ALU.add))
    ac.wait("ve", v_argg)
    a_thg = ac.emit(lambda e: e.activation(out=thg[:], in_=argg[:], func=AF.Tanh), "ac")
    te.wait("ac", a_thg)
    pe_bank("pmini")
    sc_mmv = None
    for c in range(8):
        sc_mmv = te.emit(lambda e, c=c: e.matmul(
            pmini[0:16, 0:1], thg[:, 16 * c:16 * (c + 1)], vf32[:, c:c + 1],
            start=(c == 0), stop=(c == 7)), "pe" if c == 7 else None)
    ve.wait("pe", sc_mmv)
    v_scg = ve.emit(lambda e: e.tensor_copy(out=scg[:], in_=pmini[0:16, 0:1]), "ve")
    set_bank_free("pmini", "ve", v_scg)
    te.wait("ve", v_scg)
    pe_bank("pmini")
    t_scg = te.emit(lambda e: e.transpose(pmini[0:1, 0:16], scg[:], idn[0:16, 0:16]), "pe")
    ve.wait("pe", t_scg)
    v_scr = vf(lambda e: e.tensor_copy(out=scrow[:], in_=pmini[0:1, 0:16]))
    set_bank_free("pmini", "ve", v_scr)
    vf(lambda e: e.tensor_reduce(out=gls[0:1, 0:1], in_=scrow[:], axis=AXX, op=ALU.max))
    vf(lambda e: e.tensor_single_scalar(out=gls[0:1, 0:1], in_=gls[0:1, 0:1],
                                        scalar=-1.0, op=ALU.mult))
    v_nm0 = ve.emit(lambda e: e.tensor_copy(out=gls[0:1, 1:2], in_=gls[0:1, 0:1]), "ve")
    ac.wait("ve", v_nm0)
    a_eg = ac.emit(lambda e: e.activation(out=wgrow[:], in_=scrow[:], func=AF.Exp,
                                          bias=gls[0:1, 0:1], accum_out=gls[0:1, 2:3]), "ac")
    ve.wait("ac", a_eg)
    vf(lambda e: e.reciprocal(out=gls[0:1, 3:4], in_=gls[0:1, 2:3]))
    v_wg = ve.emit(lambda e: e.tensor_scalar(out=wgrow[:], in0=wgrow[:],
                                             scalar1=gls[0:1, 3:4], scalar2=None,
                                             op0=ALU.mult), "ve")
    te.wait("ve", v_wg)
    pe_bank("pmini")
    for c in range(8):
        t_h = te.emit(lambda e, c=c: e.transpose(
            pmini[0:16, 0:128], hpadg[:, 17 * c:17 * c + 16], idn[:]), "pe")
        ve.wait("pe", t_h)
        v_h = ve.emit(lambda e, c=c: e.tensor_copy(
            out=hpadgT[0:16, 128 * c:128 * (c + 1)], in_=pmini[0:16, 0:128]), "ve")
        te.wait("ve", v_h)
    t_wgT = te.emit(lambda e: e.transpose(pmini[0:16, 130:131], wgrow[:], idn[0:1, 0:1]), "pe")
    ve.wait("pe", t_wgT)
    v_wgT = ve.emit(lambda e: e.tensor_copy(out=wgT[:], in_=pmini[0:16, 130:131]), "ve")
    set_bank_free("pmini", "ve", v_wgT)
    te.wait("ve", v_wgT)
    pe_bank("pms0"); pe_bank("pms1")
    mm_at = {}
    for hh in range(2):
        mm_at[hh] = te.emit(lambda e, hh=hh: e.matmul(
            pms[hh][0:1, :], wgT[:], hpadgT[0:16, 512 * hh:512 * (hh + 1)],
            start=True, stop=True), "pe")
    ve.wait("pe", mm_at[1])
    v_atg = None
    for hh in range(2):
        v_atg = ve.emit(lambda e, hh=hh: e.tensor_copy(
            out=attng[0:1, 512 * hh:512 * (hh + 1)], in_=pms[hh][0:1, :]),
            "ve" if hh == 1 else None)
    set_bank_free("pms0", "ve", v_atg); set_bank_free("pms1", "ve", v_atg)
    ld_ga = dma(sy, rowb[:], rowsc_d[3:4, :])
    ve.wait("ld", ld_ga)
    ve.wait("ve", v_atg)
    vf(lambda e: e.tensor_tensor(out=x2g[:], in0=attng[:], in1=rowb[:], op=ALU.mult))
    vf(lambda e: e.tensor_tensor(out=x2g[:], in0=x2g[:], in1=xg_s[0:1, :], op=ALU.add))
    v_x2g = vf(lambda e: e.reduce_sum(out=gls[0:1, 8:9], in_=x2g[:], axis=AXX))
    ac.wait("ve", v_x2g)
    a_msq2 = ac.emit(lambda e: e.activation(out=hpadgT[0:1, :], in_=x2g[:], func=AF.Square,
                                            accum_out=gls[0:1, 9:10]), "ac")
    ve.wait("ac", a_msq2)
    vf(lambda e: e.tensor_single_scalar(out=gls[0:1, 10:11], in_=gls[0:1, 8:9],
                                        scalar=1.0 / H, op=ALU.mult))
    vf(lambda e: e.tensor_tensor(out=gls[0:1, 11:12], in0=gls[0:1, 10:11],
                                 in1=gls[0:1, 10:11], op=ALU.mult))
    vf(lambda e: e.tensor_single_scalar(out=gls[0:1, 12:13], in_=gls[0:1, 9:10],
                                        scalar=1.0 / H, op=ALU.mult))
    vf(lambda e: e.tensor_tensor(out=gls[0:1, 12:13], in0=gls[0:1, 12:13],
                                 in1=gls[0:1, 11:12], op=ALU.subtract))
    vf(lambda e: e.tensor_single_scalar(out=gls[0:1, 12:13], in_=gls[0:1, 12:13],
                                        scalar=EPS, op=ALU.add))
    newton_rsqrt(gls[0:1, 13:14], gls[0:1, 12:13], 1, rows=1)
    vf(lambda e: e.tensor_scalar(out=h2g[:], in0=x2g[:], scalar1=gls[0:1, 10:11],
                                 scalar2=gls[0:1, 13:14], op0=ALU.subtract, op1=ALU.mult))
    sy.wait("ve", v_x2g)
    ld_ocb = dma(sy, rowb[:], rowsc_d[4:5, :])
    ve.wait("ld", ld_ocb)
    v_m1 = vf(lambda e: e.tensor_tensor(out=h2g[:], in0=h2g[:], in1=rowb[:],
                                        op=ALU.mult))
    sy.wait("ve", v_m1)
    ld_sb = dma(sy, rowb[:], rowsc_d[5:6, :])
    ve.wait("ld", ld_sb)
    v_h2g = vf(lambda e: e.tensor_tensor(out=h2g[:], in0=h2g[:], in1=rowb[:],
                                         op=ALU.add))
    te.wait("ve", v_h2g)
    pe_bank("pmini")
    t_h2g = None
    for c in range(8):
        t_h2g = te.emit(lambda e, c=c: e.transpose(
            pmini[0:128, 140 + c:141 + c], h2g[0:1, 128 * c:128 * (c + 1)],
            idn[0:1, 0:1]),
            "pe" if c == 7 else None)
    ve.wait("pe", t_h2g)
    v_h2gfm = vf(lambda e: e.tensor_copy(out=h2gfm[:], in_=pmini[0:128, 140:148]))
    te.wait("ve", v_h2gfm)
    lg_mm = None
    for c in range(8):
        lg_mm = te.emit(lambda e, c=c: e.matmul(
            pmini[0:8, 0:1], wrts[:, 8 * c:8 * (c + 1)], h2gfm[:, c:c + 1],
            start=(c == 0), stop=(c == 7)), "pe" if c == 7 else None)
    ve.wait("pe", lg_mm)
    v_lg = ve.emit(lambda e: e.tensor_copy(out=scg[0:8, 0:1], in_=pmini[0:8, 0:1]), "ve")
    te.wait("ve", v_lg)
    t_lg = te.emit(lambda e: e.transpose(pmini[0:1, 32:40], scg[0:8, 0:1], idn[0:8, 0:8]), "pe")
    ve.wait("pe", t_lg)
    lgrow = gls[0:1, 40:48]
    v_lgr = vf(lambda e: e.tensor_copy(out=lgrow, in_=pmini[0:1, 32:40]))
    set_bank_free("pmini", "ve", v_lgr)
    vf(lambda e: e.tensor_reduce(out=gls[0:1, 16:17], in_=lgrow, axis=AXX, op=ALU.max))
    vf(lambda e: e.tensor_scalar(out=gls[0:1, 24:32], in0=lgrow,
                                      scalar1=gls[0:1, 16:17], scalar2=-1e9,
                                      op0=ALU.is_ge, op1=ALU.mult))
    vf(lambda e: e.tensor_tensor(out=gls[0:1, 24:32], in0=gls[0:1, 24:32], in1=lgrow,
                                      op=ALU.add))
    vf(lambda e: e.tensor_reduce(out=gls[0:1, 17:18], in_=gls[0:1, 24:32], axis=AXX,
                                 op=ALU.max))
    vf(lambda e: e.tensor_scalar(out=gls[0:1, 24:32], in0=lgrow,
                                      scalar1=gls[0:1, 17:18], scalar2=1e4,
                                      op0=ALU.is_ge, op1=ALU.mult))
    vf(lambda e: e.tensor_tensor(out=gls[0:1, 24:32], in0=gls[0:1, 24:32], in1=lgrow,
                                 op=ALU.add))
    vf(lambda e: e.tensor_single_scalar(out=gls[0:1, 24:32], in_=gls[0:1, 24:32],
                                             scalar=1e4, op=ALU.subtract))
    vf(lambda e: e.tensor_reduce(out=gls[0:1, 18:19], in_=gls[0:1, 24:32],
                                       axis=AXX, op=ALU.max))
    v_ml = ve.emit(lambda e: e.tensor_single_scalar(out=gls[0:1, 18:19], in_=gls[0:1, 18:19],
                                                    scalar=-1.0, op=ALU.mult), "ve")
    ac.wait("ve", v_ml)
    a_ge = ac.emit(lambda e: e.activation(out=gatesrow[:], in_=gls[0:1, 24:32], func=AF.Exp,
                                          bias=gls[0:1, 18:19], accum_out=gls[0:1, 19:20]), "ac")
    mini_last_ac = a_ge
    ve.wait("ac", a_ge)
    vf(lambda e: e.reciprocal(out=gls[0:1, 20:21], in_=gls[0:1, 19:20]))
    v_g = ve.emit(lambda e: e.tensor_scalar(out=gatesrow[:], in0=gatesrow[:],
                                            scalar1=gls[0:1, 20:21], scalar2=None,
                                            op0=ALU.mult), "ve")
    te.wait("ve", v_g)
    pe_bank("pms0")
    mm_gb = te.emit(lambda e: e.matmul(pms[0][0:128, 0:8], onesc[:], gatesrow[:],
                                       start=True, stop=True), "pe")
    mini_last_pe = mm_gb
    ve.wait("pe", mm_gb)
    v_gbc = vf(lambda e: e.tensor_copy(out=gbc[:], in_=pms[0][0:128, 0:8]))
    set_bank_free("pms0", "ve", v_gbc)
    for j in range(4):
        vf(lambda e, j=j: e.tensor_copy(out=sv[0:64, j:j + 1], in_=gbc[0:64, 2 * j:2 * j + 1]))
        if j < 3:
            vf(lambda e, j=j: e.tensor_copy(out=sv[64:128, j:j + 1],
                                            in_=gbc[64:128, 2 * j + 1:2 * j + 2]))
    v_sv = ve.emit(lambda e: e.tensor_copy(out=sv[64:128, 3:4], in_=gbc[64:128, 7:8]), "ve")
    if debug:
        gp.wait("ve", v_sv)
        dma(gp, dbg["d_gates"], gatesrow[:], sem="st")
        dma(gp, dbg["d_gdbg"], gls[:], sem="st")
        dma(gp, dbg["d_gbc"], gbc[:], sem="st")

    at_mark = {}
    for e_ in range(8):
        if e_ >= 2:
            gp.wait("pe", at_mark[e_ - 2])
        ld_a = dma(gp, atmp[e_ % 2][:], a_d[e_, :, :], sem="ldg")
        te.wait("ldg", ld_a)
        g, half = e_ // 2, e_ % 2
        for c in range(8):
            pe_bank("pms0")
            t_a = te.emit(lambda e, e_=e_, c=c: e.transpose(
                pms[0][0:128, 0:32].bitcast(BF16), atmp[e_ % 2][:, 128 * c:128 * (c + 1)],
                idnb[0:64, 0:64]), "pe")
            ve.wait("pe", t_a)
            v_a = ve.emit(lambda e, g=g, half=half, c=c: e.tensor_copy(
                out=alh[:, (c * 4 + g) * 128 + 64 * half:(c * 4 + g) * 128 + 64 * (half + 1)],
                in_=pms[0][0:128, 0:32].bitcast(BF16)), "ve")
            set_bank_free("pms0", "ve", v_a)
            te.wait("ve", v_a)
        at_mark[e_] = te.emit(lambda e: e.matmul(
            pmini[0:1, 504:505], idn[0:1, 0:1], idn[0:1, 0:1], start=False, stop=False, skip_group_check=True), "pe")
    bw_mark = {}
    bw_i = 0
    for e_ in range(8):
        g, half = e_ // 2, e_ % 2
        for c in range(8):
            if bw_i >= 2:
                gp.wait("pe", bw_mark[bw_i - 2])
            buf = bw_i % 2
            ld_b = dma(gp, bwtmp[buf][:], bw_d[e_, 128 * c:128 * (c + 1), :], sem="ldg")
            ve.wait("ldg", ld_b)
            v_b = ve.emit(lambda e, buf=buf, c=c: e.tensor_scalar(
                out=bwtmp2[buf][:], in0=bwtmp[buf][:], scalar1=gbpr[:, c:c + 1],
                scalar2=None, op0=ALU.mult), "ve")
            te.wait("ve", v_b)
            pe_bank("pms1")
            t_b = te.emit(lambda e, buf=buf: e.transpose(
                pms[1][0:64, 0:64].bitcast(BF16), bwtmp2[buf][:], idnb[:]), "pe")
            bw_mark[bw_i] = t_b
            ve.wait("pe", t_b)
            v_b2 = ve.emit(lambda e, g=g, half=half, c=c: e.tensor_copy(
                out=bwt[64 * half:64 * (half + 1), 1024 * g + 128 * c:1024 * g + 128 * (c + 1)],
                in_=pms[1][0:64, 0:64].bitcast(BF16)), "ve")
            set_bank_free("pms1", "ve", v_b2)
            bw_i += 1
    v_bwt = v_b2
    if debug:
        gp.wait("ve", v_bwt)
        dma(gp, dbg["d_sv"], sv[:], sem="st")
        dma(gp, dbg["d_alh"], alh[:], sem="st")
        dma(gp, dbg["d_bwt"], bwt[:], sem="st")

    xu_ld = {}
    xu_free = {}
    xn_done = {}
    xn_lastT = {}
    gxn_done = {}
    hp_done = {}
    pq_mm_last = {}
    pq_evac = {}
    wtm_store = {}
    comb_mm = {}
    comb_done = {}
    xtok_ld = {}
    ffn_done = {}
    ffn_tr = {}
    x2_dump = {}
    sc_bank_free = None

    tile_sc = {}

    def ln1_load_stats(jj, sc_):
        slot = jj % XU_RING
        xslot = jj % XN_RING
        tile_sc[jj] = sc_
        if jj == 16:
            vz = ve.emit(lambda e, s=slot: e.memset(xu[s][:], 0.0), "ve")
            gp.wait("ve", vz)
        if jj >= XU_RING:
            gp.wait("ve", xu_free[jj - XU_RING])
        rows = 128 if jj < 16 else U - 2048
        xu_ld[jj] = dma(gp, xu[slot][0:rows, :], x_d[128 * jj:128 * jj + rows, :], sem="ldg")
        ve.wait("ldg", xu_ld[jj])
        vf(lambda e, s=slot, sc=sc_: e.reduce_sum(out=statb[:, sc:sc + 1], in_=xu[s][:],
                                                  axis=AXX))
        ac.wait("ldg", xu_ld[jj])
        if jj - XN_RING in xn_lastT:
            ac.wait("xb", xn_lastT[jj - XN_RING])
        a_sq = ac.emit(lambda e, s=slot, xs=xslot, sc=sc_: e.activation(
            out=xnr[xs][:], in_=xu[s][:], func=AF.Square,
            accum_out=statb[:, 5 + sc:6 + sc]), "ac")
        return a_sq

    def ln1_stats_batch(sclo, schi, a_sq_last):
        n = schi - sclo
        ve.wait("ac", a_sq_last)
        vf(lambda e: e.tensor_single_scalar(
            out=statb[:, 20 + sclo:20 + schi], in_=statb[:, sclo:schi],
            scalar=1.0 / H, op=ALU.mult))
        vf(lambda e: e.tensor_tensor(
            out=nsc[:, 28:28 + n], in0=statb[:, 20 + sclo:20 + schi],
            in1=statb[:, 20 + sclo:20 + schi], op=ALU.mult))
        vf(lambda e: e.tensor_single_scalar(
            out=statb[:, 10 + sclo:10 + schi], in_=statb[:, 5 + sclo:5 + schi],
            scalar=1.0 / H, op=ALU.mult))
        vf(lambda e: e.tensor_tensor(
            out=statb[:, 10 + sclo:10 + schi], in0=statb[:, 10 + sclo:10 + schi],
            in1=nsc[:, 28:28 + n], op=ALU.subtract))
        vf(lambda e: e.tensor_single_scalar(
            out=statb[:, 10 + sclo:10 + schi], in_=statb[:, 10 + sclo:10 + schi],
            scalar=EPS, op=ALU.add))
        newton_rsqrt(statb[:, 15 + sclo:15 + schi], statb[:, 10 + sclo:10 + schi], n)

    def ln1_norm(jj):
        slot = jj % XU_RING
        xslot = jj % XN_RING
        gslot = jj % GXN_RING
        sc_ = tile_sc[jj]
        v_xn = ve.emit(lambda e, s=slot, xs=xslot, sc=sc_: e.tensor_scalar(
            out=xnr[xs][:], in0=xu[s][:], scalar1=statb[:, 20 + sc:21 + sc],
            scalar2=statb[:, 15 + sc:16 + sc], op0=ALU.subtract, op1=ALU.mult), "ve")
        xu_free[jj] = v_xn
        if jj in (0, 16):
            vf(lambda e, jj=jj: e.tensor_scalar(
                out=nsc[:, 34:35], in0=mcol[:, jj:jj + 1], scalar1=-1.0, scalar2=1.0,
                op0=ALU.mult, op1=ALU.add))
            ve.emit(lambda e: e.tensor_scalar(
                out=sqd[:], in0=padbc[:], scalar1=nsc[:, 34:35], scalar2=None, op0=ALU.mult))
            ve.emit(lambda e, xs=xslot, jj=jj: e.tensor_scalar(
                out=xnr[xs][:], in0=xnr[xs][:], scalar1=mcol[:, jj:jj + 1], scalar2=None,
                op0=ALU.mult))
            v_xn = ve.emit(lambda e, xs=xslot: e.tensor_tensor(
                out=xnr[xs][:], in0=xnr[xs][:], in1=sqd[:], op=ALU.add), "ve")
        xn_done[jj] = v_xn
        if jj - GXN_RING in comb_mm:
            ve.wait("pe", comb_mm[jj - GXN_RING])
        ve.emit(lambda e, xs=xslot, gs=gslot: e.tensor_tensor(
            out=gxn[gs][:], in0=xnr[xs][:], in1=gcbc[:], op=ALU.mult))
        g_v = ve.emit(lambda e, gs=gslot: e.tensor_tensor(
            out=gxn[gs][:], in0=gxn[gs][:], in1=gsbc[:], op=ALU.add), "ve")
        gxn_done[jj] = g_v
        if debug:
            gp.wait("ve", g_v)
            dma(gp, dbg["d_xn"][128 * jj:128 * jj + 128, :], xnr[xslot][:], sem="st")
            dma(gp, dbg["d_gxn"][128 * jj:128 * jj + 128, :], gxn[gslot][:], sem="st")

    def build_hpad(ut):
        if ut > 0:
            sy.wait("pe", pq_mm_last[ut - 1])
        last = None
        for fi, jj in enumerate(range(4 * ut, min(4 * ut + 5, NUT))):
            xslot = jj % XN_RING
            sy.wait("ve", xn_done[jj])
            for c in range(HC):
                last = dmat(sy, hpad[:, 640 * c + 128 * fi:640 * c + 128 * (fi + 1)],
                            xnr[xslot][:, 128 * c:128 * (c + 1)])
            xn_lastT[jj] = last
        hp_done[ut] = last

    def proj_q(ut):
        te.wait("xb", hp_done[ut])
        mm = None
        last_v = None
        for hc in range(HC):
            for seg, (c0, cw) in enumerate([(0, 512), (512, 16)]):
                bank = f"pmm{(2 * hc + seg) % 2}"
                pe_bank(bank)
                for kc in range(HC):
                    mm = te.emit(lambda e, hc=hc, kc=kc, c0=c0, cw=cw, b=(2 * hc + seg) % 2:
                                 e.matmul(pmm[b][:, 0:cw],
                                          w2b[:, 1024 * kc + 128 * hc:1024 * kc + 128 * (hc + 1)],
                                          hpad[:, 640 * kc + c0:640 * kc + c0 + cw],
                                          start=(kc == 0), stop=(kc == 7)),
                                 "pe" if kc == 7 else None)
                ve.wait("pe", mm)
                last_v = ve.emit(lambda e, hc=hc, c0=c0, cw=cw, b=(2 * hc + seg) % 2:
                                 e.tensor_scalar(out=p_s[:, 528 * hc + c0:528 * hc + c0 + cw],
                                                 in0=pmm[b][:, 0:cw],
                                                 scalar1=c2fm[:, hc:hc + 1], scalar2=None,
                                                 op0=ALU.add), "ve")
                set_bank_free(bank, "ve", last_v)
        for hc in range(HC):
            bank = f"pmm{hc % 2}"
            pe_bank(bank)
            for kc in range(HC):
                mm = te.emit(lambda e, hc=hc, kc=kc, b=hc % 2:
                             e.matmul(pmm[b][:, :],
                                      w1b[:, 1024 * kc + 128 * hc:1024 * kc + 128 * (hc + 1)],
                                      hpad[:, 640 * kc + 7:640 * kc + 7 + 512],
                                      start=(kc == 0), stop=(kc == 7)),
                             "pe" if kc == 7 else None)
            ve.wait("pe", mm)
            last_v = ve.emit(lambda e, hc=hc, b=hc % 2:
                             e.tensor_scalar(out=q_s[:, 512 * hc:512 * (hc + 1)],
                                             in0=pmm[b][:, :],
                                             scalar1=c1fm[:, hc:hc + 1], scalar2=None,
                                             op0=ALU.add), "ve")
            set_bank_free(bank, "ve", last_v)
        pq_mm_last[ut] = cnt["pe"]
        pq_evac[ut] = last_v
        if debug and ut == 0:
            gp.wait("ve", last_v)
            dma(gp, dbg["d_p"], p_s[:], sem="st")
            dma(gp, dbg["d_q"], q_s[:], sem="st")

    def score_unit(ut):
        nonlocal sc_bank_free
        if sc_bank_free is not None:
            te.wait("ac", sc_bank_free)
        step = 0
        th_cons = {}
        arg_cons = {}
        mm = None
        arg_dump = th_dump = None
        dump_step = 32
        for k in range(KW):
            for hf in range(4):
                par = step % 2
                if arg_dump is not None and step == dump_step + 2:
                    ve.wait("st", arg_dump)
                if step >= 2:
                    ve.wait("ac", arg_cons[par])
                if step == 0:
                    ve.wait("ve", pq_evac[ut])
                    if ut == 0:
                        ve.wait("pe", mini_last_pe)
                        ve.wait("ac", mini_last_ac)
                v_add = ve.emit(lambda e, par=par, k=k, hf=hf: e.tensor_tensor(
                    out=RAP(argb[par], 0, [[1024, 128], [512, 2], [1, 512]]),
                    in0=RAP(p_s, 528 * 2 * hf + k, [[HC * 528, 128], [528, 2], [1, 512]]),
                    in1=RAP(q_s, 512 * 2 * hf, [[HC * 512, 128], [512, 2], [1, 512]]),
                    op=ALU.add), "ve")
                if debug and ut == 0 and step == dump_step:
                    gp.wait("ve", v_add)
                    arg_dump = dma(gp, dbg["d_arg"], argb[par][:], sem="st")
                ac.wait("ve", v_add)
                if step >= 2:
                    ac.wait("pe", th_cons[par])
                if th_dump is not None and step == dump_step + 2:
                    ac.wait("st", th_dump)
                a_th = ac.emit(lambda e, par=par: e.activation(
                    out=thb[par][:], in_=argb[par][:], func=AF.Tanh), "ac")
                arg_cons[par] = a_th
                if debug and ut == 0 and step == dump_step:
                    gp.wait("ac", a_th)
                    th_dump = dma(gp, dbg["d_th"], thb[par][:], sem="st")
                te.wait("ac", a_th)
                for cl in range(2):
                    for tb in range(4):
                        first = (k == 0 and hf == 0 and cl == 0 and tb == 0)
                        last = (k == KW - 1 and hf == 3 and cl == 1 and tb == 3)
                        mm = te.emit(lambda e, par=par, k=k, hf=hf, cl=cl, tb=tb,
                                     first=first, last=last: e.matmul(
                            psc[:, 16 * tb + k:16 * tb + k + 1],
                            thb[par][:, 512 * cl + 128 * tb:512 * cl + 128 * tb + 128],
                            vb16[:, 2 * hf + cl:2 * hf + cl + 1],
                            start=first, stop=last, skip_group_check=True),
                            "pe" if (cl == 1 and tb == 3) else None)
                th_cons[par] = mm
                step += 1
        ve.wait("pe", mm)
        dsc_st = None
        if debug and ut == 0:
            v_dbg = ve.emit(lambda e: e.tensor_copy(out=expw[:], in_=psc[:]), "ve")
            gp.wait("ve", v_dbg)
            dsc_st = dma(gp, dbg["d_sc"], expw[:], sem="st")
        for tb in range(4):
            vf(lambda e, tb=tb: e.tensor_reduce(
                out=negmax[:, tb:tb + 1], in_=psc[:, 16 * tb:16 * (tb + 1)], axis=AXX,
                op=ALU.max))
        v_nm = ve.emit(lambda e: e.tensor_single_scalar(
            out=negmax[:], in_=negmax[:], scalar=-1.0, op=ALU.mult), "ve")
        ac.wait("ve", v_nm)
        if dsc_st is not None:
            ac.wait("st", dsc_st)
        a_exp = None
        for tb in range(4):
            a_exp = ac.emit(lambda e, tb=tb: e.activation(
                out=expw[:, 16 * tb:16 * (tb + 1)], in_=psc[:, 16 * tb:16 * (tb + 1)],
                func=AF.Exp, bias=negmax[:, tb:tb + 1], accum_out=sumexp[:, tb:tb + 1]),
                "ac" if tb == 3 else None)
        sc_bank_free = a_exp
        ve.wait("ac", a_exp)
        if ut > 0:
            ve.wait("ldg", wtm_store[4 * (ut - 1) + 3])
        vf(lambda e: e.reciprocal(out=rsum[:], in_=sumexp[:]))
        v_w = None
        for tb in range(4):
            v_w = ve.emit(lambda e, tb=tb: e.tensor_scalar(
                out=wtm[:, 16 * tb:16 * (tb + 1)], in0=expw[:, 16 * tb:16 * (tb + 1)],
                scalar1=rsum[:, tb:tb + 1], scalar2=None, op0=ALU.mult),
                "ve" if tb == 3 else None)
        if debug and ut == 0:
            gp.wait("ve", v_w)
            dma(gp, dbg["d_w"], RAP(wtm, 0, [[64, 128], [1, 64]]), sem="st")
        gp.wait("ve", v_w)
        for tb in range(4):
            t0r = 15 + 512 * ut + 128 * tb
            wtm_store[4 * ut + tb] = dma(
                gp, wsc_d[t0r:t0r + 128, 0:16], RAP(wtm, 16 * tb, [[64, 128], [1, 16]]),
                sem="ldg")

    def load_xtok(j):
        slot = j % XTOK_RING
        if j >= XTOK_RING:
            gp.wait("st", ffn_done[j - XTOK_RING])
        xtok_ld[j] = dma(gp, xtok[slot][:], x_d[7 + 128 * j:7 + 128 * j + 128, :], sem="ldg")

    def combine_block(B):
        t0 = 128 * B
        par = B % 2
        need_blk = min(B + 1, NTOK - 1)
        sy.wait("ldg", wtm_store[need_blk])
        sy.wait("ldg", zw_last)
        if B >= 2:
            sy.wait("pe", comb_mm[B - 2])
        dma(sy, pcA[par][0:64, 15:79],
            RAP(wsc_d.tensor, WSCW * (15 + t0), [[1, 64], [WSCW - 1, 64]]),
            allow_slow_non_contiguous=True)
        dma(sy, pcB[par][64:128, 15:79],
            RAP(wsc_d.tensor, WSCW * (15 + t0 + 64), [[1, 64], [WSCW - 1, 64]]),
            allow_slow_non_contiguous=True)
        dma(sy, pcX[par][64:79, 15:79],
            RAP(wsc_d.tensor, WSCW * (15 + t0) + 64, [[1, 15], [WSCW - 1, 64]]),
            allow_slow_non_contiguous=True)
        ldY = dma(sy, pcY[par][0:15, 15:79],
                  RAP(wsc_d.tensor, WSCW * (15 + t0 + 64) + 64, [[1, 15], [WSCW - 1, 64]]),
                  allow_slow_non_contiguous=True)
        if debug and B == 1:
            gp.wait("ld", ldY)
            dma(gp, dbg["d_pcA"], pcA[par][:], sem="st")
            dma(gp, dbg["d_pcX"], pcX[par][64:79, :], sem="st")
            dma(gp, dbg["d_pcB"], pcB[par][64:128, :], sem="st")
            st_pc = dma(gp, dbg["d_pcY"], pcY[par][0:15, :], sem="st")
            ve.wait("st", st_pc)
        te.wait("ld", ldY)
        te.wait("ve", gxn_done[B])
        te.wait("ve", gxn_done[min(B + 1, NUT - 1)])
        gB = gxn[B % GXN_RING]
        gB1 = gxn[(B + 1) % GXN_RING]
        pe_bank("pms0"); pe_bank("pms1")
        mm = None
        for hh in range(2):
            plist = [
                (pcA[par][0:64, 15:79], gB, 0, 0, True),
                (pcX[par][64:79, 15:79], gB, 64, 0, False),
                (pcB[par][64:128, 15:79], gB, 64, 64, True),
                (pcY[par][0:15, 15:79], gB1, 0, 64, False),
            ]
            for pi, (lh, rhs_t, rb, g0, stf) in enumerate(plist):
                rows = lh.shape[0]
                tp = (rb, g0) if (rb or g0) else None
                mm = te.emit(lambda e, lh=lh, rows=rows, rhs_t=rhs_t, rb=rb, g0=g0,
                             hh=hh, tp=tp, stf=stf: e.matmul(
                    pms[hh][g0:g0 + 64, :],
                    lh,
                    rhs_t[rb:rb + rows, 512 * hh:512 * (hh + 1)],
                    start=stf, stop=False, tile_position=tp),
                    "pe" if (hh == 1 and pi == 3) else None)
        comb_mm[B] = mm
        tslot = B % XTOK_RING
        ve.wait("pe", mm)
        if debug and B == 1:
            ve.emit(lambda e: e.tensor_copy(out=argb[0][:, 0:512], in_=pms[0][:]))
            v_att = ve.emit(lambda e: e.tensor_copy(out=argb[0][:, 512:1024], in_=pms[1][:]),
                            "ve")
            gp.wait("ve", v_att)
            st_att = dma(gp, dbg["d_att"], argb[0][:], sem="st")
            ve.wait("st", st_att)
        ve.wait("ldg", xtok_ld[B])
        vv = None
        for hh in range(2):
            vv = ve.emit(lambda e, hh=hh, ts=tslot: e.tensor_tensor(
                out=xtok[ts][:, 512 * hh:512 * (hh + 1)],
                in0=pms[hh][:, :], in1=xtok[ts][:, 512 * hh:512 * (hh + 1)],
                op=ALU.add), "ve" if hh == 1 else None)
        set_bank_free("pms0", "ve", vv); set_bank_free("pms1", "ve", vv)
        comb_done[B] = vv
        if debug:
            gp.wait("ve", vv)
            x2_dump[B] = dma(gp, dbg["d_x2"][128 * B:128 * (B + 1), :], xtok[tslot][:],
                             sem="st")

    def ffn_stats(j, gi):
        tslot = j % XTOK_RING
        vf(lambda e, ts=tslot, gi=gi: e.reduce_sum(out=st2[:, gi:gi + 1],
                                                   in_=xtok[ts][:], axis=AXX))
        ac.wait("ve", comb_done[j])
        a_sq2 = ac.emit(lambda e, ts=tslot, gi=gi: e.activation(
            out=xn2tm[:], in_=xtok[ts][:], func=AF.Square,
            accum_out=st2[:, 6 + gi:7 + gi]), "ac")
        return a_sq2

    def ffn_chain(n, a_last):
        ve.wait("ac", a_last)
        vf(lambda e: e.tensor_single_scalar(out=st2[:, 12:12 + n], in_=st2[:, 0:n],
                                            scalar=1.0 / H, op=ALU.mult))
        vf(lambda e: e.tensor_tensor(out=nsc[:, 28:28 + n], in0=st2[:, 12:12 + n],
                                     in1=st2[:, 12:12 + n], op=ALU.mult))
        vf(lambda e: e.tensor_single_scalar(out=st2[:, 18:18 + n], in_=st2[:, 6:6 + n],
                                            scalar=1.0 / H, op=ALU.mult))
        vf(lambda e: e.tensor_tensor(out=st2[:, 18:18 + n], in0=st2[:, 18:18 + n],
                                     in1=nsc[:, 28:28 + n], op=ALU.subtract))
        vf(lambda e: e.tensor_single_scalar(out=st2[:, 18:18 + n], in_=st2[:, 18:18 + n],
                                            scalar=EPS, op=ALU.add))
        newton_rsqrt(st2[:, 24:24 + n], st2[:, 18:18 + n], n)

    def ffn_rest(j, gi):
        tslot = j % XTOK_RING
        v_xn2 = ve.emit(lambda e, ts=tslot, gi=gi: e.tensor_scalar(
            out=xn2tm[:], in0=xtok[ts][:], scalar1=st2[:, 12 + gi:13 + gi],
            scalar2=st2[:, 24 + gi:25 + gi], op0=ALU.subtract, op1=ALU.mult), "ve")
        sy.wait("ve", v_xn2)
        if j - 1 in ffn_tr:
            sy.wait("pe", ffn_tr[j - 1])
        t_last = None
        for c in range(HC):
            t_last = dmat(sy, xn2fm[:, 128 * c:128 * (c + 1)], xn2tm[:, 128 * c:128 * (c + 1)])
        ve.wait("xb", t_last)
        v_h2 = None
        for c in range(HC):
            v_h2 = ve.emit(lambda e, c=c: e.tensor_scalar(
                out=h2fm[:, 128 * c:128 * (c + 1)], in0=xn2fm[:, 128 * c:128 * (c + 1)],
                scalar1=opcb[:, c:c + 1], scalar2=modfm[:, 24 + c:25 + c],
                op0=ALU.mult, op1=ALU.add), "ve" if c == 7 else None)
        if debug and j == 1:
            gp.wait("ve", v_h2)
            dma(gp, dbg["d_h2"], h2fm[:], sem="st")
        te.wait("ve", v_h2)
        if j == 0:
            te.wait("ve", v_sv)
            te.wait("pe", at_mark[7])
            te.wait("ve", v_bwt)
        pe_bank("pmm2")
        mm = None
        for g in range(4):
            for kc in range(HC):
                mm = te.emit(lambda e, g=g, kc=kc: e.matmul(
                    pmm[2][:, 128 * g:128 * (g + 1)],
                    alh[:, (kc * 4 + g) * 128:(kc * 4 + g) * 128 + 128],
                    h2fm[:, 128 * kc:128 * (kc + 1)],
                    start=(kc == 0), stop=(kc == 7)),
                    "pe" if (g == 3 and kc == 7) else None)
        ve.wait("pe", mm)
        v_mid = None
        for g in range(4):
            v_mid = ve.emit(lambda e, g=g: e.tensor_scalar(
                out=midb[:, 128 * g:128 * (g + 1)], in0=pmm[2][:, 128 * g:128 * (g + 1)],
                scalar1=sv[:, g:g + 1], scalar2=None, op0=ALU.mult), "ve" if g == 3 else None)
        set_bank_free("pmm2", "ve", v_mid)
        if debug and j == 1:
            gp.wait("ve", v_mid)
            dma(gp, dbg["d_mid"], midb[:], sem="st")
        v_gh = None
        for c in range(HC):
            v_gh = ve.emit(lambda e, c=c: e.tensor_scalar(
                out=xn2fm[:, 128 * c:128 * (c + 1)], in0=h2fm[:, 128 * c:128 * (c + 1)],
                scalar1=gbpr[:, c:c + 1], scalar2=None, op0=ALU.mult),
                "ve" if c == 7 else None)
        te.wait("ve", v_mid)
        pe_bank("pmm2"); pe_bank("pmm3")
        mm2 = None
        for hh in range(2):
            for g in range(4):
                mm2 = te.emit(lambda e, hh=hh, g=g: e.matmul(
                    pmm[2 + hh][:, :],
                    midb[:, 128 * g:128 * (g + 1)],
                    bwt[:, 1024 * g + 512 * hh:1024 * g + 512 * (hh + 1)],
                    start=(g == 0), stop=False))
        te.wait("ve", v_gh)
        for hh in range(2):
            for ci in range(4):
                c = 4 * hh + ci
                mm2 = te.emit(lambda e, hh=hh, ci=ci, c=c: e.matmul(
                    pmm[2 + hh][:, 128 * ci:128 * (ci + 1)],
                    xn2fm[:, 128 * c:128 * (c + 1)], idnb[:],
                    start=False, stop=(ci == 3)),
                    "pe" if (hh == 1 and ci == 3) else None)
        ffn_tr[j] = mm2
        ve.wait("pe", mm2)
        if debug and j in x2_dump:
            ve.wait("st", x2_dump[j])
        vv = None
        for hh in range(2):
            vv = ve.emit(lambda e, hh=hh, ts=tslot: e.tensor_tensor(
                out=xtok[ts][:, 512 * hh:512 * (hh + 1)], in0=pmm[2 + hh][:, :],
                in1=xtok[ts][:, 512 * hh:512 * (hh + 1)], op=ALU.add), "ve" if hh == 1 else None)
        set_bank_free("pmm2", "ve", vv); set_bank_free("pmm3", "ve", vv)
        gp.wait("ve", vv)
        ffn_done[j] = dma(gp, out_d[128 * j:128 * (j + 1), :], xtok[tslot][:], sem="st")

    next_comb = 0
    for ut in range(NUNIT):
        newt = [jj for jj in range(4 * ut, min(4 * ut + 5, NUT)) if jj not in tile_sc]
        sclo = newt[0] - 4 * ut
        a_last = None
        for jj in newt:
            a_last = ln1_load_stats(jj, jj - 4 * ut)
        ln1_stats_batch(sclo, newt[-1] - 4 * ut + 1, a_last)
        for jj in newt:
            ln1_norm(jj)
        build_hpad(ut)
        proj_q(ut)
        score_unit(ut)
        group = []
        while next_comb < NTOK and (512 * (ut + 1) >= 128 * next_comb + 143 or ut == NUNIT - 1):
            group.append(next_comb)
            next_comb += 1
        for B in group:
            load_xtok(B)
            combine_block(B)
        a_last = None
        for gi, B in enumerate(group):
            a_last = ffn_stats(B, gi)
        if group:
            ffn_chain(len(group), a_last)
        for gi, B in enumerate(group):
            ffn_rest(B, gi)

    gp.wait("st", cnt["st"])
    gp.wait("ldg", cnt["ldg"])
    gp.wait("ld", cnt["ld"])
    gp.wait("xb", cnt["xb"])

    import contextlib
    with contextlib.ExitStack() as stk:
        for s in semnames:
            SEM[s] = stk.enter_context(nc.semaphore(s))
        with nc.Block() as block:
            @block.sync
            def _(e):
                for fi, f in enumerate(prog["sync"]):
                    try:
                        f(e)
                    except Exception as ex:
                        raise RuntimeError(f"sync op index {fi}: {ex}") from ex

            @block.gpsimd
            def _(e):
                for f in prog["gpsimd"]:
                    f(e)

            @block.tensor
            def _(e):
                for fi, f in enumerate(prog["tensor"]):
                    try:
                        f(e)
                    except Exception as ex:
                        raise RuntimeError(f"tensor op index {fi} line {tags['tensor'][fi]}: {ex}") from ex

            @block.vector
            def _(e):
                for f in prog["vector"]:
                    f(e)

            @block.scalar
            def _(e):
                for f in prog["scalar"]:
                    f(e)

    return nc, dbg


def make_in_maps(inputs):
    x = np.asarray(inputs["L0_attn_inputs"], dtype=np.float32)
    w1 = np.ascontiguousarray(inputs["L0_attn_w1"], dtype=np.float32)
    w2 = np.ascontiguousarray(inputs["L0_attn_w2"], dtype=np.float32)
    v = np.asarray(inputs["L0_attn_v"], dtype=np.float32).reshape(1024)
    Wr = np.asarray(inputs["L0_ffn_Wr"], dtype=np.float32)
    A = np.ascontiguousarray(np.asarray(inputs["L0_ffn_A"], dtype=np.float32))
    Bw = np.ascontiguousarray(np.asarray(inputs["L0_ffn_Bw"], dtype=np.float32))
    c = np.asarray(inputs["L0_c"], dtype=np.float32)
    wmod = np.ascontiguousarray(inputs["L0_wmod"], dtype=np.float32)

    v_r = np.ascontiguousarray(v.reshape(8, 128))
    wrt = np.ascontiguousarray(Wr.T)
    ident = np.eye(128, dtype=np.float32)
    onescol = np.ones((1, 128), dtype=np.float32)

    in_maps = []
    for core in range(8):
        b = core // 4
        seg = core % 4
        s0 = 2048 * seg
        lo = s0 - 7
        hi = s0 + 2048 + 8
        xs = np.zeros((UPAD, H), dtype=np.float32)
        a0, a1 = max(lo, 0), min(hi, 8192)
        xs[a0 - lo:a1 - lo, :] = x[b, a0:a1, :]
        mv = np.zeros((UPAD, 1), dtype=np.float32)
        mv[a0 - lo:a1 - lo, 0] = 1.0
        xg = np.ascontiguousarray(x[b, 0:9, :])
        in_maps.append({
            "x_slice": xs, "w1": w1, "w2": w2, "v_r": v_r, "wrt": wrt,
            "A": A, "Bw": Bw, "c_row": c[b:b + 1, :], "wmod": wmod,
            "mvec": mv, "xg": xg, "ident": ident, "onescol": onescol,
        })
    return in_maps


def kernel(**inputs):
    nc, _ = build_nc(debug=False)
    in_maps = make_in_maps(inputs)
    res = run_bass_kernel_spmd(nc, in_maps, core_ids=list(range(8)))
    out = np.zeros((2, 8192, 1024), dtype=np.float32)
    for core in range(8):
        b = core // 4
        s0 = 2048 * (core % 4)
        out[b, s0:s0 + 2048, :] = np.asarray(res.results[core]["out"])
    return out


# version 06
# speedup vs baseline: 1.1299x; 1.1299x over previous
import numpy as np
import concourse.bass as bass
import concourse.mybir as mybir
from concourse.bass_utils import run_bass_kernel_spmd

F32 = mybir.dt.float32
F32R = mybir.dt.float32r
BF16 = mybir.dt.bfloat16
I32 = mybir.dt.int32
AF = mybir.ActivationFunctionType
ALU = mybir.AluOpType
AXX = mybir.AxisListType.X

H = 1024
HC = 8
T = 2048
KW = 16
U = T + KW - 1
UPAD = 17 * 128
EPS = 1e-5
NUT = 17
NTOK = 16
NUNIT = 4
MAGIC = 0x5f3759df

XTOK_RING = 5
XU_RING = 5
XN_RING = 5
GXN_RING = 5


def build_nc(debug=False):
    nc = bass.Bass("TRN2", target_bir_lowering=False)

    x_d = nc.dram_tensor("x_slice", [UPAD, H], F32, kind="ExternalInput").ap()
    w1_d = nc.dram_tensor("w1", [H, H], F32, kind="ExternalInput").ap()
    w2_d = nc.dram_tensor("w2", [H, H], F32, kind="ExternalInput").ap()
    vr_d = nc.dram_tensor("v_r", [8, 128], F32, kind="ExternalInput").ap()
    wrt_d = nc.dram_tensor("wrt", [H, 8], F32, kind="ExternalInput").ap()
    a_d = nc.dram_tensor("A", [8, 64, H], F32, kind="ExternalInput").ap()
    bw_d = nc.dram_tensor("Bw", [8, H, 64], F32, kind="ExternalInput").ap()
    c_d = nc.dram_tensor("c_row", [1, H], F32, kind="ExternalInput").ap()
    wmod_d = nc.dram_tensor("wmod", [H, 6 * H], F32, kind="ExternalInput").ap()
    mv_d = nc.dram_tensor("mvec", [UPAD, 1], F32, kind="ExternalInput").ap()
    xg_d = nc.dram_tensor("xg", [9, H], F32, kind="ExternalInput").ap()
    id_d = nc.dram_tensor("ident", [128, 128], F32, kind="ExternalInput").ap()
    ones_d = nc.dram_tensor("onescol", [1, 128], F32, kind="ExternalInput").ap()
    out_d = nc.dram_tensor("out", [T, H], F32, kind="ExternalOutput").ap()

    WSCW = 128
    WSCR = 15 + T + 16
    wsc_d = nc.dram_tensor("wsc", [WSCR, WSCW], BF16, kind="Internal").ap()
    rowsc_d = nc.dram_tensor("rowsc", [6, H], F32, kind="Internal").ap()
    modsc_d = nc.dram_tensor("modsc", [12, 512], F32, kind="Internal").ap()

    dbg = {}
    def dbg_t(nm, shp):
        if debug:
            dbg[nm] = nc.dram_tensor(nm, shp, F32, kind="ExternalOutput").ap()
        return dbg.get(nm)
    for nm, shp in [("d_mod", [128, 48]), ("d_fmp", [128, 48]), ("d_xn", [UPAD, H]),
                    ("d_gxn", [UPAD, H]), ("d_p", [128, HC * 528]), ("d_q", [128, HC * 512]),
                    ("d_sc", [128, 64]), ("d_w", [128, 64]), ("d_x2", [T, H]),
                    ("d_gates", [1, 8]), ("d_h2", [128, HC * 128]), ("d_mid", [128, 512]),
                    ("d_gdbg", [1, 48]), ("d_arg", [128, 1024]), ("d_th", [128, 1024]),
                    ("d_sv", [128, 4]), ("d_alh", [128, 4096]), ("d_bwt", [128, 4096]),
                    ("d_att", [128, 1024]), ("d_pcA", [64, 96]), ("d_pcX", [15, 96]),
                    ("d_pcB", [64, 96]), ("d_pcY", [15, 96]), ("d_gbc", [128, 8])]:
        dbg_t(nm, shp)

    S = lambda name, shape, dt=F32: nc.alloc_sbuf_tensor(name, shape, dt)
    xtok = [S(f"xtok{j}", [128, H]) for j in range(XTOK_RING)]
    xu = [S(f"xu{j}", [128, H]) for j in range(XU_RING)]
    xnr = [S(f"xn{i}", [128, H], BF16) for i in range(XN_RING)]
    gxn = [S(f"gxn{j}", [128, H], BF16) for j in range(GXN_RING)]
    hpad = S("hpad", [128, HC * 640], BF16)
    p_s = S("p_s", [128, HC * 528], BF16)
    q_s = S("q_s", [128, HC * 512], BF16)
    argb = [S(f"arg{i}", [128, 1024], BF16) for i in range(2)]
    hpT = S("hpT", [16, 1024])
    vtmp2 = S("vtmp2", [8, 128])
    mrow2 = S("mrow2", [1, H])
    packst = S("packst", [48, 128])
    thb = [S(f"th{i}", [128, 1024], BF16) for i in range(2)]
    w1b = S("w1b", [128, HC * H], BF16)
    w2b = S("w2b", [128, HC * H], BF16)
    alh = S("alh", [128, 32 * 128], BF16)
    bwt = S("bwt", [128, 4 * H], BF16)
    midb = S("midb", [128, 512], BF16)
    xn2tm = S("xn2tm", [128, H], BF16)
    xn2fm = S("xn2fm", [128, HC * 128], BF16)
    h2fm = S("h2fm", [128, HC * 128], BF16)
    atmp = [S(f"atmp{i}", [64, H], BF16) for i in range(2)]
    bwtmp = [S(f"bwtmp{i}", [128, 64]) for i in range(2)]
    bwtmp2 = [S(f"bwtmp2{i}", [128, 64], BF16) for i in range(2)]

    modfm = S("modfm", [128, 48])
    opca = S("opca", [128, 8])
    opcb = S("opcb", [128, 8])
    gapr = S("gapr", [128, 8])
    gbpr = S("gbpr", [128, 8])
    fmpack = S("fmpack", [128, 48])
    rowa = S("rowa", [1, H])
    rowb = S("rowb", [1, H])
    padbc = S("padbc", [128, H], BF16)
    gcbc = S("gcbc", [128, H], BF16)
    gsbc = S("gsbc", [128, H], BF16)
    mcol = S("mcol", [128, 17])
    silfm_t = S("silfm_t", [128, 8], F32R)
    c2fm = S("c2fm", [128, 8])
    c1fm = S("c1fm", [128, 8])
    safm = S("safm", [128, 8])
    wmst_t = [S(f"wmst_t{i}", [128, 512], F32R) for i in range(2)]
    vb16 = S("vb16", [128, 8], BF16)
    vf32 = S("vf32", [128, 8])
    wrts = S("wrts", [128, 64])
    sqd = S("sqd", [128, H], BF16)
    statb = S("statb", [128, 28])
    st2 = S("st2", [128, 32])
    nsc = S("nsc", [128, 44])
    negmax = S("negmax", [128, 4])
    sumexp = S("sumexp", [128, 4])
    rsum = S("rsum", [128, 4])
    expw = S("expw", [128, 64])
    wtm = S("wtm", [128, 64], BF16)
    pcA = [S(f"pcA{i}", [64, 96], BF16) for i in range(2)]
    pcB = [S(f"pcB{i}", [128, 96], BF16) for i in range(2)]
    pcX = [S(f"pcX{i}", [79, 96], BF16) for i in range(2)]
    pcY = [S(f"pcY{i}", [15, 96], BF16) for i in range(2)]

    xg_s = S("xg_s", [9, H])
    hpadgT = hpT[0:16, 0:HC * 128]
    vtmp = vtmp2[0:8, 0:128]
    gs1 = S("gs1", [9, 8])
    hpadg = S("hpadg", [128, 144], F32R)
    idnr = S("idnr", [128, 128], F32R)
    projg = S("projg", [128, 128])
    qgev = S("qgev", [128, 8])
    argg = S("argg", [128, 128])
    thg = S("thg", [128, 128])
    scg = S("scg", [16, 1])
    scrow = S("scrow", [1, 16])
    wgrow = S("wgrow", [1, 16])
    wgT = S("wgT", [16, 1])
    mrow = mrow2[0:1, 0:H]
    attng = mrow; x2g = mrow; h2g = mrow
    gls = S("gls", [1, 48])
    h2gfm = S("h2gfm", [128, 8])
    gatesrow = S("gatesrow", [1, 8])
    gbc = S("gbc", [128, 8])
    sv = S("sv", [128, 4])
    idn = S("idn", [128, 128])
    idnb = S("idnb", [128, 128], BF16)
    onesc = S("onesc", [1, 128])
    zrow = S("zrow", [128, 128], BF16)

    wstA = p_s[:, 0:1024]
    wstB = p_s[:, 1024:2048]

    modbuf = None

    pmm = [nc.alloc_psum_tensor(f"pmm{i}", [128, 512], F32) for i in range(4)]
    psc = nc.alloc_psum_tensor("psc", [128, 64], F32)
    pms = [nc.alloc_psum_tensor(f"pms{i}", [128, 512], F32) for i in range(2)]
    pmini = nc.alloc_psum_tensor("pmini", [128, 512], F32)

    ENGS = ("sync", "gpsimd", "tensor", "vector", "scalar")
    prog = {e: [] for e in ENGS}
    semnames = ["ld", "ldg", "xb", "pe", "ve", "ac", "st", "wv"]
    cnt = {s: 0 for s in semnames}
    SEM = {}

    import inspect
    tags = {e: [] for e in ENGS}

    class Eng:
        def __init__(self, name):
            self.name = name
        def emit(self, fn, sem=None, n=1):
            tags[self.name].append(inspect.stack()[1].lineno)
            if sem is None:
                prog[self.name].append(fn)
                return None
            cnt[sem] += n
            v = cnt[sem]
            prog[self.name].append(lambda e, f=fn, s=sem, k=n: f(e).then_inc(SEM[s], k))
            return v
        def wait(self, sem, val):
            tags[self.name].append(-1)
            assert val is not None and val <= cnt[sem], \
                f"[{self.name}] wait {sem}>={val} emitted={cnt[sem]}"
            prog[self.name].append(lambda e, s=sem, v=val: e.wait_ge(SEM[s], v))

    sy, gp, te, ve, ac = (Eng(n) for n in ENGS)

    def vf(fn):
        v = ve.emit(fn, "ve")
        ve.wait("ve", v)
        return v

    def dma(E_, out, in_, sem="ld", **kw):
        return E_.emit(lambda e, o=out, i=in_, kww=dict(kw): e.dma_start(out=o, in_=i, **kww),
                       sem, 16)

    def dmat(E_, out, in_, sem="xb"):
        return E_.emit(lambda e, o=out, i=in_: e.dma_start(out=o, in_=i, transpose=True),
                       sem, 16)

    RAP = bass.AP

    def newton_rsqrt(dst, src, ncols, emit_sem=None, rows=128):
        s0 = nsc[0:rows, 0:ncols]
        s1 = nsc[0:rows, 22:22 + ncols]
        vf(lambda e: e.tensor_single_scalar(out=s1, in_=src, scalar=1.0, op=ALU.add))
        vf(lambda e: e.reciprocal(out=s0, in_=s1))
        for _ in range(6):
            vf(lambda e: e.tensor_tensor(out=s1, in0=s0, in1=s0, op=ALU.mult))
            vf(lambda e: e.tensor_tensor(out=s1, in0=s1, in1=src, op=ALU.mult))
            vf(lambda e: e.tensor_scalar(out=s1, in0=s1, scalar1=-0.5, scalar2=1.5,
                                         op0=ALU.mult, op1=ALU.add))
            vf(lambda e: e.tensor_tensor(out=s0, in0=s0, in1=s1, op=ALU.mult))
        v = vf(lambda e: e.tensor_copy(out=dst, in_=s0))
        if emit_sem is None:
            return v
        cnt[emit_sem] += 1
        vv = cnt[emit_sem]
        prog["vector"].append(lambda e, s=emit_sem: e.tensor_copy(
            out=nsc[0:1, 40:41], in_=nsc[0:1, 40:41]).then_inc(SEM[s], 1))
        return vv

    bank_free = {}
    def pe_bank(nm):
        if nm in bank_free:
            te.wait(bank_free[nm][0], bank_free[nm][1])
    def set_bank_free(nm, sem, val):
        bank_free[nm] = (sem, val)

    dma(sy, idn[:], id_d)
    dma(sy, onesc[:], ones_d)
    dma(sy, mcol[:], RAP(mv_d.tensor, 0, [[1, 128], [128, 17]]),
        allow_slow_non_contiguous=True)
    dma(sy, xg_s[:], xg_d)
    dma(sy, rowa[:], c_d)
    dma(sy, vtmp[:], vr_d)
    ld_small = dma(sy, RAP(wrts, 0, [[64, 128], [8, 8], [1, 8]]),
                   RAP(wrt_d.tensor, 0, [[8, 128], [1024, 8], [1, 8]]),
                   allow_slow_non_contiguous=True)

    z_ve = ve.emit(lambda e: e.memset(zrow[:], 0.0), "ve")
    gp.wait("ve", z_ve)
    dma(gp, idnb[:], id_d, sem="ldg")
    dma(gp, idnr[:], id_d, sem="ldg")
    zw_last = None
    for zi in range(0, WSCR, 128):
        rows = min(128, WSCR - zi)
        zw_last = dma(gp, wsc_d[zi:zi + rows, :], zrow[0:rows, :], sem="ldg")

    ac.wait("ld", ld_small)
    sil_a = ac.emit(lambda e: e.activation(out=rowa[:], in_=rowa[:], func=AF.Silu), "ac")
    te.wait("ac", sil_a)
    t_sil = None
    for c in range(8):
        t_sil = te.emit(lambda e, c=c: e.transpose(
            pmini[0:128, c:c + 1], rowa[0:1, 128 * c:128 * (c + 1)], idn[0:1, 0:1]),
            "pe" if c == 7 else None)
    ve.wait("pe", t_sil)
    silfm = silfm_t[:]
    ve.emit(lambda e: e.tensor_copy(out=vf32[:], in_=pmini[0:128, 0:8]))
    v_silr = ve.emit(lambda e: e.tensor_copy(out=silfm, in_=pmini[0:128, 0:8]), "ve")
    set_bank_free("pmini", "ve", v_silr)

    te.wait("ve", v_silr)
    wm_pe = {}
    st_mod = {}
    wm_banks = [pms[0], pms[1], pmm[0], pmm[1]]
    wm_bname = ["pms0", "pms1", "pmm0", "pmm1"]
    last_view_mm = {}
    last_cp = {}
    for g in range(12):
        for c in range(8):
            vk0 = (0, c % 2)
            if vk0 in last_view_mm:
                gp.wait("pe", last_view_mm[vk0])
            ld0 = dma(gp, wmst_t[0][:, 256 * (c % 2):256 * (c % 2) + 256],
                      wmod_d[128 * c:128 * (c + 1), 512 * g:512 * g + 256], sem="ldg")
            stg1 = xtok[4][:, 256 * (c % 2):256 * (c % 2) + 256]
            if (c % 2) in last_cp:
                sy.wait("ve", last_cp[c % 2])
            ld1 = dma(sy, stg1,
                      wmod_d[128 * c:128 * (c + 1), 512 * g + 256:512 * g + 512], sem="ld")
            ve.wait("ld", ld1)
            vk1 = (1, c % 2)
            if vk1 in last_view_mm:
                ve.wait("pe", last_view_mm[vk1])
            vcp = ve.emit(lambda e, c=c, stg1=stg1: e.tensor_scalar(
                out=wmst_t[1][:, 256 * (c % 2):256 * (c % 2) + 256], in0=stg1,
                scalar1=1.0, scalar2=None, op0=ALU.mult), "ve")
            last_cp[c % 2] = vcp
            for h in range(2):
                bi = (g % 2) * 2 + h
                if c == 0:
                    pe_bank(wm_bname[bi])
                if h == 0:
                    te.wait("ldg", ld0)
                else:
                    te.wait("ve", vcp)
                wm_pe[(g, c, h)] = te.emit(lambda e, g=g, c=c, h=h, bi=bi: e.matmul(
                    wm_banks[bi][0:1, 0:256], silfm[:, c:c + 1],
                    wmst_t[h][:, 256 * (c % 2):256 * (c % 2) + 256],
                    start=(c == 0), stop=(c == 7)), "pe")
                last_view_mm[(h, c % 2)] = wm_pe[(g, c, h)]
        ve.wait("pe", wm_pe[(g, 7, 1)])
        if g > 0:
            ve.wait("ldg", st_mod[g - 1])
        bA, bB = (g % 2) * 2, (g % 2) * 2 + 1
        ve.emit(lambda e, bA=bA: e.tensor_copy(
            out=midb[0:1, 0:256], in_=wm_banks[bA][0:1, 0:256]))
        v_m = ve.emit(lambda e, bB=bB: e.tensor_copy(
            out=midb[0:1, 256:512], in_=wm_banks[bB][0:1, 0:256]), "ve")
        set_bank_free(wm_bname[bA], "ve", v_m)
        set_bank_free(wm_bname[bB], "ve", v_m)
        gp.wait("ve", v_m)
        st_mod[g] = dma(gp, modsc_d[g:g + 1, :], midb[0:1, 0:512], sem="ldg")
    wm_last_pe = wm_pe[(11, 7, 1)]

    sy.wait("ldg", st_mod[11])
    ld_modfm = dma(sy, RAP(modfm, 0, [[48, 128], [8, 6], [1, 8]]),
                   RAP(modsc_d.tensor, 0, [[1, 128], [1024, 6], [128, 8]]),
                   allow_slow_non_contiguous=True)

    ve.wait("ld", ld_modfm)
    vf(lambda e: e.tensor_single_scalar(out=opca[:], in_=modfm[:, 8:16], scalar=1.0, op=ALU.add))
    vf(lambda e: e.tensor_single_scalar(out=opcb[:], in_=modfm[:, 32:40], scalar=1.0, op=ALU.add))

    def clamp_fm(dstt, srcsl):
        vf(lambda e: e.tensor_scalar(out=nsc[:, 0:8], in0=srcsl, scalar1=0.0, scalar2=2.0,
                                     op0=ALU.is_ge, op1=ALU.mult))
        vf(lambda e: e.tensor_single_scalar(out=nsc[:, 0:8], in_=nsc[:, 0:8],
                                            scalar=1.0, op=ALU.subtract))
        vf(lambda e: e.tensor_single_scalar(out=nsc[:, 8:16], in_=srcsl, scalar=-1.0,
                                            op=ALU.mult))
        vf(lambda e: e.tensor_tensor(out=nsc[:, 8:16], in0=nsc[:, 8:16], in1=srcsl,
                                     op=ALU.max))
        vf(lambda e: e.tensor_single_scalar(out=nsc[:, 8:16], in_=nsc[:, 8:16],
                                            scalar=0.25, op=ALU.max))
        vf(lambda e: e.tensor_tensor(out=dstt, in0=nsc[:, 0:8], in1=nsc[:, 8:16],
                                     op=ALU.mult))
    clamp_fm(gapr[:], modfm[:, 16:24])
    clamp_fm(gbpr[:], modfm[:, 40:48])
    vf(lambda e: e.reciprocal(out=nsc[:, 0:8], in_=opca[:]))
    vf(lambda e: e.tensor_tensor(out=fmpack[:, 0:8], in0=modfm[:, 0:8], in1=nsc[:, 0:8],
                                 op=ALU.mult))
    vf(lambda e: e.tensor_single_scalar(out=fmpack[:, 0:8], in_=fmpack[:, 0:8],
                                        scalar=-1.0, op=ALU.mult))
    vf(lambda e: e.tensor_tensor(out=fmpack[:, 8:16], in0=gapr[:], in1=opca[:], op=ALU.mult))
    vf(lambda e: e.tensor_tensor(out=fmpack[:, 16:24], in0=gapr[:], in1=modfm[:, 0:8],
                                 op=ALU.mult))
    vf(lambda e: e.tensor_copy(out=fmpack[:, 24:32], in_=gapr[:]))
    vf(lambda e: e.tensor_copy(out=fmpack[:, 32:40], in_=opcb[:]))
    v_pack = ve.emit(lambda e: e.tensor_copy(out=fmpack[:, 40:48], in_=modfm[:, 24:32]), "ve")

    te.wait("ve", v_pack)
    pe_bank("pmini")
    t_pack = te.emit(lambda e: e.transpose(pmini[0:48, 0:128], fmpack[:], idn[:]), "pe")
    ve.wait("pe", t_pack)
    v_packs = ve.emit(lambda e: e.tensor_copy(out=packst[0:48, 0:128], in_=pmini[0:48, 0:128]), "ve")
    set_bank_free("pmini", "ve", v_packs)
    gp.wait("ve", v_packs)
    st_pack = dma(gp, RAP(rowsc_d.tensor, 0, [[128, 48], [1, 128]]), packst[0:48, 0:128], sem="ldg")
    sy.wait("ldg", st_pack)

    def bcast_row(dst_sb, row_idx):
        ldr = dma(sy, rowa[:], rowsc_d[row_idx:row_idx + 1, :])
        last = None
        for hh in range(2):
            bank = f"pms{hh}"
            pe_bank(bank)
            te.wait("ld", ldr)
            mmv = te.emit(lambda e, hh=hh: e.matmul(
                pms[hh][:, :], onesc[:], rowa[0:1, 512 * hh:512 * (hh + 1)],
                start=True, stop=True), "pe")
            ve.wait("pe", mmv)
            last = ve.emit(lambda e, hh=hh, d=dst_sb: e.tensor_copy(
                out=d[:, 512 * hh:512 * (hh + 1)], in_=pms[hh][:, :]), "ve")
            set_bank_free(bank, "ve", last)
        sy.wait("pe", cnt["pe"])
        return last
    bcast_row(padbc, 0)
    bcast_row(gcbc, 1)
    v_bcast = bcast_row(gsbc, 2)

    pe_bank("pmini")
    t_v = te.emit(lambda e: e.transpose(pmini[0:128, 0:8], vtmp, idn[0:8, 0:8]), "pe")
    ve.wait("pe", t_v)
    ve.emit(lambda e: e.tensor_copy(out=vb16[:], in_=pmini[0:128, 0:8]))
    v_v = ve.emit(lambda e: e.tensor_copy(out=vf32[:], in_=pmini[0:128, 0:8]), "ve")
    set_bank_free("pmini", "ve", v_v)

    if debug:
        gp.wait("ve", v_v)
        dma(gp, dbg["d_mod"], modfm[:], sem="st")
        dma(gp, dbg["d_fmp"], fmpack[:], sem="st")

    vf(lambda e: e.reduce_sum(out=gs1[:, 0:1], in_=xg_s[:], axis=AXX))
    ac.wait("ld", ld_small)
    a_msq = ac.emit(lambda e: e.activation(out=hpadgT[0:9, :], in_=xg_s[:], func=AF.Square,
                                           accum_out=gs1[:, 1:2]), "ac")
    ve.wait("ac", a_msq)
    vf(lambda e: e.tensor_single_scalar(out=gs1[:, 2:3], in_=gs1[:, 0:1],
                                        scalar=1.0 / H, op=ALU.mult))
    vf(lambda e: e.tensor_tensor(out=gs1[:, 3:4], in0=gs1[:, 2:3], in1=gs1[:, 2:3],
                                 op=ALU.mult))
    vf(lambda e: e.tensor_single_scalar(out=gs1[:, 4:5], in_=gs1[:, 1:2],
                                        scalar=1.0 / H, op=ALU.mult))
    vf(lambda e: e.tensor_tensor(out=gs1[:, 4:5], in0=gs1[:, 4:5], in1=gs1[:, 3:4],
                                 op=ALU.subtract))
    vf(lambda e: e.tensor_single_scalar(out=gs1[:, 4:5], in_=gs1[:, 4:5],
                                        scalar=EPS, op=ALU.add))
    newton_rsqrt(gs1[:, 5:6], gs1[:, 4:5], 1, rows=9)
    xngv = hpT[0:9, 0:H]
    v_mln = ve.emit(lambda e: e.tensor_scalar(out=xngv, in0=xg_s[:],
                                              scalar1=gs1[:, 2:3], scalar2=gs1[:, 5:6],
                                              op0=ALU.subtract, op1=ALU.mult), "ve")
    te.wait("ve", v_mln)
    pe_bank("pmini")
    t_xng = None
    for c in range(8):
        t_xng = te.emit(lambda e, c=c: e.transpose(
            pmini[0:128, 9 * c:9 * (c + 1)], hpT[0:9, 128 * c:128 * (c + 1)],
            idn[0:9, 0:9]),
            "pe" if c == 7 else None)
    ve.wait("pe", t_xng)
    vf(lambda e: e.memset(hpadg[:].bitcast(F32), 0.0))
    for c in range(8):
        ve.emit(lambda e, c=c: e.tensor_scalar(
            out=hpadg[:, 18 * c + 7:18 * c + 16], in0=pmini[0:128, 9 * c:9 * (c + 1)],
            scalar1=opca[:, c:c + 1], scalar2=modfm[:, c:c + 1],
            op0=ALU.mult, op1=ALU.add))
    v_hpg = ve.emit(lambda e: e.tensor_copy(
        out=RAP(hpadg, 16, [[144, 128], [18, 8]]), in_=modfm[:, 0:8]), "ve")
    set_bank_free("pmini", "ve", v_hpg)

    pe_bank("pmini")
    pe_bank("pms0")
    te.wait("ve", v_hpg)
    gp.wait("pe", wm_last_pe)
    wv_marks = {}
    step = 0
    first_flag = {0: True, 1: True}
    for wi in range(2):
        wd = w2_d if wi == 0 else w1_d
        wb = w2b if wi == 0 else w1b
        for c in range(8):
            for half in range(2):
                buf = step % 2
                if step >= 2:
                    gp.wait("wv", wv_marks[step - 2])
                ldw = dma(gp, wmst_t[buf][:],
                          wd[128 * c:128 * (c + 1), 512 * half:512 * (half + 1)], sem="ldg")
                ve.wait("ldg", ldw)
                v_cast = ve.emit(lambda e, wb=wb, c=c, half=half, buf=buf: e.tensor_scalar(
                    out=wb[:, 1024 * c + 512 * half:1024 * c + 512 * (half + 1)],
                    in0=wmst_t[buf][:],
                    scalar1=opca[:, c:c + 1], scalar2=None, op0=ALU.mult), "ve")
                te.wait("ve", v_cast)
                for m in range(4):
                    mg = 4 * half + m
                    st_f = first_flag[wi]
                    first_flag[wi] = False
                    sp_f = (c == 7 and mg == 7)
                    if wi == 0:
                        te.emit(lambda e, c=c, m=m, mg=mg, buf=buf, st_f=st_f, sp_f=sp_f:
                                e.matmul(
                            pmini[0:128, 18 * mg:18 * mg + 18],
                            wmst_t[buf][:, 128 * m:128 * (m + 1)],
                            hpadg[:, 18 * c:18 * c + 18],
                            start=st_f, stop=sp_f, skip_group_check=True))
                    else:
                        te.emit(lambda e, c=c, m=m, mg=mg, buf=buf, st_f=st_f, sp_f=sp_f:
                                e.matmul(
                            RAP(pms[0], 140 + 16 * mg, [[512, 128], [1, 2]]),
                            wmst_t[buf][:, 128 * m:128 * (m + 1)],
                            RAP(hpadg, 18 * c + 7, [[144, 128], [9, 2]]),
                            start=st_f, stop=sp_f, skip_group_check=True))
                wv_marks[step] = te.emit(lambda e: e.matmul(
                    pmini[0:1, 500:501], idn[0:1, 0:1], idn[0:1, 0:1], start=False,
                    stop=False, skip_group_check=True), "wv")
                step += 1
    qg_done = te.emit(lambda e: e.matmul(
        pmini[0:1, 502:503], idn[0:1, 0:1], idn[0:1, 0:1], start=False, stop=False, skip_group_check=True), "pe")

    ve.wait("pe", qg_done)
    vf(lambda e: e.tensor_copy(
        out=RAP(projg, 0, [[128, 128], [16, 8], [1, 16]]),
        in_=RAP(pmini, 0, [[512, 128], [18, 8], [1, 16]])))
    vf(lambda e: e.tensor_copy(
        out=c2fm[:], in_=RAP(pmini, 16, [[512, 128], [18, 8], [1, 1]])))
    vf(lambda e: e.tensor_copy(
        out=c1fm[:], in_=RAP(pms[0], 141, [[512, 128], [16, 8], [1, 1]])))
    v_pj = vf(lambda e: e.tensor_copy(
        out=qgev[:], in_=RAP(pms[0], 140, [[512, 128], [16, 8], [1, 1]])))
    set_bank_free("pmini", "ve", v_pj)
    set_bank_free("pms0", "ve", v_pj)

    v_argg = None
    for c in range(8):
        v_argg = vf(lambda e, c=c: e.tensor_scalar(
            out=argg[:, 16 * c:16 * (c + 1)], in0=projg[:, 16 * c:16 * (c + 1)],
            scalar1=qgev[:, c:c + 1], scalar2=None, op0=ALU.add))
    ac.wait("ve", v_argg)
    a_thg = ac.emit(lambda e: e.activation(out=thg[:], in_=argg[:], func=AF.Tanh), "ac")
    te.wait("ac", a_thg)
    pe_bank("pmini")
    sc_mmv = None
    for c in range(8):
        sc_mmv = te.emit(lambda e, c=c: e.matmul(
            pmini[0:16, 0:1], thg[:, 16 * c:16 * (c + 1)], vf32[:, c:c + 1],
            start=(c == 0), stop=(c == 7)), "pe" if c == 7 else None)
    ve.wait("pe", sc_mmv)
    v_scg = ve.emit(lambda e: e.tensor_copy(out=scg[:], in_=pmini[0:16, 0:1]), "ve")
    set_bank_free("pmini", "ve", v_scg)
    te.wait("ve", v_scg)
    pe_bank("pmini")
    t_scg = te.emit(lambda e: e.transpose(pmini[0:1, 0:16], scg[:], idn[0:16, 0:16]), "pe")
    ve.wait("pe", t_scg)
    v_scr = vf(lambda e: e.tensor_copy(out=scrow[:], in_=pmini[0:1, 0:16]))
    set_bank_free("pmini", "ve", v_scr)
    vf(lambda e: e.tensor_reduce(out=gls[0:1, 0:1], in_=scrow[:], axis=AXX, op=ALU.max))
    vf(lambda e: e.tensor_single_scalar(out=gls[0:1, 0:1], in_=gls[0:1, 0:1],
                                        scalar=-1.0, op=ALU.mult))
    v_nm0 = ve.emit(lambda e: e.tensor_copy(out=gls[0:1, 1:2], in_=gls[0:1, 0:1]), "ve")
    ac.wait("ve", v_nm0)
    a_eg = ac.emit(lambda e: e.activation(out=wgrow[:], in_=scrow[:], func=AF.Exp,
                                          bias=gls[0:1, 0:1], accum_out=gls[0:1, 2:3]), "ac")
    ve.wait("ac", a_eg)
    vf(lambda e: e.reciprocal(out=gls[0:1, 3:4], in_=gls[0:1, 2:3]))
    v_wg = ve.emit(lambda e: e.tensor_scalar(out=wgrow[:], in0=wgrow[:],
                                             scalar1=gls[0:1, 3:4], scalar2=None,
                                             op0=ALU.mult), "ve")
    te.wait("ve", v_wg)
    pe_bank("pmini")
    for c in range(8):
        t_h = te.emit(lambda e, c=c: e.transpose(
            pmini[0:16, 0:128].bitcast(F32R), hpadg[:, 18 * c:18 * c + 16],
            idnr[:]), "pe")
        ve.wait("pe", t_h)
        v_h = ve.emit(lambda e, c=c: e.tensor_copy(
            out=hpadgT[0:16, 128 * c:128 * (c + 1)], in_=pmini[0:16, 0:128]), "ve")
        te.wait("ve", v_h)
    t_wgT = te.emit(lambda e: e.transpose(pmini[0:16, 130:131], wgrow[:], idn[0:1, 0:1]), "pe")
    ve.wait("pe", t_wgT)
    v_wgT = ve.emit(lambda e: e.tensor_copy(out=wgT[:], in_=pmini[0:16, 130:131]), "ve")
    set_bank_free("pmini", "ve", v_wgT)
    te.wait("ve", v_wgT)
    pe_bank("pms0"); pe_bank("pms1")
    mm_at = {}
    for hh in range(2):
        mm_at[hh] = te.emit(lambda e, hh=hh: e.matmul(
            pms[hh][0:1, :], wgT[:], hpadgT[0:16, 512 * hh:512 * (hh + 1)],
            start=True, stop=True), "pe")
    ve.wait("pe", mm_at[1])
    v_atg = None
    for hh in range(2):
        v_atg = ve.emit(lambda e, hh=hh: e.tensor_copy(
            out=attng[0:1, 512 * hh:512 * (hh + 1)], in_=pms[hh][0:1, :]),
            "ve" if hh == 1 else None)
    set_bank_free("pms0", "ve", v_atg); set_bank_free("pms1", "ve", v_atg)
    ld_ga = dma(sy, rowb[:], rowsc_d[3:4, :])
    ve.wait("ld", ld_ga)
    ve.wait("ve", v_atg)
    vf(lambda e: e.tensor_tensor(out=x2g[:], in0=attng[:], in1=rowb[:], op=ALU.mult))
    vf(lambda e: e.tensor_tensor(out=x2g[:], in0=x2g[:], in1=xg_s[0:1, :], op=ALU.add))
    v_x2g = vf(lambda e: e.reduce_sum(out=gls[0:1, 8:9], in_=x2g[:], axis=AXX))
    ac.wait("ve", v_x2g)
    a_msq2 = ac.emit(lambda e: e.activation(out=hpadgT[0:1, :], in_=x2g[:], func=AF.Square,
                                            accum_out=gls[0:1, 9:10]), "ac")
    ve.wait("ac", a_msq2)
    vf(lambda e: e.tensor_single_scalar(out=gls[0:1, 10:11], in_=gls[0:1, 8:9],
                                        scalar=1.0 / H, op=ALU.mult))
    vf(lambda e: e.tensor_tensor(out=gls[0:1, 11:12], in0=gls[0:1, 10:11],
                                 in1=gls[0:1, 10:11], op=ALU.mult))
    vf(lambda e: e.tensor_single_scalar(out=gls[0:1, 12:13], in_=gls[0:1, 9:10],
                                        scalar=1.0 / H, op=ALU.mult))
    vf(lambda e: e.tensor_tensor(out=gls[0:1, 12:13], in0=gls[0:1, 12:13],
                                 in1=gls[0:1, 11:12], op=ALU.subtract))
    vf(lambda e: e.tensor_single_scalar(out=gls[0:1, 12:13], in_=gls[0:1, 12:13],
                                        scalar=EPS, op=ALU.add))
    newton_rsqrt(gls[0:1, 13:14], gls[0:1, 12:13], 1, rows=1)
    vf(lambda e: e.tensor_scalar(out=h2g[:], in0=x2g[:], scalar1=gls[0:1, 10:11],
                                 scalar2=gls[0:1, 13:14], op0=ALU.subtract, op1=ALU.mult))
    sy.wait("ve", v_x2g)
    ld_ocb = dma(sy, rowb[:], rowsc_d[4:5, :])
    ve.wait("ld", ld_ocb)
    v_m1 = vf(lambda e: e.tensor_tensor(out=h2g[:], in0=h2g[:], in1=rowb[:],
                                        op=ALU.mult))
    sy.wait("ve", v_m1)
    ld_sb = dma(sy, rowb[:], rowsc_d[5:6, :])
    ve.wait("ld", ld_sb)
    v_h2g = vf(lambda e: e.tensor_tensor(out=h2g[:], in0=h2g[:], in1=rowb[:],
                                         op=ALU.add))
    te.wait("ve", v_h2g)
    pe_bank("pmini")
    t_h2g = None
    for c in range(8):
        t_h2g = te.emit(lambda e, c=c: e.transpose(
            pmini[0:128, 140 + c:141 + c], h2g[0:1, 128 * c:128 * (c + 1)],
            idn[0:1, 0:1]),
            "pe" if c == 7 else None)
    ve.wait("pe", t_h2g)
    v_h2gfm = vf(lambda e: e.tensor_copy(out=h2gfm[:], in_=pmini[0:128, 140:148]))
    te.wait("ve", v_h2gfm)
    lg_mm = None
    for c in range(8):
        lg_mm = te.emit(lambda e, c=c: e.matmul(
            pmini[0:8, 0:1], wrts[:, 8 * c:8 * (c + 1)], h2gfm[:, c:c + 1],
            start=(c == 0), stop=(c == 7)), "pe" if c == 7 else None)
    ve.wait("pe", lg_mm)
    v_lg = ve.emit(lambda e: e.tensor_copy(out=scg[0:8, 0:1], in_=pmini[0:8, 0:1]), "ve")
    te.wait("ve", v_lg)
    t_lg = te.emit(lambda e: e.transpose(pmini[0:1, 32:40], scg[0:8, 0:1], idn[0:8, 0:8]), "pe")
    ve.wait("pe", t_lg)
    lgrow = gls[0:1, 40:48]
    v_lgr = vf(lambda e: e.tensor_copy(out=lgrow, in_=pmini[0:1, 32:40]))
    set_bank_free("pmini", "ve", v_lgr)
    vf(lambda e: e.tensor_reduce(out=gls[0:1, 16:17], in_=lgrow, axis=AXX, op=ALU.max))
    vf(lambda e: e.tensor_scalar(out=gls[0:1, 24:32], in0=lgrow,
                                      scalar1=gls[0:1, 16:17], scalar2=-1e9,
                                      op0=ALU.is_ge, op1=ALU.mult))
    vf(lambda e: e.tensor_tensor(out=gls[0:1, 24:32], in0=gls[0:1, 24:32], in1=lgrow,
                                      op=ALU.add))
    vf(lambda e: e.tensor_reduce(out=gls[0:1, 17:18], in_=gls[0:1, 24:32], axis=AXX,
                                 op=ALU.max))
    vf(lambda e: e.tensor_scalar(out=gls[0:1, 24:32], in0=lgrow,
                                      scalar1=gls[0:1, 17:18], scalar2=1e4,
                                      op0=ALU.is_ge, op1=ALU.mult))
    vf(lambda e: e.tensor_tensor(out=gls[0:1, 24:32], in0=gls[0:1, 24:32], in1=lgrow,
                                 op=ALU.add))
    vf(lambda e: e.tensor_single_scalar(out=gls[0:1, 24:32], in_=gls[0:1, 24:32],
                                             scalar=1e4, op=ALU.subtract))
    vf(lambda e: e.tensor_reduce(out=gls[0:1, 18:19], in_=gls[0:1, 24:32],
                                       axis=AXX, op=ALU.max))
    v_ml = ve.emit(lambda e: e.tensor_single_scalar(out=gls[0:1, 18:19], in_=gls[0:1, 18:19],
                                                    scalar=-1.0, op=ALU.mult), "ve")
    ac.wait("ve", v_ml)
    a_ge = ac.emit(lambda e: e.activation(out=gatesrow[:], in_=gls[0:1, 24:32], func=AF.Exp,
                                          bias=gls[0:1, 18:19], accum_out=gls[0:1, 19:20]), "ac")
    mini_last_ac = a_ge
    ve.wait("ac", a_ge)
    vf(lambda e: e.reciprocal(out=gls[0:1, 20:21], in_=gls[0:1, 19:20]))
    v_g = ve.emit(lambda e: e.tensor_scalar(out=gatesrow[:], in0=gatesrow[:],
                                            scalar1=gls[0:1, 20:21], scalar2=None,
                                            op0=ALU.mult), "ve")
    te.wait("ve", v_g)
    pe_bank("pms0")
    mm_gb = te.emit(lambda e: e.matmul(pms[0][0:128, 0:8], onesc[:], gatesrow[:],
                                       start=True, stop=True), "pe")
    mini_last_pe = mm_gb
    ve.wait("pe", mm_gb)
    v_gbc = vf(lambda e: e.tensor_copy(out=gbc[:], in_=pms[0][0:128, 0:8]))
    set_bank_free("pms0", "ve", v_gbc)
    for j in range(4):
        vf(lambda e, j=j: e.tensor_copy(out=sv[0:64, j:j + 1], in_=gbc[0:64, 2 * j:2 * j + 1]))
        if j < 3:
            vf(lambda e, j=j: e.tensor_copy(out=sv[64:128, j:j + 1],
                                            in_=gbc[64:128, 2 * j + 1:2 * j + 2]))
    v_sv = ve.emit(lambda e: e.tensor_copy(out=sv[64:128, 3:4], in_=gbc[64:128, 7:8]), "ve")
    if debug:
        gp.wait("ve", v_sv)
        dma(gp, dbg["d_gates"], gatesrow[:], sem="st")
        dma(gp, dbg["d_gdbg"], gls[:], sem="st")
        dma(gp, dbg["d_gbc"], gbc[:], sem="st")

    at_mark = {}
    for e_ in range(8):
        if e_ >= 2:
            gp.wait("pe", at_mark[e_ - 2])
        ld_a = dma(gp, atmp[e_ % 2][:], a_d[e_, :, :], sem="ldg")
        te.wait("ldg", ld_a)
        g, half = e_ // 2, e_ % 2
        for c in range(8):
            pe_bank("pms0")
            t_a = te.emit(lambda e, e_=e_, c=c: e.transpose(
                pms[0][0:128, 0:32].bitcast(BF16), atmp[e_ % 2][:, 128 * c:128 * (c + 1)],
                idnb[0:64, 0:64]), "pe")
            ve.wait("pe", t_a)
            v_a = ve.emit(lambda e, g=g, half=half, c=c: e.tensor_copy(
                out=alh[:, (c * 4 + g) * 128 + 64 * half:(c * 4 + g) * 128 + 64 * (half + 1)],
                in_=pms[0][0:128, 0:32].bitcast(BF16)), "ve")
            set_bank_free("pms0", "ve", v_a)
            te.wait("ve", v_a)
        at_mark[e_] = te.emit(lambda e: e.matmul(
            pmini[0:1, 504:505], idn[0:1, 0:1], idn[0:1, 0:1], start=False, stop=False, skip_group_check=True), "pe")
    bw_mark = {}
    bw_i = 0
    for e_ in range(8):
        g, half = e_ // 2, e_ % 2
        for c in range(8):
            if bw_i >= 2:
                gp.wait("pe", bw_mark[bw_i - 2])
            buf = bw_i % 2
            ld_b = dma(gp, bwtmp[buf][:], bw_d[e_, 128 * c:128 * (c + 1), :], sem="ldg")
            ve.wait("ldg", ld_b)
            v_b = ve.emit(lambda e, buf=buf, c=c: e.tensor_scalar(
                out=bwtmp2[buf][:], in0=bwtmp[buf][:], scalar1=gbpr[:, c:c + 1],
                scalar2=None, op0=ALU.mult), "ve")
            te.wait("ve", v_b)
            pe_bank("pms1")
            t_b = te.emit(lambda e, buf=buf: e.transpose(
                pms[1][0:64, 0:64].bitcast(BF16), bwtmp2[buf][:], idnb[:]), "pe")
            bw_mark[bw_i] = t_b
            ve.wait("pe", t_b)
            v_b2 = ve.emit(lambda e, g=g, half=half, c=c: e.tensor_copy(
                out=bwt[64 * half:64 * (half + 1), 1024 * g + 128 * c:1024 * g + 128 * (c + 1)],
                in_=pms[1][0:64, 0:64].bitcast(BF16)), "ve")
            set_bank_free("pms1", "ve", v_b2)
            bw_i += 1
    v_bwt = v_b2
    if debug:
        gp.wait("ve", v_bwt)
        dma(gp, dbg["d_sv"], sv[:], sem="st")
        dma(gp, dbg["d_alh"], alh[:], sem="st")
        dma(gp, dbg["d_bwt"], bwt[:], sem="st")

    xu_ld = {}
    xu_free = {}
    xn_done = {}
    xn_lastT = {}
    gxn_done = {}
    hp_done = {}
    pq_mm_last = {}
    pq_evac = {}
    wtm_store = {}
    comb_mm = {}
    comb_done = {}
    xtok_ld = {}
    ffn_done = {}
    ffn_tr = {}
    x2_dump = {}
    pending_gxn = []
    sc_bank_free = None

    tile_sc = {}

    def ln1_load_stats(jj, sc_):
        slot = jj % XU_RING
        xslot = jj % XN_RING
        tile_sc[jj] = sc_
        if jj == 16:
            vz = ve.emit(lambda e, s=slot: e.memset(xu[s][:], 0.0), "ve")
            gp.wait("ve", vz)
        if jj >= XU_RING:
            gp.wait("ve", xu_free[jj - XU_RING])
        rows = 128 if jj < 16 else U - 2048
        xu_ld[jj] = dma(gp, xu[slot][0:rows, :], x_d[128 * jj:128 * jj + rows, :], sem="ldg")
        ve.wait("ldg", xu_ld[jj])
        vf(lambda e, s=slot, sc=sc_: e.reduce_sum(out=statb[:, sc:sc + 1], in_=xu[s][:],
                                                  axis=AXX))
        ac.wait("ldg", xu_ld[jj])
        if jj - XN_RING in xn_lastT:
            ac.wait("pe", xn_lastT[jj - XN_RING])
        a_sq = ac.emit(lambda e, s=slot, xs=xslot, sc=sc_: e.activation(
            out=xnr[xs][:], in_=xu[s][:], func=AF.Square,
            accum_out=statb[:, 5 + sc:6 + sc]), "ac")
        return a_sq

    def ln1_stats_batch(sclo, schi, a_sq_last):
        n = schi - sclo
        ve.wait("ac", a_sq_last)
        vf(lambda e: e.tensor_single_scalar(
            out=statb[:, 20 + sclo:20 + schi], in_=statb[:, sclo:schi],
            scalar=1.0 / H, op=ALU.mult))
        vf(lambda e: e.tensor_tensor(
            out=nsc[:, 28:28 + n], in0=statb[:, 20 + sclo:20 + schi],
            in1=statb[:, 20 + sclo:20 + schi], op=ALU.mult))
        vf(lambda e: e.tensor_single_scalar(
            out=statb[:, 10 + sclo:10 + schi], in_=statb[:, 5 + sclo:5 + schi],
            scalar=1.0 / H, op=ALU.mult))
        vf(lambda e: e.tensor_tensor(
            out=statb[:, 10 + sclo:10 + schi], in0=statb[:, 10 + sclo:10 + schi],
            in1=nsc[:, 28:28 + n], op=ALU.subtract))
        vf(lambda e: e.tensor_single_scalar(
            out=statb[:, 10 + sclo:10 + schi], in_=statb[:, 10 + sclo:10 + schi],
            scalar=EPS, op=ALU.add))
        newton_rsqrt(statb[:, 15 + sclo:15 + schi], statb[:, 10 + sclo:10 + schi], n)

    def ln1_norm(jj):
        slot = jj % XU_RING
        xslot = jj % XN_RING
        gslot = jj % GXN_RING
        sc_ = tile_sc[jj]
        v_xn = ve.emit(lambda e, s=slot, xs=xslot, sc=sc_: e.tensor_scalar(
            out=xnr[xs][:], in0=xu[s][:], scalar1=statb[:, 20 + sc:21 + sc],
            scalar2=statb[:, 15 + sc:16 + sc], op0=ALU.subtract, op1=ALU.mult), "ve")
        xu_free[jj] = v_xn
        if jj in (0, 16):
            vf(lambda e, jj=jj: e.tensor_scalar(
                out=nsc[:, 34:35], in0=mcol[:, jj:jj + 1], scalar1=-1.0, scalar2=1.0,
                op0=ALU.mult, op1=ALU.add))
            ve.emit(lambda e: e.tensor_scalar(
                out=sqd[:], in0=padbc[:], scalar1=nsc[:, 34:35], scalar2=None, op0=ALU.mult))
            ve.emit(lambda e, xs=xslot, jj=jj: e.tensor_scalar(
                out=xnr[xs][:], in0=xnr[xs][:], scalar1=mcol[:, jj:jj + 1], scalar2=None,
                op0=ALU.mult))
            v_xn = ve.emit(lambda e, xs=xslot: e.tensor_tensor(
                out=xnr[xs][:], in0=xnr[xs][:], in1=sqd[:], op=ALU.add), "ve")
        xn_done[jj] = v_xn
        if jj >= GXN_RING and jj - GXN_RING not in comb_mm:
            pending_gxn.append(jj)
        else:
            gxn_norm(jj)

    def gxn_norm(jj):
        xslot = jj % XN_RING
        gslot = jj % GXN_RING
        if jj - GXN_RING in comb_mm:
            ve.wait("pe", comb_mm[jj - GXN_RING])
        ve.emit(lambda e, xs=xslot, gs=gslot: e.tensor_tensor(
            out=gxn[gs][:], in0=xnr[xs][:], in1=gcbc[:], op=ALU.mult))
        g_v = ve.emit(lambda e, gs=gslot: e.tensor_tensor(
            out=gxn[gs][:], in0=gxn[gs][:], in1=gsbc[:], op=ALU.add), "ve")
        gxn_done[jj] = g_v
        if debug:
            gp.wait("ve", g_v)
            dma(gp, dbg["d_xn"][128 * jj:128 * jj + 128, :], xnr[xslot][:], sem="st")
            dma(gp, dbg["d_gxn"][128 * jj:128 * jj + 128, :], gxn[gslot][:], sem="st")

    def build_hpad(ut):
        last = None
        for fi, jj in enumerate(range(4 * ut, min(4 * ut + 5, NUT))):
            xslot = jj % XN_RING
            te.wait("ve", xn_done[jj])
            pe_bank("pmm0"); pe_bank("pmm1")
            t_tr = None
            for c in range(HC):
                t_tr = te.emit(lambda e, c=c, xs=xslot: e.matmul(
                    pmm[c // 4][:, 128 * (c % 4):128 * (c % 4) + 128],
                    xnr[xs][:, 128 * c:128 * (c + 1)], idnb[:],
                    start=True, stop=True), "pe" if c == 7 else None)
            xn_lastT[jj] = t_tr
            ve.wait("pe", t_tr)
            if ut > 0 and fi == 0:
                ve.wait("pe", pq_mm_last[ut - 1])
            for c in range(HC):
                last = ve.emit(lambda e, c=c, fi=fi: e.tensor_copy(
                    out=hpad[:, 640 * c + 128 * fi:640 * c + 128 * fi + 128],
                    in_=pmm[c // 4][:, 128 * (c % 4):128 * (c % 4) + 128]),
                    "ve" if c == 7 else None)
            set_bank_free("pmm0", "ve", last); set_bank_free("pmm1", "ve", last)
        hp_done[ut] = last

    def proj_q(ut):
        te.wait("ve", hp_done[ut])
        mm = None
        last_v = None
        for hc in range(HC):
            for seg, (c0, cw) in enumerate([(0, 512), (512, 16)]):
                bank = f"pmm{(2 * hc + seg) % 2}"
                pe_bank(bank)
                for kc in range(HC):
                    mm = te.emit(lambda e, hc=hc, kc=kc, c0=c0, cw=cw, b=(2 * hc + seg) % 2:
                                 e.matmul(pmm[b][:, 0:cw],
                                          w2b[:, 1024 * kc + 128 * hc:1024 * kc + 128 * (hc + 1)],
                                          hpad[:, 640 * kc + c0:640 * kc + c0 + cw],
                                          start=(kc == 0), stop=(kc == 7)),
                                 "pe" if kc == 7 else None)
                ve.wait("pe", mm)
                last_v = ve.emit(lambda e, hc=hc, c0=c0, cw=cw, b=(2 * hc + seg) % 2:
                                 e.tensor_scalar(out=p_s[:, 528 * hc + c0:528 * hc + c0 + cw],
                                                 in0=pmm[b][:, 0:cw],
                                                 scalar1=c2fm[:, hc:hc + 1], scalar2=None,
                                                 op0=ALU.add), "ve")
                set_bank_free(bank, "ve", last_v)
        for hc in range(HC):
            bank = f"pmm{hc % 2}"
            pe_bank(bank)
            for kc in range(HC):
                mm = te.emit(lambda e, hc=hc, kc=kc, b=hc % 2:
                             e.matmul(pmm[b][:, :],
                                      w1b[:, 1024 * kc + 128 * hc:1024 * kc + 128 * (hc + 1)],
                                      hpad[:, 640 * kc + 7:640 * kc + 7 + 512],
                                      start=(kc == 0), stop=(kc == 7)),
                             "pe" if kc == 7 else None)
            ve.wait("pe", mm)
            last_v = ve.emit(lambda e, hc=hc, b=hc % 2:
                             e.tensor_scalar(out=q_s[:, 512 * hc:512 * (hc + 1)],
                                             in0=pmm[b][:, :],
                                             scalar1=c1fm[:, hc:hc + 1], scalar2=None,
                                             op0=ALU.add), "ve")
            set_bank_free(bank, "ve", last_v)
        pq_mm_last[ut] = cnt["pe"]
        pq_evac[ut] = last_v
        if debug and ut == 0:
            gp.wait("ve", last_v)
            dma(gp, dbg["d_p"], p_s[:], sem="st")
            dma(gp, dbg["d_q"], q_s[:], sem="st")

    def score_unit(ut):
        nonlocal sc_bank_free
        if sc_bank_free is not None:
            te.wait("ac", sc_bank_free)
        step = 0
        th_cons = {}
        arg_cons = {}
        mm = None
        arg_dump = th_dump = None
        dump_step = 32
        for k in range(KW):
            for hf in range(4):
                par = step % 2
                if arg_dump is not None and step == dump_step + 2:
                    ve.wait("st", arg_dump)
                if step >= 2:
                    ve.wait("ac", arg_cons[par])
                if step == 0:
                    ve.wait("ve", pq_evac[ut])
                    if ut == 0:
                        ve.wait("pe", mini_last_pe)
                        ve.wait("ac", mini_last_ac)
                v_add = ve.emit(lambda e, par=par, k=k, hf=hf: e.tensor_tensor(
                    out=RAP(argb[par], 0, [[1024, 128], [512, 2], [1, 512]]),
                    in0=RAP(p_s, 528 * 2 * hf + k, [[HC * 528, 128], [528, 2], [1, 512]]),
                    in1=RAP(q_s, 512 * 2 * hf, [[HC * 512, 128], [512, 2], [1, 512]]),
                    op=ALU.add), "ve")
                if debug and ut == 0 and step == dump_step:
                    gp.wait("ve", v_add)
                    arg_dump = dma(gp, dbg["d_arg"], argb[par][:], sem="st")
                ac.wait("ve", v_add)
                if step >= 2:
                    ac.wait("pe", th_cons[par])
                if th_dump is not None and step == dump_step + 2:
                    ac.wait("st", th_dump)
                a_th = ac.emit(lambda e, par=par: e.activation(
                    out=thb[par][:], in_=argb[par][:], func=AF.Tanh), "ac")
                arg_cons[par] = a_th
                if debug and ut == 0 and step == dump_step:
                    gp.wait("ac", a_th)
                    th_dump = dma(gp, dbg["d_th"], thb[par][:], sem="st")
                te.wait("ac", a_th)
                for cl in range(2):
                    for tb in range(4):
                        first = (k == 0 and hf == 0 and cl == 0 and tb == 0)
                        last = (k == KW - 1 and hf == 3 and cl == 1 and tb == 3)
                        mm = te.emit(lambda e, par=par, k=k, hf=hf, cl=cl, tb=tb,
                                     first=first, last=last: e.matmul(
                            psc[:, 16 * tb + k:16 * tb + k + 1],
                            thb[par][:, 512 * cl + 128 * tb:512 * cl + 128 * tb + 128],
                            vb16[:, 2 * hf + cl:2 * hf + cl + 1],
                            start=first, stop=last, skip_group_check=True),
                            "pe" if (cl == 1 and tb == 3) else None)
                th_cons[par] = mm
                step += 1
        ve.wait("pe", mm)
        dsc_st = None
        if debug and ut == 0:
            v_dbg = ve.emit(lambda e: e.tensor_copy(out=expw[:], in_=psc[:]), "ve")
            gp.wait("ve", v_dbg)
            dsc_st = dma(gp, dbg["d_sc"], expw[:], sem="st")
        for tb in range(4):
            vf(lambda e, tb=tb: e.tensor_reduce(
                out=negmax[:, tb:tb + 1], in_=psc[:, 16 * tb:16 * (tb + 1)], axis=AXX,
                op=ALU.max))
        v_nm = ve.emit(lambda e: e.tensor_single_scalar(
            out=negmax[:], in_=negmax[:], scalar=-1.0, op=ALU.mult), "ve")
        ac.wait("ve", v_nm)
        if dsc_st is not None:
            ac.wait("st", dsc_st)
        a_exp = None
        for tb in range(4):
            a_exp = ac.emit(lambda e, tb=tb: e.activation(
                out=expw[:, 16 * tb:16 * (tb + 1)], in_=psc[:, 16 * tb:16 * (tb + 1)],
                func=AF.Exp, bias=negmax[:, tb:tb + 1], accum_out=sumexp[:, tb:tb + 1]),
                "ac" if tb == 3 else None)
        sc_bank_free = a_exp
        ve.wait("ac", a_exp)
        if ut > 0:
            ve.wait("ldg", wtm_store[4 * (ut - 1) + 3])
        vf(lambda e: e.reciprocal(out=rsum[:], in_=sumexp[:]))
        v_w = None
        for tb in range(4):
            v_w = ve.emit(lambda e, tb=tb: e.tensor_scalar(
                out=wtm[:, 16 * tb:16 * (tb + 1)], in0=expw[:, 16 * tb:16 * (tb + 1)],
                scalar1=rsum[:, tb:tb + 1], scalar2=None, op0=ALU.mult),
                "ve" if tb == 3 else None)
        if debug and ut == 0:
            gp.wait("ve", v_w)
            dma(gp, dbg["d_w"], RAP(wtm, 0, [[64, 128], [1, 64]]), sem="st")
        gp.wait("ve", v_w)
        for tb in range(4):
            t0r = 15 + 512 * ut + 128 * tb
            wtm_store[4 * ut + tb] = dma(
                gp, wsc_d[t0r:t0r + 128, 0:16], RAP(wtm, 16 * tb, [[64, 128], [1, 16]]),
                sem="ldg")

    def load_xtok(j):
        slot = j % XTOK_RING
        if j >= XTOK_RING:
            gp.wait("st", ffn_done[j - XTOK_RING])
        xtok_ld[j] = dma(gp, xtok[slot][:], x_d[7 + 128 * j:7 + 128 * j + 128, :], sem="ldg")

    def combine_block(B):
        t0 = 128 * B
        par = B % 2
        need_blk = min(B + 1, NTOK - 1)
        gp.wait("ldg", wtm_store[need_blk])
        gp.wait("ldg", zw_last)
        if B >= 2:
            gp.wait("pe", comb_mm[B - 2])
        dma(gp, pcA[par][0:64, 15:79],
            RAP(wsc_d.tensor, WSCW * (15 + t0), [[1, 64], [WSCW - 1, 64]]),
            sem="ldg", allow_slow_non_contiguous=True)
        dma(gp, pcB[par][64:128, 15:79],
            RAP(wsc_d.tensor, WSCW * (15 + t0 + 64), [[1, 64], [WSCW - 1, 64]]),
            sem="ldg", allow_slow_non_contiguous=True)
        dma(gp, pcX[par][64:79, 15:79],
            RAP(wsc_d.tensor, WSCW * (15 + t0) + 64, [[1, 15], [WSCW - 1, 64]]),
            sem="ldg", allow_slow_non_contiguous=True)
        ldY = dma(gp, pcY[par][0:15, 15:79],
                  RAP(wsc_d.tensor, WSCW * (15 + t0 + 64) + 64, [[1, 15], [WSCW - 1, 64]]),
                  sem="ldg", allow_slow_non_contiguous=True)
        if debug and B == 1:
            gp.wait("ldg", ldY)
            dma(gp, dbg["d_pcA"], pcA[par][:], sem="st")
            dma(gp, dbg["d_pcX"], pcX[par][64:79, :], sem="st")
            dma(gp, dbg["d_pcB"], pcB[par][64:128, :], sem="st")
            st_pc = dma(gp, dbg["d_pcY"], pcY[par][0:15, :], sem="st")
            ve.wait("st", st_pc)
        te.wait("ldg", ldY)
        te.wait("ve", gxn_done[B])
        te.wait("ve", gxn_done[min(B + 1, NUT - 1)])
        gB = gxn[B % GXN_RING]
        gB1 = gxn[(B + 1) % GXN_RING]
        pe_bank("pms0"); pe_bank("pms1")
        mm = None
        for hh in range(2):
            plist = [
                (pcA[par][0:64, 15:79], gB, 0, 0, True),
                (pcX[par][64:79, 15:79], gB, 64, 0, False),
                (pcB[par][64:128, 15:79], gB, 64, 64, True),
                (pcY[par][0:15, 15:79], gB1, 0, 64, False),
            ]
            for pi, (lh, rhs_t, rb, g0, stf) in enumerate(plist):
                rows = lh.shape[0]
                tp = (rb, g0) if (rb or g0) else None
                mm = te.emit(lambda e, lh=lh, rows=rows, rhs_t=rhs_t, rb=rb, g0=g0,
                             hh=hh, tp=tp, stf=stf: e.matmul(
                    pms[hh][g0:g0 + 64, :],
                    lh,
                    rhs_t[rb:rb + rows, 512 * hh:512 * (hh + 1)],
                    start=stf, stop=False, tile_position=tp),
                    "pe" if (hh == 1 and pi == 3) else None)
        comb_mm[B] = mm
        tslot = B % XTOK_RING
        ve.wait("pe", mm)
        if debug and B == 1:
            ve.emit(lambda e: e.tensor_copy(out=argb[0][:, 0:512], in_=pms[0][:]))
            v_att = ve.emit(lambda e: e.tensor_copy(out=argb[0][:, 512:1024], in_=pms[1][:]),
                            "ve")
            gp.wait("ve", v_att)
            st_att = dma(gp, dbg["d_att"], argb[0][:], sem="st")
            ve.wait("st", st_att)
        ve.wait("ldg", xtok_ld[B])
        vv = None
        for hh in range(2):
            vv = ve.emit(lambda e, hh=hh, ts=tslot: e.tensor_tensor(
                out=xtok[ts][:, 512 * hh:512 * (hh + 1)],
                in0=pms[hh][:, :], in1=xtok[ts][:, 512 * hh:512 * (hh + 1)],
                op=ALU.add), "ve" if hh == 1 else None)
        set_bank_free("pms0", "ve", vv); set_bank_free("pms1", "ve", vv)
        comb_done[B] = vv
        if debug:
            gp.wait("ve", vv)
            x2_dump[B] = dma(gp, dbg["d_x2"][128 * B:128 * (B + 1), :], xtok[tslot][:],
                             sem="st")

    def ffn_stats(j, gi):
        tslot = j % XTOK_RING
        vf(lambda e, ts=tslot, gi=gi: e.reduce_sum(out=st2[:, gi:gi + 1],
                                                   in_=xtok[ts][:], axis=AXX))
        ac.wait("ve", comb_done[j])
        a_sq2 = ac.emit(lambda e, ts=tslot, gi=gi: e.activation(
            out=xn2tm[:], in_=xtok[ts][:], func=AF.Square,
            accum_out=st2[:, 6 + gi:7 + gi]), "ac")
        return a_sq2

    def ffn_chain(n, a_last):
        ve.wait("ac", a_last)
        vf(lambda e: e.tensor_single_scalar(out=st2[:, 12:12 + n], in_=st2[:, 0:n],
                                            scalar=1.0 / H, op=ALU.mult))
        vf(lambda e: e.tensor_tensor(out=nsc[:, 28:28 + n], in0=st2[:, 12:12 + n],
                                     in1=st2[:, 12:12 + n], op=ALU.mult))
        vf(lambda e: e.tensor_single_scalar(out=st2[:, 18:18 + n], in_=st2[:, 6:6 + n],
                                            scalar=1.0 / H, op=ALU.mult))
        vf(lambda e: e.tensor_tensor(out=st2[:, 18:18 + n], in0=st2[:, 18:18 + n],
                                     in1=nsc[:, 28:28 + n], op=ALU.subtract))
        vf(lambda e: e.tensor_single_scalar(out=st2[:, 18:18 + n], in_=st2[:, 18:18 + n],
                                            scalar=EPS, op=ALU.add))
        newton_rsqrt(st2[:, 24:24 + n], st2[:, 18:18 + n], n)

    def ffn_rest(j, gi):
        tslot = j % XTOK_RING
        v_xn2 = ve.emit(lambda e, ts=tslot, gi=gi: e.tensor_scalar(
            out=xn2tm[:], in0=xtok[ts][:], scalar1=st2[:, 12 + gi:13 + gi],
            scalar2=st2[:, 24 + gi:25 + gi], op0=ALU.subtract, op1=ALU.mult), "ve")
        te.wait("ve", v_xn2)
        pe_bank("pmm2"); pe_bank("pmm3")
        t_last = None
        for c in range(HC):
            t_last = te.emit(lambda e, c=c: e.matmul(
                pmm[2 + c // 4][:, 128 * (c % 4):128 * (c % 4) + 128],
                xn2tm[:, 128 * c:128 * (c + 1)], idnb[:],
                start=True, stop=True), "pe" if c == 7 else None)
        ve.wait("pe", t_last)
        v_h2 = None
        for c in range(HC):
            v_h2 = ve.emit(lambda e, c=c: e.tensor_scalar(
                out=h2fm[:, 128 * c:128 * (c + 1)],
                in0=pmm[2 + c // 4][:, 128 * (c % 4):128 * (c % 4) + 128],
                scalar1=opcb[:, c:c + 1], scalar2=modfm[:, 24 + c:25 + c],
                op0=ALU.mult, op1=ALU.add), "ve" if c == 7 else None)
        set_bank_free("pmm2", "ve", v_h2); set_bank_free("pmm3", "ve", v_h2)
        if debug and j == 1:
            gp.wait("ve", v_h2)
            dma(gp, dbg["d_h2"], h2fm[:], sem="st")
        te.wait("ve", v_h2)
        if j == 0:
            te.wait("ve", v_sv)
            te.wait("pe", at_mark[7])
            te.wait("ve", v_bwt)
        pe_bank("pmm2")
        mm = None
        for g in range(4):
            for kc in range(HC):
                mm = te.emit(lambda e, g=g, kc=kc: e.matmul(
                    pmm[2][:, 128 * g:128 * (g + 1)],
                    alh[:, (kc * 4 + g) * 128:(kc * 4 + g) * 128 + 128],
                    h2fm[:, 128 * kc:128 * (kc + 1)],
                    start=(kc == 0), stop=(kc == 7)),
                    "pe" if (g == 3 and kc == 7) else None)
        ve.wait("pe", mm)
        v_mid = None
        for g in range(4):
            v_mid = ve.emit(lambda e, g=g: e.tensor_scalar(
                out=midb[:, 128 * g:128 * (g + 1)], in0=pmm[2][:, 128 * g:128 * (g + 1)],
                scalar1=sv[:, g:g + 1], scalar2=None, op0=ALU.mult), "ve" if g == 3 else None)
        set_bank_free("pmm2", "ve", v_mid)
        if debug and j == 1:
            gp.wait("ve", v_mid)
            dma(gp, dbg["d_mid"], midb[:], sem="st")
        v_gh = None
        for c in range(HC):
            v_gh = ve.emit(lambda e, c=c: e.tensor_scalar(
                out=xn2fm[:, 128 * c:128 * (c + 1)], in0=h2fm[:, 128 * c:128 * (c + 1)],
                scalar1=gbpr[:, c:c + 1], scalar2=None, op0=ALU.mult),
                "ve" if c == 7 else None)
        te.wait("ve", v_mid)
        pe_bank("pmm2"); pe_bank("pmm3")
        mm2 = None
        for hh in range(2):
            for g in range(4):
                mm2 = te.emit(lambda e, hh=hh, g=g: e.matmul(
                    pmm[2 + hh][:, :],
                    midb[:, 128 * g:128 * (g + 1)],
                    bwt[:, 1024 * g + 512 * hh:1024 * g + 512 * (hh + 1)],
                    start=(g == 0), stop=False))
        te.wait("ve", v_gh)
        for hh in range(2):
            for ci in range(4):
                c = 4 * hh + ci
                mm2 = te.emit(lambda e, hh=hh, ci=ci, c=c: e.matmul(
                    pmm[2 + hh][:, 128 * ci:128 * (ci + 1)],
                    xn2fm[:, 128 * c:128 * (c + 1)], idnb[:],
                    start=False, stop=(ci == 3)),
                    "pe" if (hh == 1 and ci == 3) else None)
        ffn_tr[j] = mm2
        ve.wait("pe", mm2)
        if debug and j in x2_dump:
            ve.wait("st", x2_dump[j])
        vv = None
        for hh in range(2):
            vv = ve.emit(lambda e, hh=hh, ts=tslot: e.tensor_tensor(
                out=xtok[ts][:, 512 * hh:512 * (hh + 1)], in0=pmm[2 + hh][:, :],
                in1=xtok[ts][:, 512 * hh:512 * (hh + 1)], op=ALU.add), "ve" if hh == 1 else None)
        set_bank_free("pmm2", "ve", vv); set_bank_free("pmm3", "ve", vv)
        gp.wait("ve", vv)
        ffn_done[j] = dma(gp, out_d[128 * j:128 * (j + 1), :], xtok[tslot][:], sem="st")

    next_comb = 0
    for ut in range(NUNIT):
        newt = [jj for jj in range(4 * ut, min(4 * ut + 5, NUT)) if jj not in tile_sc]
        sclo = newt[0] - 4 * ut
        a_last = None
        for jj in newt:
            a_last = ln1_load_stats(jj, jj - 4 * ut)
        ln1_stats_batch(sclo, newt[-1] - 4 * ut + 1, a_last)
        for jj in newt:
            ln1_norm(jj)
        build_hpad(ut)
        proj_q(ut)
        score_unit(ut)
        group = []
        while next_comb < NTOK and (512 * (ut + 1) >= 128 * next_comb + 143 or ut == NUNIT - 1):
            group.append(next_comb)
            next_comb += 1
        for B in group:
            load_xtok(B)
            combine_block(B)
            for jj in [p for p in pending_gxn if p - GXN_RING == B]:
                gxn_norm(jj)
                pending_gxn.remove(jj)
        a_last = None
        for gi, B in enumerate(group):
            a_last = ffn_stats(B, gi)
        if group:
            ffn_chain(len(group), a_last)
        for gi, B in enumerate(group):
            ffn_rest(B, gi)

    gp.wait("st", cnt["st"])
    gp.wait("ldg", cnt["ldg"])
    gp.wait("ld", cnt["ld"])
    gp.wait("xb", cnt["xb"])

    import contextlib
    with contextlib.ExitStack() as stk:
        for s in semnames:
            SEM[s] = stk.enter_context(nc.semaphore(s))
        with nc.Block() as block:
            @block.sync
            def _(e):
                for fi, f in enumerate(prog["sync"]):
                    try:
                        f(e)
                    except Exception as ex:
                        raise RuntimeError(f"sync op index {fi}: {ex}") from ex

            @block.gpsimd
            def _(e):
                for f in prog["gpsimd"]:
                    f(e)

            @block.tensor
            def _(e):
                for fi, f in enumerate(prog["tensor"]):
                    try:
                        f(e)
                    except Exception as ex:
                        raise RuntimeError(f"tensor op index {fi} line {tags['tensor'][fi]}: {ex}") from ex

            @block.vector
            def _(e):
                for f in prog["vector"]:
                    f(e)

            @block.scalar
            def _(e):
                for f in prog["scalar"]:
                    f(e)

    return nc, dbg


def make_in_maps(inputs):
    x = np.asarray(inputs["L0_attn_inputs"], dtype=np.float32)
    w1 = np.ascontiguousarray(inputs["L0_attn_w1"], dtype=np.float32)
    w2 = np.ascontiguousarray(inputs["L0_attn_w2"], dtype=np.float32)
    v = np.asarray(inputs["L0_attn_v"], dtype=np.float32).reshape(1024)
    Wr = np.asarray(inputs["L0_ffn_Wr"], dtype=np.float32)
    A = np.ascontiguousarray(np.asarray(inputs["L0_ffn_A"], dtype=np.float32))
    Bw = np.ascontiguousarray(np.asarray(inputs["L0_ffn_Bw"], dtype=np.float32))
    c = np.asarray(inputs["L0_c"], dtype=np.float32)
    wmod = np.ascontiguousarray(inputs["L0_wmod"], dtype=np.float32)

    v_r = np.ascontiguousarray(v.reshape(8, 128))
    wrt = np.ascontiguousarray(Wr.T)
    ident = np.eye(128, dtype=np.float32)
    onescol = np.ones((1, 128), dtype=np.float32)

    in_maps = []
    for core in range(8):
        b = core // 4
        seg = core % 4
        s0 = 2048 * seg
        lo = s0 - 7
        hi = s0 + 2048 + 8
        xs = np.zeros((UPAD, H), dtype=np.float32)
        a0, a1 = max(lo, 0), min(hi, 8192)
        xs[a0 - lo:a1 - lo, :] = x[b, a0:a1, :]
        mv = np.zeros((UPAD, 1), dtype=np.float32)
        mv[a0 - lo:a1 - lo, 0] = 1.0
        xg = np.ascontiguousarray(x[b, 0:9, :])
        in_maps.append({
            "x_slice": xs, "w1": w1, "w2": w2, "v_r": v_r, "wrt": wrt,
            "A": A, "Bw": Bw, "c_row": c[b:b + 1, :], "wmod": wmod,
            "mvec": mv, "xg": xg, "ident": ident, "onescol": onescol,
        })
    return in_maps


def kernel(**inputs):
    nc, _ = build_nc(debug=False)
    in_maps = make_in_maps(inputs)
    res = run_bass_kernel_spmd(nc, in_maps, core_ids=list(range(8)))
    out = np.zeros((2, 8192, 1024), dtype=np.float32)
    for core in range(8):
        b = core // 4
        s0 = 2048 * (core % 4)
        out[b, s0:s0 + 2048, :] = np.asarray(res.results[core]["out"])
    return out


# version 07
# speedup vs baseline: 1.2376x; 1.0953x over previous
import numpy as np
import concourse.bass as bass
import concourse.mybir as mybir
from concourse.bass_utils import run_bass_kernel_spmd

F32 = mybir.dt.float32
F32R = mybir.dt.float32r
BF16 = mybir.dt.bfloat16
I32 = mybir.dt.int32
AF = mybir.ActivationFunctionType
ALU = mybir.AluOpType
AXX = mybir.AxisListType.X

H = 1024
HC = 8
T = 2048
KW = 16
U = T + KW - 1
UPAD = 17 * 128
EPS = 1e-5
NUT = 17
NTOK = 16
NUNIT = 4
MAGIC = 0x5f3759df

XTOK_RING = 5
XU_RING = 5
XN_RING = 5
GXN_RING = 5


def build_nc(debug=False):
    nc = bass.Bass("TRN2", target_bir_lowering=False)

    x_d = nc.dram_tensor("x_slice", [UPAD, H], F32, kind="ExternalInput").ap()
    w1_d = nc.dram_tensor("w1", [H, H], F32, kind="ExternalInput").ap()
    w2_d = nc.dram_tensor("w2", [H, H], F32, kind="ExternalInput").ap()
    vr_d = nc.dram_tensor("v_r", [8, 128], F32, kind="ExternalInput").ap()
    wrt_d = nc.dram_tensor("wrt", [H, 8], F32, kind="ExternalInput").ap()
    a_d = nc.dram_tensor("A", [8, 64, H], F32, kind="ExternalInput").ap()
    bw_d = nc.dram_tensor("Bw", [8, H, 64], F32, kind="ExternalInput").ap()
    c_d = nc.dram_tensor("c_row", [1, H], F32, kind="ExternalInput").ap()
    wmod_d = nc.dram_tensor("wmod", [H, 6 * H], F32, kind="ExternalInput").ap()
    mv_d = nc.dram_tensor("mvec", [UPAD, 1], F32, kind="ExternalInput").ap()
    xg_d = nc.dram_tensor("xg", [9, H], F32, kind="ExternalInput").ap()
    id_d = nc.dram_tensor("ident", [128, 128], F32, kind="ExternalInput").ap()
    ones_d = nc.dram_tensor("onescol", [1, 128], F32, kind="ExternalInput").ap()
    out_d = nc.dram_tensor("out", [T, H], F32, kind="ExternalOutput").ap()

    WSCW = 128
    WSCR = 15 + T + 16
    wsc_d = nc.dram_tensor("wsc", [WSCR, WSCW], BF16, kind="Internal").ap()
    rowsc_d = nc.dram_tensor("rowsc", [6, H], F32, kind="Internal").ap()
    modsc_d = nc.dram_tensor("modsc", [12, 512], F32, kind="Internal").ap()

    dbg = {}
    def dbg_t(nm, shp):
        if debug:
            dbg[nm] = nc.dram_tensor(nm, shp, F32, kind="ExternalOutput").ap()
        return dbg.get(nm)
    for nm, shp in [("d_mod", [128, 48]), ("d_fmp", [128, 48]), ("d_xn", [UPAD, H]),
                    ("d_gxn", [UPAD, H]), ("d_p", [128, HC * 528]), ("d_q", [128, HC * 512]),
                    ("d_sc", [128, 64]), ("d_w", [128, 64]), ("d_x2", [T, H]),
                    ("d_gates", [1, 8]), ("d_h2", [128, HC * 128]), ("d_mid", [128, 512]),
                    ("d_gdbg", [1, 48]), ("d_arg", [128, 1024]), ("d_th", [128, 1024]),
                    ("d_sv", [128, 4]), ("d_alh", [128, 4096]), ("d_bwt", [128, 4096]),
                    ("d_att", [128, 1024]), ("d_pcA", [64, 96]), ("d_pcX", [15, 96]),
                    ("d_pcB", [64, 96]), ("d_pcY", [15, 96]), ("d_gbc", [128, 8])]:
        dbg_t(nm, shp)

    S = lambda name, shape, dt=F32: nc.alloc_sbuf_tensor(name, shape, dt)
    xtok = [S(f"xtok{j}", [128, H]) for j in range(XTOK_RING)]
    xu = [S(f"xu{j}", [128, H]) for j in range(XU_RING)]
    xnr = [S(f"xn{i}", [128, H], BF16) for i in range(XN_RING)]
    gxn = [S(f"gxn{j}", [128, H], BF16) for j in range(GXN_RING)]
    hpad = S("hpad", [128, HC * 640], BF16)
    p_s = S("p_s", [128, HC * 528], BF16)
    q_s = S("q_s", [128, HC * 512], BF16)
    argb = [S(f"arg{i}", [128, 1024], BF16) for i in range(2)]
    hpT = S("hpT", [16, 1024])
    vtmp2 = S("vtmp2", [8, 128])
    mrow2 = S("mrow2", [1, H])
    packst = S("packst", [48, 128])
    thb = [S(f"th{i}", [128, 1024], BF16) for i in range(2)]
    w1b = S("w1b", [128, HC * H], BF16)
    w2b = S("w2b", [128, HC * H], BF16)
    alh = S("alh", [128, 32 * 128], BF16)
    bwt = S("bwt", [128, 4 * H], BF16)
    midb = S("midb", [128, 512], BF16)
    xn2tm = S("xn2tm", [128, H], BF16)
    xn2fm = S("xn2fm", [128, HC * 128], BF16)
    h2fm = S("h2fm", [128, HC * 128], BF16)
    atmp = [S(f"atmp{i}", [64, H], BF16) for i in range(2)]
    bwtmp = [S(f"bwtmp{i}", [128, 64]) for i in range(2)]
    bwtmp2 = [S(f"bwtmp2{i}", [128, 64], BF16) for i in range(2)]

    modfm = S("modfm", [128, 48])
    opca = S("opca", [128, 8])
    opcb = S("opcb", [128, 8])
    gapr = S("gapr", [128, 8])
    gbpr = S("gbpr", [128, 8])
    fmpack = S("fmpack", [128, 48])
    rowa = S("rowa", [1, H])
    rowb = S("rowb", [1, H])
    padbc = S("padbc", [128, H], BF16)
    gcbc = S("gcbc", [128, H], BF16)
    gsbc = S("gsbc", [128, H], BF16)
    mcol = S("mcol", [128, 17])
    silfm_t = S("silfm_t", [128, 8], F32R)
    c2fm = S("c2fm", [128, 8])
    c1fm = S("c1fm", [128, 8])
    safm = S("safm", [128, 8])
    wmst_t = [S(f"wmst_t{i}", [128, 512], F32R) for i in range(4)]
    vb16 = S("vb16", [128, 8], BF16)
    vf32 = S("vf32", [128, 8])
    wrts = S("wrts", [128, 64])
    sqd = S("sqd", [128, H], BF16)
    statb = S("statb", [128, 28])
    st2 = S("st2", [128, 32])
    nsc = S("nsc", [128, 44])
    negmax = S("negmax", [128, 4])
    sumexp = S("sumexp", [128, 4])
    rsum = S("rsum", [128, 4])
    expw = S("expw", [128, 64])
    wtm = S("wtm", [128, 64], BF16)
    pcA = [S(f"pcA{i}", [64, 96], BF16) for i in range(2)]
    pcB = [S(f"pcB{i}", [128, 96], BF16) for i in range(2)]
    pcX = [S(f"pcX{i}", [79, 96], BF16) for i in range(2)]
    pcY = [S(f"pcY{i}", [15, 96], BF16) for i in range(2)]

    xg_s = S("xg_s", [9, H])
    hpadgT = hpT[0:16, 0:HC * 128]
    vtmp = vtmp2[0:8, 0:128]
    gs1 = S("gs1", [9, 8])
    hpadg = S("hpadg", [128, 144], F32R)
    idnr = S("idnr", [128, 128], F32R)
    projg = S("projg", [128, 128])
    qgev = S("qgev", [128, 8])
    argg = S("argg", [128, 128])
    thg = S("thg", [128, 128])
    scg = S("scg", [16, 1])
    scrow = S("scrow", [1, 16])
    wgrow = S("wgrow", [1, 16])
    wgT = S("wgT", [16, 1])
    mrow = mrow2[0:1, 0:H]
    attng = mrow; x2g = mrow; h2g = mrow
    gls = S("gls", [1, 48])
    h2gfm = S("h2gfm", [128, 8])
    gatesrow = S("gatesrow", [1, 8])
    gbc = S("gbc", [128, 8])
    sv = S("sv", [128, 4])
    idn = S("idn", [128, 128])
    idnb = S("idnb", [128, 128], BF16)
    onesc = S("onesc", [1, 128])
    zrow = S("zrow", [128, 128], BF16)

    wstA = p_s[:, 0:1024]
    wstB = p_s[:, 1024:2048]

    modbuf = None

    pmm = [nc.alloc_psum_tensor(f"pmm{i}", [128, 512], F32) for i in range(4)]
    psc = nc.alloc_psum_tensor("psc", [128, 64], F32)
    pms = [nc.alloc_psum_tensor(f"pms{i}", [128, 512], F32) for i in range(2)]
    pmini = nc.alloc_psum_tensor("pmini", [128, 512], F32)

    ENGS = ("sync", "gpsimd", "tensor", "vector", "scalar")
    prog = {e: [] for e in ENGS}
    semnames = ["ld", "ldg", "xb", "pe", "ve", "ac", "st", "wv"]
    cnt = {s: 0 for s in semnames}
    SEM = {}

    import inspect
    tags = {e: [] for e in ENGS}

    class Eng:
        def __init__(self, name):
            self.name = name
        def emit(self, fn, sem=None, n=1):
            tags[self.name].append(inspect.stack()[1].lineno)
            if sem is None:
                prog[self.name].append(fn)
                return None
            cnt[sem] += n
            v = cnt[sem]
            prog[self.name].append(lambda e, f=fn, s=sem, k=n: f(e).then_inc(SEM[s], k))
            return v
        def wait(self, sem, val):
            tags[self.name].append(-1)
            assert val is not None and val <= cnt[sem], \
                f"[{self.name}] wait {sem}>={val} emitted={cnt[sem]}"
            prog[self.name].append(lambda e, s=sem, v=val: e.wait_ge(SEM[s], v))

    sy, gp, te, ve, ac = (Eng(n) for n in ENGS)

    def vf(fn):
        v = ve.emit(fn, "ve")
        ve.wait("ve", v)
        return v

    def dma(E_, out, in_, sem="ld", **kw):
        return E_.emit(lambda e, o=out, i=in_, kww=dict(kw): e.dma_start(out=o, in_=i, **kww),
                       sem, 16)

    def dmat(E_, out, in_, sem="xb"):
        return E_.emit(lambda e, o=out, i=in_: e.dma_start(out=o, in_=i, transpose=True),
                       sem, 16)

    RAP = bass.AP

    def newton_rsqrt(dst, src, ncols, emit_sem=None, rows=128):
        s0 = nsc[0:rows, 0:ncols]
        s1 = nsc[0:rows, 22:22 + ncols]
        vf(lambda e: e.tensor_single_scalar(out=s1, in_=src, scalar=1.0, op=ALU.add))
        vf(lambda e: e.reciprocal(out=s0, in_=s1))
        for _ in range(6):
            vf(lambda e: e.tensor_tensor(out=s1, in0=s0, in1=s0, op=ALU.mult))
            vf(lambda e: e.tensor_tensor(out=s1, in0=s1, in1=src, op=ALU.mult))
            vf(lambda e: e.tensor_scalar(out=s1, in0=s1, scalar1=-0.5, scalar2=1.5,
                                         op0=ALU.mult, op1=ALU.add))
            vf(lambda e: e.tensor_tensor(out=s0, in0=s0, in1=s1, op=ALU.mult))
        v = vf(lambda e: e.tensor_copy(out=dst, in_=s0))
        if emit_sem is None:
            return v
        cnt[emit_sem] += 1
        vv = cnt[emit_sem]
        prog["vector"].append(lambda e, s=emit_sem: e.tensor_copy(
            out=nsc[0:1, 40:41], in_=nsc[0:1, 40:41]).then_inc(SEM[s], 1))
        return vv

    bank_free = {}
    def pe_bank(nm):
        if nm in bank_free:
            te.wait(bank_free[nm][0], bank_free[nm][1])
    def set_bank_free(nm, sem, val):
        bank_free[nm] = (sem, val)

    dma(sy, idn[:], id_d)
    dma(sy, onesc[:], ones_d)
    dma(sy, mcol[:], RAP(mv_d.tensor, 0, [[1, 128], [128, 17]]),
        allow_slow_non_contiguous=True)
    dma(sy, xg_s[:], xg_d)
    dma(sy, rowa[:], c_d)
    dma(sy, vtmp[:], vr_d)
    ld_small = dma(sy, RAP(wrts, 0, [[64, 128], [8, 8], [1, 8]]),
                   RAP(wrt_d.tensor, 0, [[8, 128], [1024, 8], [1, 8]]),
                   allow_slow_non_contiguous=True)

    z_ve = ve.emit(lambda e: e.memset(zrow[:], 0.0), "ve")
    gp.wait("ve", z_ve)
    dma(gp, idnb[:], id_d, sem="ldg")
    dma(gp, idnr[:], id_d, sem="ldg")
    zw_last = None
    for zi in range(0, WSCR, 128):
        rows = min(128, WSCR - zi)
        zw_last = dma(gp, wsc_d[zi:zi + rows, :], zrow[0:rows, :], sem="ldg")

    ac.wait("ld", ld_small)
    sil_a = ac.emit(lambda e: e.activation(out=rowa[:], in_=rowa[:], func=AF.Silu), "ac")
    te.wait("ac", sil_a)
    t_sil = None
    for c in range(8):
        t_sil = te.emit(lambda e, c=c: e.transpose(
            pmini[0:128, c:c + 1], rowa[0:1, 128 * c:128 * (c + 1)], idn[0:1, 0:1]),
            "pe" if c == 7 else None)
    ve.wait("pe", t_sil)
    silfm = silfm_t[:]
    ve.emit(lambda e: e.tensor_copy(out=vf32[:], in_=pmini[0:128, 0:8]))
    v_silr = ve.emit(lambda e: e.tensor_copy(out=silfm, in_=pmini[0:128, 0:8]), "ve")
    set_bank_free("pmini", "ve", v_silr)

    te.wait("ve", v_silr)
    wm_pe = {}
    st_mod = {}
    wm_banks = [pms[0], pms[1], pmm[0], pmm[1]]
    wm_bname = ["pms0", "pms1", "pmm0", "pmm1"]
    last_view_mm = {}
    last_cp = {}
    for g in range(12):
        for c in range(8):
            vk0 = (0, c % 4)
            if vk0 in last_view_mm:
                gp.wait("pe", last_view_mm[vk0])
            ld0 = dma(gp, wmst_t[2 * (c % 4 // 2)][:, 256 * (c % 2):256 * (c % 2) + 256],
                      wmod_d[128 * c:128 * (c + 1), 512 * g:512 * g + 256], sem="ldg")
            stg1 = xtok[4][:, 256 * (c % 4):256 * (c % 4) + 256]
            if (c % 4) in last_cp:
                sy.wait("ve", last_cp[c % 4])
            ld1 = dma(sy, stg1,
                      wmod_d[128 * c:128 * (c + 1), 512 * g + 256:512 * g + 512], sem="ld")
            ve.wait("ld", ld1)
            vk1 = (1, c % 4)
            if vk1 in last_view_mm:
                ve.wait("pe", last_view_mm[vk1])
            vcp = ve.emit(lambda e, c=c, stg1=stg1: e.tensor_scalar(
                out=wmst_t[1 + 2 * (c % 4 // 2)][:, 256 * (c % 2):256 * (c % 2) + 256], in0=stg1,
                scalar1=1.0, scalar2=None, op0=ALU.mult), "ve")
            last_cp[c % 4] = vcp
            for h in range(2):
                bi = (g % 2) * 2 + h
                if c == 0:
                    pe_bank(wm_bname[bi])
                if h == 0:
                    te.wait("ldg", ld0)
                else:
                    te.wait("ve", vcp)
                wm_pe[(g, c, h)] = te.emit(lambda e, g=g, c=c, h=h, bi=bi: e.matmul(
                    wm_banks[bi][0:1, 0:256], silfm[:, c:c + 1],
                    wmst_t[h + 2 * (c % 4 // 2)][:, 256 * (c % 2):256 * (c % 2) + 256],
                    start=(c == 0), stop=(c == 7)), "pe")
                last_view_mm[(h, c % 4)] = wm_pe[(g, c, h)]
        ve.wait("pe", wm_pe[(g, 7, 1)])
        if g > 0:
            ve.wait("ldg", st_mod[g - 1])
        bA, bB = (g % 2) * 2, (g % 2) * 2 + 1
        ve.emit(lambda e, bA=bA: e.tensor_copy(
            out=midb[0:1, 0:256], in_=wm_banks[bA][0:1, 0:256]))
        v_m = ve.emit(lambda e, bB=bB: e.tensor_copy(
            out=midb[0:1, 256:512], in_=wm_banks[bB][0:1, 0:256]), "ve")
        set_bank_free(wm_bname[bA], "ve", v_m)
        set_bank_free(wm_bname[bB], "ve", v_m)
        gp.wait("ve", v_m)
        st_mod[g] = dma(gp, modsc_d[g:g + 1, :], midb[0:1, 0:512], sem="ldg")
    wm_last_pe = wm_pe[(11, 7, 1)]

    sy.wait("ldg", st_mod[11])
    ld_modfm = dma(sy, RAP(modfm, 0, [[48, 128], [8, 6], [1, 8]]),
                   RAP(modsc_d.tensor, 0, [[1, 128], [1024, 6], [128, 8]]),
                   allow_slow_non_contiguous=True)

    ve.wait("ld", ld_modfm)
    vf(lambda e: e.tensor_single_scalar(out=opca[:], in_=modfm[:, 8:16], scalar=1.0, op=ALU.add))
    vf(lambda e: e.tensor_single_scalar(out=opcb[:], in_=modfm[:, 32:40], scalar=1.0, op=ALU.add))

    def clamp_fm(dstt, srcsl):
        vf(lambda e: e.tensor_scalar(out=nsc[:, 0:8], in0=srcsl, scalar1=0.0, scalar2=2.0,
                                     op0=ALU.is_ge, op1=ALU.mult))
        vf(lambda e: e.tensor_single_scalar(out=nsc[:, 0:8], in_=nsc[:, 0:8],
                                            scalar=1.0, op=ALU.subtract))
        vf(lambda e: e.tensor_single_scalar(out=nsc[:, 8:16], in_=srcsl, scalar=-1.0,
                                            op=ALU.mult))
        vf(lambda e: e.tensor_tensor(out=nsc[:, 8:16], in0=nsc[:, 8:16], in1=srcsl,
                                     op=ALU.max))
        vf(lambda e: e.tensor_single_scalar(out=nsc[:, 8:16], in_=nsc[:, 8:16],
                                            scalar=0.25, op=ALU.max))
        vf(lambda e: e.tensor_tensor(out=dstt, in0=nsc[:, 0:8], in1=nsc[:, 8:16],
                                     op=ALU.mult))
    clamp_fm(gapr[:], modfm[:, 16:24])
    clamp_fm(gbpr[:], modfm[:, 40:48])
    vf(lambda e: e.reciprocal(out=nsc[:, 0:8], in_=opca[:]))
    vf(lambda e: e.tensor_tensor(out=fmpack[:, 0:8], in0=modfm[:, 0:8], in1=nsc[:, 0:8],
                                 op=ALU.mult))
    vf(lambda e: e.tensor_single_scalar(out=fmpack[:, 0:8], in_=fmpack[:, 0:8],
                                        scalar=-1.0, op=ALU.mult))
    vf(lambda e: e.tensor_tensor(out=fmpack[:, 8:16], in0=gapr[:], in1=opca[:], op=ALU.mult))
    vf(lambda e: e.tensor_tensor(out=fmpack[:, 16:24], in0=gapr[:], in1=modfm[:, 0:8],
                                 op=ALU.mult))
    vf(lambda e: e.tensor_copy(out=fmpack[:, 24:32], in_=gapr[:]))
    vf(lambda e: e.tensor_copy(out=fmpack[:, 32:40], in_=opcb[:]))
    v_pack = ve.emit(lambda e: e.tensor_copy(out=fmpack[:, 40:48], in_=modfm[:, 24:32]), "ve")

    te.wait("ve", v_pack)
    pe_bank("pmini")
    t_pack = te.emit(lambda e: e.transpose(pmini[0:48, 0:128], fmpack[:], idn[:]), "pe")
    ve.wait("pe", t_pack)
    v_packs = ve.emit(lambda e: e.tensor_copy(out=packst[0:48, 0:128], in_=pmini[0:48, 0:128]), "ve")
    set_bank_free("pmini", "ve", v_packs)
    gp.wait("ve", v_packs)
    st_pack = dma(gp, RAP(rowsc_d.tensor, 0, [[128, 48], [1, 128]]), packst[0:48, 0:128], sem="ldg")
    sy.wait("ldg", st_pack)

    def bcast_row(dst_sb, row_idx):
        ldr = dma(sy, rowa[:], rowsc_d[row_idx:row_idx + 1, :])
        last = None
        for hh in range(2):
            bank = f"pms{hh}"
            pe_bank(bank)
            te.wait("ld", ldr)
            mmv = te.emit(lambda e, hh=hh: e.matmul(
                pms[hh][:, :], onesc[:], rowa[0:1, 512 * hh:512 * (hh + 1)],
                start=True, stop=True), "pe")
            ve.wait("pe", mmv)
            last = ve.emit(lambda e, hh=hh, d=dst_sb: e.tensor_copy(
                out=d[:, 512 * hh:512 * (hh + 1)], in_=pms[hh][:, :]), "ve")
            set_bank_free(bank, "ve", last)
        sy.wait("pe", cnt["pe"])
        return last
    bcast_row(padbc, 0)
    bcast_row(gcbc, 1)
    v_bcast = bcast_row(gsbc, 2)

    pe_bank("pmini")
    t_v = te.emit(lambda e: e.transpose(pmini[0:128, 0:8], vtmp, idn[0:8, 0:8]), "pe")
    ve.wait("pe", t_v)
    ve.emit(lambda e: e.tensor_copy(out=vb16[:], in_=pmini[0:128, 0:8]))
    v_v = ve.emit(lambda e: e.tensor_copy(out=vf32[:], in_=pmini[0:128, 0:8]), "ve")
    set_bank_free("pmini", "ve", v_v)

    if debug:
        gp.wait("ve", v_v)
        dma(gp, dbg["d_mod"], modfm[:], sem="st")
        dma(gp, dbg["d_fmp"], fmpack[:], sem="st")

    vf(lambda e: e.reduce_sum(out=gs1[:, 0:1], in_=xg_s[:], axis=AXX))
    ac.wait("ld", ld_small)
    a_msq = ac.emit(lambda e: e.activation(out=hpadgT[0:9, :], in_=xg_s[:], func=AF.Square,
                                           accum_out=gs1[:, 1:2]), "ac")
    ve.wait("ac", a_msq)
    vf(lambda e: e.tensor_single_scalar(out=gs1[:, 2:3], in_=gs1[:, 0:1],
                                        scalar=1.0 / H, op=ALU.mult))
    vf(lambda e: e.tensor_tensor(out=gs1[:, 3:4], in0=gs1[:, 2:3], in1=gs1[:, 2:3],
                                 op=ALU.mult))
    vf(lambda e: e.tensor_single_scalar(out=gs1[:, 4:5], in_=gs1[:, 1:2],
                                        scalar=1.0 / H, op=ALU.mult))
    vf(lambda e: e.tensor_tensor(out=gs1[:, 4:5], in0=gs1[:, 4:5], in1=gs1[:, 3:4],
                                 op=ALU.subtract))
    vf(lambda e: e.tensor_single_scalar(out=gs1[:, 4:5], in_=gs1[:, 4:5],
                                        scalar=EPS, op=ALU.add))
    newton_rsqrt(gs1[:, 5:6], gs1[:, 4:5], 1, rows=9)
    xngv = hpT[0:9, 0:H]
    v_mln = ve.emit(lambda e: e.tensor_scalar(out=xngv, in0=xg_s[:],
                                              scalar1=gs1[:, 2:3], scalar2=gs1[:, 5:6],
                                              op0=ALU.subtract, op1=ALU.mult), "ve")
    te.wait("ve", v_mln)
    pe_bank("pmini")
    t_xng = None
    for c in range(8):
        t_xng = te.emit(lambda e, c=c: e.transpose(
            pmini[0:128, 9 * c:9 * (c + 1)], hpT[0:9, 128 * c:128 * (c + 1)],
            idn[0:9, 0:9]),
            "pe" if c == 7 else None)
    ve.wait("pe", t_xng)
    vf(lambda e: e.memset(hpadg[:].bitcast(F32), 0.0))
    for c in range(8):
        ve.emit(lambda e, c=c: e.tensor_scalar(
            out=hpadg[:, 18 * c + 7:18 * c + 16], in0=pmini[0:128, 9 * c:9 * (c + 1)],
            scalar1=opca[:, c:c + 1], scalar2=modfm[:, c:c + 1],
            op0=ALU.mult, op1=ALU.add))
    v_hpg = ve.emit(lambda e: e.tensor_copy(
        out=RAP(hpadg, 16, [[144, 128], [18, 8]]), in_=modfm[:, 0:8]), "ve")
    set_bank_free("pmini", "ve", v_hpg)

    pe_bank("pmini")
    pe_bank("pms0")
    te.wait("ve", v_hpg)
    gp.wait("pe", wm_last_pe)
    wv_marks = {}
    step = 0
    first_flag = {0: True, 1: True}
    for wi in range(2):
        wd = w2_d if wi == 0 else w1_d
        wb = w2b if wi == 0 else w1b
        for c in range(8):
            for half in range(2):
                buf = step % 4
                if step >= 4:
                    gp.wait("wv", wv_marks[step - 4])
                ldw = dma(gp, wmst_t[buf][:],
                          wd[128 * c:128 * (c + 1), 512 * half:512 * (half + 1)], sem="ldg")
                ve.wait("ldg", ldw)
                v_cast = ve.emit(lambda e, wb=wb, c=c, half=half, buf=buf: e.tensor_scalar(
                    out=wb[:, 1024 * c + 512 * half:1024 * c + 512 * (half + 1)],
                    in0=wmst_t[buf][:],
                    scalar1=opca[:, c:c + 1], scalar2=None, op0=ALU.mult), "ve")
                te.wait("ve", v_cast)
                for m in range(4):
                    mg = 4 * half + m
                    st_f = first_flag[wi]
                    first_flag[wi] = False
                    sp_f = (c == 7 and mg == 7)
                    if wi == 0:
                        te.emit(lambda e, c=c, m=m, mg=mg, buf=buf, st_f=st_f, sp_f=sp_f:
                                e.matmul(
                            pmini[0:128, 18 * mg:18 * mg + 18],
                            wmst_t[buf][:, 128 * m:128 * (m + 1)],
                            hpadg[:, 18 * c:18 * c + 18],
                            start=st_f, stop=sp_f, skip_group_check=True))
                    else:
                        te.emit(lambda e, c=c, m=m, mg=mg, buf=buf, st_f=st_f, sp_f=sp_f:
                                e.matmul(
                            RAP(pms[0], 140 + 16 * mg, [[512, 128], [1, 2]]),
                            wmst_t[buf][:, 128 * m:128 * (m + 1)],
                            RAP(hpadg, 18 * c + 7, [[144, 128], [9, 2]]),
                            start=st_f, stop=sp_f, skip_group_check=True))
                wv_marks[step] = te.emit(lambda e: e.matmul(
                    pmini[0:1, 500:501], idn[0:1, 0:1], idn[0:1, 0:1], start=False,
                    stop=False, skip_group_check=True), "wv")
                step += 1
    qg_done = te.emit(lambda e: e.matmul(
        pmini[0:1, 502:503], idn[0:1, 0:1], idn[0:1, 0:1], start=False, stop=False, skip_group_check=True), "pe")

    ve.wait("pe", qg_done)
    vf(lambda e: e.tensor_copy(
        out=RAP(projg, 0, [[128, 128], [16, 8], [1, 16]]),
        in_=RAP(pmini, 0, [[512, 128], [18, 8], [1, 16]])))
    vf(lambda e: e.tensor_copy(
        out=c2fm[:], in_=RAP(pmini, 16, [[512, 128], [18, 8], [1, 1]])))
    vf(lambda e: e.tensor_copy(
        out=c1fm[:], in_=RAP(pms[0], 141, [[512, 128], [16, 8], [1, 1]])))
    v_pj = vf(lambda e: e.tensor_copy(
        out=qgev[:], in_=RAP(pms[0], 140, [[512, 128], [16, 8], [1, 1]])))
    set_bank_free("pmini", "ve", v_pj)
    set_bank_free("pms0", "ve", v_pj)

    v_argg = None
    for c in range(8):
        v_argg = vf(lambda e, c=c: e.tensor_scalar(
            out=argg[:, 16 * c:16 * (c + 1)], in0=projg[:, 16 * c:16 * (c + 1)],
            scalar1=qgev[:, c:c + 1], scalar2=None, op0=ALU.add))
    ac.wait("ve", v_argg)
    a_thg = ac.emit(lambda e: e.activation(out=thg[:], in_=argg[:], func=AF.Tanh), "ac")
    te.wait("ac", a_thg)
    pe_bank("pmini")
    sc_mmv = None
    for c in range(8):
        sc_mmv = te.emit(lambda e, c=c: e.matmul(
            pmini[0:16, 0:1], thg[:, 16 * c:16 * (c + 1)], vf32[:, c:c + 1],
            start=(c == 0), stop=(c == 7)), "pe" if c == 7 else None)
    ve.wait("pe", sc_mmv)
    v_scg = ve.emit(lambda e: e.tensor_copy(out=scg[:], in_=pmini[0:16, 0:1]), "ve")
    set_bank_free("pmini", "ve", v_scg)
    te.wait("ve", v_scg)
    pe_bank("pmini")
    t_scg = te.emit(lambda e: e.transpose(pmini[0:1, 0:16], scg[:], idn[0:16, 0:16]), "pe")
    ve.wait("pe", t_scg)
    v_scr = vf(lambda e: e.tensor_copy(out=scrow[:], in_=pmini[0:1, 0:16]))
    set_bank_free("pmini", "ve", v_scr)
    vf(lambda e: e.tensor_reduce(out=gls[0:1, 0:1], in_=scrow[:], axis=AXX, op=ALU.max))
    vf(lambda e: e.tensor_single_scalar(out=gls[0:1, 0:1], in_=gls[0:1, 0:1],
                                        scalar=-1.0, op=ALU.mult))
    v_nm0 = ve.emit(lambda e: e.tensor_copy(out=gls[0:1, 1:2], in_=gls[0:1, 0:1]), "ve")
    ac.wait("ve", v_nm0)
    a_eg = ac.emit(lambda e: e.activation(out=wgrow[:], in_=scrow[:], func=AF.Exp,
                                          bias=gls[0:1, 0:1], accum_out=gls[0:1, 2:3]), "ac")
    ve.wait("ac", a_eg)
    vf(lambda e: e.reciprocal(out=gls[0:1, 3:4], in_=gls[0:1, 2:3]))
    v_wg = ve.emit(lambda e: e.tensor_scalar(out=wgrow[:], in0=wgrow[:],
                                             scalar1=gls[0:1, 3:4], scalar2=None,
                                             op0=ALU.mult), "ve")
    te.wait("ve", v_wg)
    pe_bank("pmini")
    for c in range(8):
        t_h = te.emit(lambda e, c=c: e.transpose(
            pmini[0:16, 0:128].bitcast(F32R), hpadg[:, 18 * c:18 * c + 16],
            idnr[:]), "pe")
        ve.wait("pe", t_h)
        v_h = ve.emit(lambda e, c=c: e.tensor_copy(
            out=hpadgT[0:16, 128 * c:128 * (c + 1)], in_=pmini[0:16, 0:128]), "ve")
        te.wait("ve", v_h)
    t_wgT = te.emit(lambda e: e.transpose(pmini[0:16, 130:131], wgrow[:], idn[0:1, 0:1]), "pe")
    ve.wait("pe", t_wgT)
    v_wgT = ve.emit(lambda e: e.tensor_copy(out=wgT[:], in_=pmini[0:16, 130:131]), "ve")
    set_bank_free("pmini", "ve", v_wgT)
    te.wait("ve", v_wgT)
    pe_bank("pms0"); pe_bank("pms1")
    mm_at = {}
    for hh in range(2):
        mm_at[hh] = te.emit(lambda e, hh=hh: e.matmul(
            pms[hh][0:1, :], wgT[:], hpadgT[0:16, 512 * hh:512 * (hh + 1)],
            start=True, stop=True), "pe")
    ve.wait("pe", mm_at[1])
    v_atg = None
    for hh in range(2):
        v_atg = ve.emit(lambda e, hh=hh: e.tensor_copy(
            out=attng[0:1, 512 * hh:512 * (hh + 1)], in_=pms[hh][0:1, :]),
            "ve" if hh == 1 else None)
    set_bank_free("pms0", "ve", v_atg); set_bank_free("pms1", "ve", v_atg)
    ld_ga = dma(sy, rowb[:], rowsc_d[3:4, :])
    ve.wait("ld", ld_ga)
    ve.wait("ve", v_atg)
    vf(lambda e: e.tensor_tensor(out=x2g[:], in0=attng[:], in1=rowb[:], op=ALU.mult))
    vf(lambda e: e.tensor_tensor(out=x2g[:], in0=x2g[:], in1=xg_s[0:1, :], op=ALU.add))
    v_x2g = vf(lambda e: e.reduce_sum(out=gls[0:1, 8:9], in_=x2g[:], axis=AXX))
    ac.wait("ve", v_x2g)
    a_msq2 = ac.emit(lambda e: e.activation(out=hpadgT[0:1, :], in_=x2g[:], func=AF.Square,
                                            accum_out=gls[0:1, 9:10]), "ac")
    ve.wait("ac", a_msq2)
    vf(lambda e: e.tensor_single_scalar(out=gls[0:1, 10:11], in_=gls[0:1, 8:9],
                                        scalar=1.0 / H, op=ALU.mult))
    vf(lambda e: e.tensor_tensor(out=gls[0:1, 11:12], in0=gls[0:1, 10:11],
                                 in1=gls[0:1, 10:11], op=ALU.mult))
    vf(lambda e: e.tensor_single_scalar(out=gls[0:1, 12:13], in_=gls[0:1, 9:10],
                                        scalar=1.0 / H, op=ALU.mult))
    vf(lambda e: e.tensor_tensor(out=gls[0:1, 12:13], in0=gls[0:1, 12:13],
                                 in1=gls[0:1, 11:12], op=ALU.subtract))
    vf(lambda e: e.tensor_single_scalar(out=gls[0:1, 12:13], in_=gls[0:1, 12:13],
                                        scalar=EPS, op=ALU.add))
    newton_rsqrt(gls[0:1, 13:14], gls[0:1, 12:13], 1, rows=1)
    vf(lambda e: e.tensor_scalar(out=h2g[:], in0=x2g[:], scalar1=gls[0:1, 10:11],
                                 scalar2=gls[0:1, 13:14], op0=ALU.subtract, op1=ALU.mult))
    sy.wait("ve", v_x2g)
    ld_ocb = dma(sy, rowb[:], rowsc_d[4:5, :])
    ve.wait("ld", ld_ocb)
    v_m1 = vf(lambda e: e.tensor_tensor(out=h2g[:], in0=h2g[:], in1=rowb[:],
                                        op=ALU.mult))
    sy.wait("ve", v_m1)
    ld_sb = dma(sy, rowb[:], rowsc_d[5:6, :])
    ve.wait("ld", ld_sb)
    v_h2g = vf(lambda e: e.tensor_tensor(out=h2g[:], in0=h2g[:], in1=rowb[:],
                                         op=ALU.add))
    te.wait("ve", v_h2g)
    pe_bank("pmini")
    t_h2g = None
    for c in range(8):
        t_h2g = te.emit(lambda e, c=c: e.transpose(
            pmini[0:128, 140 + c:141 + c], h2g[0:1, 128 * c:128 * (c + 1)],
            idn[0:1, 0:1]),
            "pe" if c == 7 else None)
    ve.wait("pe", t_h2g)
    v_h2gfm = vf(lambda e: e.tensor_copy(out=h2gfm[:], in_=pmini[0:128, 140:148]))
    te.wait("ve", v_h2gfm)
    lg_mm = None
    for c in range(8):
        lg_mm = te.emit(lambda e, c=c: e.matmul(
            pmini[0:8, 0:1], wrts[:, 8 * c:8 * (c + 1)], h2gfm[:, c:c + 1],
            start=(c == 0), stop=(c == 7)), "pe" if c == 7 else None)
    ve.wait("pe", lg_mm)
    v_lg = ve.emit(lambda e: e.tensor_copy(out=scg[0:8, 0:1], in_=pmini[0:8, 0:1]), "ve")
    te.wait("ve", v_lg)
    t_lg = te.emit(lambda e: e.transpose(pmini[0:1, 32:40], scg[0:8, 0:1], idn[0:8, 0:8]), "pe")
    ve.wait("pe", t_lg)
    lgrow = gls[0:1, 40:48]
    v_lgr = vf(lambda e: e.tensor_copy(out=lgrow, in_=pmini[0:1, 32:40]))
    set_bank_free("pmini", "ve", v_lgr)
    vf(lambda e: e.tensor_reduce(out=gls[0:1, 16:17], in_=lgrow, axis=AXX, op=ALU.max))
    vf(lambda e: e.tensor_scalar(out=gls[0:1, 24:32], in0=lgrow,
                                      scalar1=gls[0:1, 16:17], scalar2=-1e9,
                                      op0=ALU.is_ge, op1=ALU.mult))
    vf(lambda e: e.tensor_tensor(out=gls[0:1, 24:32], in0=gls[0:1, 24:32], in1=lgrow,
                                      op=ALU.add))
    vf(lambda e: e.tensor_reduce(out=gls[0:1, 17:18], in_=gls[0:1, 24:32], axis=AXX,
                                 op=ALU.max))
    vf(lambda e: e.tensor_scalar(out=gls[0:1, 24:32], in0=lgrow,
                                      scalar1=gls[0:1, 17:18], scalar2=1e4,
                                      op0=ALU.is_ge, op1=ALU.mult))
    vf(lambda e: e.tensor_tensor(out=gls[0:1, 24:32], in0=gls[0:1, 24:32], in1=lgrow,
                                 op=ALU.add))
    vf(lambda e: e.tensor_single_scalar(out=gls[0:1, 24:32], in_=gls[0:1, 24:32],
                                             scalar=1e4, op=ALU.subtract))
    vf(lambda e: e.tensor_reduce(out=gls[0:1, 18:19], in_=gls[0:1, 24:32],
                                       axis=AXX, op=ALU.max))
    v_ml = ve.emit(lambda e: e.tensor_single_scalar(out=gls[0:1, 18:19], in_=gls[0:1, 18:19],
                                                    scalar=-1.0, op=ALU.mult), "ve")
    ac.wait("ve", v_ml)
    a_ge = ac.emit(lambda e: e.activation(out=gatesrow[:], in_=gls[0:1, 24:32], func=AF.Exp,
                                          bias=gls[0:1, 18:19], accum_out=gls[0:1, 19:20]), "ac")
    mini_last_ac = a_ge
    ve.wait("ac", a_ge)
    vf(lambda e: e.reciprocal(out=gls[0:1, 20:21], in_=gls[0:1, 19:20]))
    v_g = ve.emit(lambda e: e.tensor_scalar(out=gatesrow[:], in0=gatesrow[:],
                                            scalar1=gls[0:1, 20:21], scalar2=None,
                                            op0=ALU.mult), "ve")
    te.wait("ve", v_g)
    pe_bank("pms0")
    mm_gb = te.emit(lambda e: e.matmul(pms[0][0:128, 0:8], onesc[:], gatesrow[:],
                                       start=True, stop=True), "pe")
    mini_last_pe = mm_gb
    ve.wait("pe", mm_gb)
    v_gbc = vf(lambda e: e.tensor_copy(out=gbc[:], in_=pms[0][0:128, 0:8]))
    set_bank_free("pms0", "ve", v_gbc)
    for j in range(4):
        vf(lambda e, j=j: e.tensor_copy(out=sv[0:64, j:j + 1], in_=gbc[0:64, 2 * j:2 * j + 1]))
        if j < 3:
            vf(lambda e, j=j: e.tensor_copy(out=sv[64:128, j:j + 1],
                                            in_=gbc[64:128, 2 * j + 1:2 * j + 2]))
    v_sv = ve.emit(lambda e: e.tensor_copy(out=sv[64:128, 3:4], in_=gbc[64:128, 7:8]), "ve")
    if debug:
        gp.wait("ve", v_sv)
        dma(gp, dbg["d_gates"], gatesrow[:], sem="st")
        dma(gp, dbg["d_gdbg"], gls[:], sem="st")
        dma(gp, dbg["d_gbc"], gbc[:], sem="st")

    at_mark = {}
    for e_ in range(8):
        if e_ >= 2:
            gp.wait("pe", at_mark[e_ - 2])
        ld_a = dma(gp, atmp[e_ % 2][:], a_d[e_, :, :], sem="ldg")
        te.wait("ldg", ld_a)
        g, half = e_ // 2, e_ % 2
        for c in range(8):
            bnm = f"pms{c % 2}"
            pe_bank(bnm)
            t_a = te.emit(lambda e, e_=e_, c=c: e.transpose(
                pms[c % 2][0:128, 0:32].bitcast(BF16), atmp[e_ % 2][:, 128 * c:128 * (c + 1)],
                idnb[0:64, 0:64]), "pe")
            ve.wait("pe", t_a)
            v_a = ve.emit(lambda e, g=g, half=half, c=c: e.tensor_copy(
                out=alh[:, (c * 4 + g) * 128 + 64 * half:(c * 4 + g) * 128 + 64 * (half + 1)],
                in_=pms[c % 2][0:128, 0:32].bitcast(BF16)), "ve")
            set_bank_free(bnm, "ve", v_a)
        at_mark[e_] = te.emit(lambda e: e.matmul(
            pmini[0:1, 504:505], idn[0:1, 0:1], idn[0:1, 0:1], start=False, stop=False, skip_group_check=True), "pe")
    bw_mark = {}
    bw_i = 0
    for e_ in range(8):
        g, half = e_ // 2, e_ % 2
        for c in range(8):
            if bw_i >= 2:
                gp.wait("pe", bw_mark[bw_i - 2])
            buf = bw_i % 2
            ld_b = dma(gp, bwtmp[buf][:], bw_d[e_, 128 * c:128 * (c + 1), :], sem="ldg")
            ve.wait("ldg", ld_b)
            v_b = ve.emit(lambda e, buf=buf, c=c: e.tensor_scalar(
                out=bwtmp2[buf][:], in0=bwtmp[buf][:], scalar1=gbpr[:, c:c + 1],
                scalar2=None, op0=ALU.mult), "ve")
            te.wait("ve", v_b)
            bnm = f"pmm{bw_i % 2}"
            pe_bank(bnm)
            t_b = te.emit(lambda e, buf=buf, bw_i=bw_i: e.transpose(
                pmm[bw_i % 2][0:64, 0:64].bitcast(BF16), bwtmp2[buf][:], idnb[:]), "pe")
            bw_mark[bw_i] = t_b
            ve.wait("pe", t_b)
            v_b2 = ve.emit(lambda e, g=g, half=half, c=c, bw_i=bw_i: e.tensor_copy(
                out=bwt[64 * half:64 * (half + 1), 1024 * g + 128 * c:1024 * g + 128 * (c + 1)],
                in_=pmm[bw_i % 2][0:64, 0:64].bitcast(BF16)), "ve")
            set_bank_free(bnm, "ve", v_b2)
            bw_i += 1
    v_bwt = v_b2
    if debug:
        gp.wait("ve", v_bwt)
        dma(gp, dbg["d_sv"], sv[:], sem="st")
        dma(gp, dbg["d_alh"], alh[:], sem="st")
        dma(gp, dbg["d_bwt"], bwt[:], sem="st")

    xu_ld = {}
    xu_free = {}
    xn_done = {}
    xn_lastT = {}
    gxn_done = {}
    hp_done = {}
    pq_mm_last = {}
    pq_evac = {}
    wtm_store = {}
    comb_mm = {}
    comb_done = {}
    xtok_ld = {}
    ffn_done = {}
    ffn_tr = {}
    x2_dump = {}
    pending_gxn = []
    sc_bank_free = None

    tile_sc = {}

    def ln1_load_stats(jj, sc_):
        slot = jj % XU_RING
        xslot = jj % XN_RING
        tile_sc[jj] = sc_
        if jj == 16:
            vz = ve.emit(lambda e, s=slot: e.memset(xu[s][:], 0.0), "ve")
            gp.wait("ve", vz)
        if jj >= XU_RING:
            gp.wait("ve", xu_free[jj - XU_RING])
        rows = 128 if jj < 16 else U - 2048
        xu_ld[jj] = dma(gp, xu[slot][0:rows, :], x_d[128 * jj:128 * jj + rows, :], sem="ldg")
        ve.wait("ldg", xu_ld[jj])
        vf(lambda e, s=slot, sc=sc_: e.reduce_sum(out=statb[:, sc:sc + 1], in_=xu[s][:],
                                                  axis=AXX))
        ac.wait("ldg", xu_ld[jj])
        if jj - XN_RING in xn_lastT:
            ac.wait("pe", xn_lastT[jj - XN_RING])
        a_sq = ac.emit(lambda e, s=slot, xs=xslot, sc=sc_: e.activation(
            out=xnr[xs][:], in_=xu[s][:], func=AF.Square,
            accum_out=statb[:, 5 + sc:6 + sc]), "ac")
        return a_sq

    def ln1_stats_batch(sclo, schi, a_sq_last):
        n = schi - sclo
        ve.wait("ac", a_sq_last)
        vf(lambda e: e.tensor_single_scalar(
            out=statb[:, 20 + sclo:20 + schi], in_=statb[:, sclo:schi],
            scalar=1.0 / H, op=ALU.mult))
        vf(lambda e: e.tensor_tensor(
            out=nsc[:, 28:28 + n], in0=statb[:, 20 + sclo:20 + schi],
            in1=statb[:, 20 + sclo:20 + schi], op=ALU.mult))
        vf(lambda e: e.tensor_single_scalar(
            out=statb[:, 10 + sclo:10 + schi], in_=statb[:, 5 + sclo:5 + schi],
            scalar=1.0 / H, op=ALU.mult))
        vf(lambda e: e.tensor_tensor(
            out=statb[:, 10 + sclo:10 + schi], in0=statb[:, 10 + sclo:10 + schi],
            in1=nsc[:, 28:28 + n], op=ALU.subtract))
        vf(lambda e: e.tensor_single_scalar(
            out=statb[:, 10 + sclo:10 + schi], in_=statb[:, 10 + sclo:10 + schi],
            scalar=EPS, op=ALU.add))
        newton_rsqrt(statb[:, 15 + sclo:15 + schi], statb[:, 10 + sclo:10 + schi], n)

    def ln1_norm(jj):
        slot = jj % XU_RING
        xslot = jj % XN_RING
        gslot = jj % GXN_RING
        sc_ = tile_sc[jj]
        v_xn = ve.emit(lambda e, s=slot, xs=xslot, sc=sc_: e.tensor_scalar(
            out=xnr[xs][:], in0=xu[s][:], scalar1=statb[:, 20 + sc:21 + sc],
            scalar2=statb[:, 15 + sc:16 + sc], op0=ALU.subtract, op1=ALU.mult), "ve")
        xu_free[jj] = v_xn
        if jj in (0, 16):
            vf(lambda e, jj=jj: e.tensor_scalar(
                out=nsc[:, 34:35], in0=mcol[:, jj:jj + 1], scalar1=-1.0, scalar2=1.0,
                op0=ALU.mult, op1=ALU.add))
            ve.emit(lambda e: e.tensor_scalar(
                out=sqd[:], in0=padbc[:], scalar1=nsc[:, 34:35], scalar2=None, op0=ALU.mult))
            ve.emit(lambda e, xs=xslot, jj=jj: e.tensor_scalar(
                out=xnr[xs][:], in0=xnr[xs][:], scalar1=mcol[:, jj:jj + 1], scalar2=None,
                op0=ALU.mult))
            v_xn = ve.emit(lambda e, xs=xslot: e.tensor_tensor(
                out=xnr[xs][:], in0=xnr[xs][:], in1=sqd[:], op=ALU.add), "ve")
        xn_done[jj] = v_xn
        if jj >= GXN_RING and jj - GXN_RING not in comb_mm:
            pending_gxn.append(jj)
        else:
            gxn_norm(jj)

    def gxn_norm(jj):
        xslot = jj % XN_RING
        gslot = jj % GXN_RING
        if jj - GXN_RING in comb_mm:
            ve.wait("pe", comb_mm[jj - GXN_RING])
        ve.emit(lambda e, xs=xslot, gs=gslot: e.tensor_tensor(
            out=gxn[gs][:], in0=xnr[xs][:], in1=gcbc[:], op=ALU.mult))
        g_v = ve.emit(lambda e, gs=gslot: e.tensor_tensor(
            out=gxn[gs][:], in0=gxn[gs][:], in1=gsbc[:], op=ALU.add), "ve")
        gxn_done[jj] = g_v
        if debug:
            gp.wait("ve", g_v)
            dma(gp, dbg["d_xn"][128 * jj:128 * jj + 128, :], xnr[xslot][:], sem="st")
            dma(gp, dbg["d_gxn"][128 * jj:128 * jj + 128, :], gxn[gslot][:], sem="st")

    def build_hpad(ut):
        last = None
        for fi, jj in enumerate(range(4 * ut, min(4 * ut + 5, NUT))):
            xslot = jj % XN_RING
            te.wait("ve", xn_done[jj])
            pe_bank("pmm0"); pe_bank("pmm1")
            t_tr = None
            for c in range(HC):
                t_tr = te.emit(lambda e, c=c, xs=xslot: e.matmul(
                    pmm[c // 4][:, 128 * (c % 4):128 * (c % 4) + 128],
                    xnr[xs][:, 128 * c:128 * (c + 1)], idnb[:],
                    start=True, stop=True), "pe" if c == 7 else None)
            xn_lastT[jj] = t_tr
            ve.wait("pe", t_tr)
            if ut > 0 and fi == 0:
                ve.wait("pe", pq_mm_last[ut - 1])
            for c in range(HC):
                last = ve.emit(lambda e, c=c, fi=fi: e.tensor_copy(
                    out=hpad[:, 640 * c + 128 * fi:640 * c + 128 * fi + 128],
                    in_=pmm[c // 4][:, 128 * (c % 4):128 * (c % 4) + 128]),
                    "ve" if c == 7 else None)
            set_bank_free("pmm0", "ve", last); set_bank_free("pmm1", "ve", last)
        hp_done[ut] = last

    def proj_q(ut):
        te.wait("ve", hp_done[ut])
        mm = None
        last_v = None
        for hc in range(HC):
            for seg, (c0, cw) in enumerate([(0, 512), (512, 16)]):
                bank = f"pmm{(2 * hc + seg) % 2}"
                pe_bank(bank)
                for kc in range(HC):
                    mm = te.emit(lambda e, hc=hc, kc=kc, c0=c0, cw=cw, b=(2 * hc + seg) % 2:
                                 e.matmul(pmm[b][:, 0:cw],
                                          w2b[:, 1024 * kc + 128 * hc:1024 * kc + 128 * (hc + 1)],
                                          hpad[:, 640 * kc + c0:640 * kc + c0 + cw],
                                          start=(kc == 0), stop=(kc == 7)),
                                 "pe" if kc == 7 else None)
                ve.wait("pe", mm)
                last_v = ve.emit(lambda e, hc=hc, c0=c0, cw=cw, b=(2 * hc + seg) % 2:
                                 e.tensor_scalar(out=p_s[:, 528 * hc + c0:528 * hc + c0 + cw],
                                                 in0=pmm[b][:, 0:cw],
                                                 scalar1=c2fm[:, hc:hc + 1], scalar2=None,
                                                 op0=ALU.add), "ve")
                set_bank_free(bank, "ve", last_v)
        for hc in range(HC):
            bank = f"pmm{hc % 2}"
            pe_bank(bank)
            for kc in range(HC):
                mm = te.emit(lambda e, hc=hc, kc=kc, b=hc % 2:
                             e.matmul(pmm[b][:, :],
                                      w1b[:, 1024 * kc + 128 * hc:1024 * kc + 128 * (hc + 1)],
                                      hpad[:, 640 * kc + 7:640 * kc + 7 + 512],
                                      start=(kc == 0), stop=(kc == 7)),
                             "pe" if kc == 7 else None)
            ve.wait("pe", mm)
            last_v = ve.emit(lambda e, hc=hc, b=hc % 2:
                             e.tensor_scalar(out=q_s[:, 512 * hc:512 * (hc + 1)],
                                             in0=pmm[b][:, :],
                                             scalar1=c1fm[:, hc:hc + 1], scalar2=None,
                                             op0=ALU.add), "ve")
            set_bank_free(bank, "ve", last_v)
        pq_mm_last[ut] = cnt["pe"]
        pq_evac[ut] = last_v
        if debug and ut == 0:
            gp.wait("ve", last_v)
            dma(gp, dbg["d_p"], p_s[:], sem="st")
            dma(gp, dbg["d_q"], q_s[:], sem="st")

    def score_unit(ut):
        nonlocal sc_bank_free
        if sc_bank_free is not None:
            te.wait("ac", sc_bank_free)
        step = 0
        th_cons = {}
        arg_cons = {}
        mm = None
        arg_dump = th_dump = None
        dump_step = 32
        for k in range(KW):
            for hf in range(4):
                par = step % 2
                if arg_dump is not None and step == dump_step + 2:
                    ve.wait("st", arg_dump)
                if step >= 2:
                    ve.wait("ac", arg_cons[par])
                if step == 0:
                    ve.wait("ve", pq_evac[ut])
                    if ut == 0:
                        ve.wait("pe", mini_last_pe)
                        ve.wait("ac", mini_last_ac)
                v_add = ve.emit(lambda e, par=par, k=k, hf=hf: e.tensor_tensor(
                    out=RAP(argb[par], 0, [[1024, 128], [512, 2], [1, 512]]),
                    in0=RAP(p_s, 528 * 2 * hf + k, [[HC * 528, 128], [528, 2], [1, 512]]),
                    in1=RAP(q_s, 512 * 2 * hf, [[HC * 512, 128], [512, 2], [1, 512]]),
                    op=ALU.add), "ve")
                if debug and ut == 0 and step == dump_step:
                    gp.wait("ve", v_add)
                    arg_dump = dma(gp, dbg["d_arg"], argb[par][:], sem="st")
                ac.wait("ve", v_add)
                if step >= 2:
                    ac.wait("pe", th_cons[par])
                if th_dump is not None and step == dump_step + 2:
                    ac.wait("st", th_dump)
                a_th = ac.emit(lambda e, par=par: e.activation(
                    out=thb[par][:], in_=argb[par][:], func=AF.Tanh), "ac")
                arg_cons[par] = a_th
                if debug and ut == 0 and step == dump_step:
                    gp.wait("ac", a_th)
                    th_dump = dma(gp, dbg["d_th"], thb[par][:], sem="st")
                te.wait("ac", a_th)
                for cl in range(2):
                    for tb in range(4):
                        first = (k == 0 and hf == 0 and cl == 0 and tb == 0)
                        last = (k == KW - 1 and hf == 3 and cl == 1 and tb == 3)
                        mm = te.emit(lambda e, par=par, k=k, hf=hf, cl=cl, tb=tb,
                                     first=first, last=last: e.matmul(
                            psc[:, 16 * tb + k:16 * tb + k + 1],
                            thb[par][:, 512 * cl + 128 * tb:512 * cl + 128 * tb + 128],
                            vb16[:, 2 * hf + cl:2 * hf + cl + 1],
                            start=first, stop=last, skip_group_check=True),
                            "pe" if (cl == 1 and tb == 3) else None)
                th_cons[par] = mm
                step += 1
        ve.wait("pe", mm)
        dsc_st = None
        if debug and ut == 0:
            v_dbg = ve.emit(lambda e: e.tensor_copy(out=expw[:], in_=psc[:]), "ve")
            gp.wait("ve", v_dbg)
            dsc_st = dma(gp, dbg["d_sc"], expw[:], sem="st")
        for tb in range(4):
            vf(lambda e, tb=tb: e.tensor_reduce(
                out=negmax[:, tb:tb + 1], in_=psc[:, 16 * tb:16 * (tb + 1)], axis=AXX,
                op=ALU.max))
        v_nm = ve.emit(lambda e: e.tensor_single_scalar(
            out=negmax[:], in_=negmax[:], scalar=-1.0, op=ALU.mult), "ve")
        ac.wait("ve", v_nm)
        if dsc_st is not None:
            ac.wait("st", dsc_st)
        a_exp = None
        for tb in range(4):
            a_exp = ac.emit(lambda e, tb=tb: e.activation(
                out=expw[:, 16 * tb:16 * (tb + 1)], in_=psc[:, 16 * tb:16 * (tb + 1)],
                func=AF.Exp, bias=negmax[:, tb:tb + 1], accum_out=sumexp[:, tb:tb + 1]),
                "ac" if tb == 3 else None)
        sc_bank_free = a_exp
        ve.wait("ac", a_exp)
        if ut > 0:
            ve.wait("ldg", wtm_store[4 * (ut - 1) + 3])
        vf(lambda e: e.reciprocal(out=rsum[:], in_=sumexp[:]))
        v_w = None
        for tb in range(4):
            v_w = ve.emit(lambda e, tb=tb: e.tensor_scalar(
                out=wtm[:, 16 * tb:16 * (tb + 1)], in0=expw[:, 16 * tb:16 * (tb + 1)],
                scalar1=rsum[:, tb:tb + 1], scalar2=None, op0=ALU.mult),
                "ve" if tb == 3 else None)
        if debug and ut == 0:
            gp.wait("ve", v_w)
            dma(gp, dbg["d_w"], RAP(wtm, 0, [[64, 128], [1, 64]]), sem="st")
        gp.wait("ve", v_w)
        for tb in range(4):
            t0r = 15 + 512 * ut + 128 * tb
            wtm_store[4 * ut + tb] = dma(
                gp, wsc_d[t0r:t0r + 128, 0:16], RAP(wtm, 16 * tb, [[64, 128], [1, 16]]),
                sem="ldg")

    def load_xtok(j):
        slot = j % XTOK_RING
        if j >= XTOK_RING:
            gp.wait("st", ffn_done[j - XTOK_RING])
        xtok_ld[j] = dma(gp, xtok[slot][:], x_d[7 + 128 * j:7 + 128 * j + 128, :], sem="ldg")

    def combine_block(B):
        t0 = 128 * B
        par = B % 2
        need_blk = min(B + 1, NTOK - 1)
        gp.wait("ldg", wtm_store[need_blk])
        gp.wait("ldg", zw_last)
        if B >= 2:
            gp.wait("pe", comb_mm[B - 2])
        dma(gp, pcA[par][0:64, 15:79],
            RAP(wsc_d.tensor, WSCW * (15 + t0), [[1, 64], [WSCW - 1, 64]]),
            sem="ldg", allow_slow_non_contiguous=True)
        dma(gp, pcB[par][64:128, 15:79],
            RAP(wsc_d.tensor, WSCW * (15 + t0 + 64), [[1, 64], [WSCW - 1, 64]]),
            sem="ldg", allow_slow_non_contiguous=True)
        dma(gp, pcX[par][64:79, 15:79],
            RAP(wsc_d.tensor, WSCW * (15 + t0) + 64, [[1, 15], [WSCW - 1, 64]]),
            sem="ldg", allow_slow_non_contiguous=True)
        ldY = dma(gp, pcY[par][0:15, 15:79],
                  RAP(wsc_d.tensor, WSCW * (15 + t0 + 64) + 64, [[1, 15], [WSCW - 1, 64]]),
                  sem="ldg", allow_slow_non_contiguous=True)
        if debug and B == 1:
            gp.wait("ldg", ldY)
            dma(gp, dbg["d_pcA"], pcA[par][:], sem="st")
            dma(gp, dbg["d_pcX"], pcX[par][64:79, :], sem="st")
            dma(gp, dbg["d_pcB"], pcB[par][64:128, :], sem="st")
            st_pc = dma(gp, dbg["d_pcY"], pcY[par][0:15, :], sem="st")
            ve.wait("st", st_pc)
        te.wait("ldg", ldY)
        te.wait("ve", gxn_done[B])
        te.wait("ve", gxn_done[min(B + 1, NUT - 1)])
        gB = gxn[B % GXN_RING]
        gB1 = gxn[(B + 1) % GXN_RING]
        pe_bank("pms0"); pe_bank("pms1")
        mm = None
        for hh in range(2):
            plist = [
                (pcA[par][0:64, 15:79], gB, 0, 0, True),
                (pcX[par][64:79, 15:79], gB, 64, 0, False),
                (pcB[par][64:128, 15:79], gB, 64, 64, True),
                (pcY[par][0:15, 15:79], gB1, 0, 64, False),
            ]
            for pi, (lh, rhs_t, rb, g0, stf) in enumerate(plist):
                rows = lh.shape[0]
                tp = (rb, g0) if (rb or g0) else None
                mm = te.emit(lambda e, lh=lh, rows=rows, rhs_t=rhs_t, rb=rb, g0=g0,
                             hh=hh, tp=tp, stf=stf: e.matmul(
                    pms[hh][g0:g0 + 64, :],
                    lh,
                    rhs_t[rb:rb + rows, 512 * hh:512 * (hh + 1)],
                    start=stf, stop=False, tile_position=tp),
                    "pe" if (hh == 1 and pi == 3) else None)
        comb_mm[B] = mm
        tslot = B % XTOK_RING
        ve.wait("pe", mm)
        if debug and B == 1:
            ve.emit(lambda e: e.tensor_copy(out=argb[0][:, 0:512], in_=pms[0][:]))
            v_att = ve.emit(lambda e: e.tensor_copy(out=argb[0][:, 512:1024], in_=pms[1][:]),
                            "ve")
            gp.wait("ve", v_att)
            st_att = dma(gp, dbg["d_att"], argb[0][:], sem="st")
            ve.wait("st", st_att)
        ve.wait("ldg", xtok_ld[B])
        vv = None
        for hh in range(2):
            vv = ve.emit(lambda e, hh=hh, ts=tslot: e.tensor_tensor(
                out=xtok[ts][:, 512 * hh:512 * (hh + 1)],
                in0=pms[hh][:, :], in1=xtok[ts][:, 512 * hh:512 * (hh + 1)],
                op=ALU.add), "ve" if hh == 1 else None)
        set_bank_free("pms0", "ve", vv); set_bank_free("pms1", "ve", vv)
        comb_done[B] = vv
        if debug:
            gp.wait("ve", vv)
            x2_dump[B] = dma(gp, dbg["d_x2"][128 * B:128 * (B + 1), :], xtok[tslot][:],
                             sem="st")

    def ffn_stats(j, gi):
        tslot = j % XTOK_RING
        vf(lambda e, ts=tslot, gi=gi: e.reduce_sum(out=st2[:, gi:gi + 1],
                                                   in_=xtok[ts][:], axis=AXX))
        ac.wait("ve", comb_done[j])
        a_sq2 = ac.emit(lambda e, ts=tslot, gi=gi: e.activation(
            out=xn2tm[:], in_=xtok[ts][:], func=AF.Square,
            accum_out=st2[:, 6 + gi:7 + gi]), "ac")
        return a_sq2

    def ffn_chain(n, a_last):
        ve.wait("ac", a_last)
        vf(lambda e: e.tensor_single_scalar(out=st2[:, 12:12 + n], in_=st2[:, 0:n],
                                            scalar=1.0 / H, op=ALU.mult))
        vf(lambda e: e.tensor_tensor(out=nsc[:, 28:28 + n], in0=st2[:, 12:12 + n],
                                     in1=st2[:, 12:12 + n], op=ALU.mult))
        vf(lambda e: e.tensor_single_scalar(out=st2[:, 18:18 + n], in_=st2[:, 6:6 + n],
                                            scalar=1.0 / H, op=ALU.mult))
        vf(lambda e: e.tensor_tensor(out=st2[:, 18:18 + n], in0=st2[:, 18:18 + n],
                                     in1=nsc[:, 28:28 + n], op=ALU.subtract))
        vf(lambda e: e.tensor_single_scalar(out=st2[:, 18:18 + n], in_=st2[:, 18:18 + n],
                                            scalar=EPS, op=ALU.add))
        newton_rsqrt(st2[:, 24:24 + n], st2[:, 18:18 + n], n)

    def ffn_rest(j, gi):
        tslot = j % XTOK_RING
        v_xn2 = ve.emit(lambda e, ts=tslot, gi=gi: e.tensor_scalar(
            out=xn2tm[:], in0=xtok[ts][:], scalar1=st2[:, 12 + gi:13 + gi],
            scalar2=st2[:, 24 + gi:25 + gi], op0=ALU.subtract, op1=ALU.mult), "ve")
        te.wait("ve", v_xn2)
        pe_bank("pmm2"); pe_bank("pmm3")
        t_last = None
        for c in range(HC):
            t_last = te.emit(lambda e, c=c: e.matmul(
                pmm[2 + c // 4][:, 128 * (c % 4):128 * (c % 4) + 128],
                xn2tm[:, 128 * c:128 * (c + 1)], idnb[:],
                start=True, stop=True), "pe" if c == 7 else None)
        ve.wait("pe", t_last)
        v_h2 = None
        for c in range(HC):
            v_h2 = ve.emit(lambda e, c=c: e.tensor_scalar(
                out=h2fm[:, 128 * c:128 * (c + 1)],
                in0=pmm[2 + c // 4][:, 128 * (c % 4):128 * (c % 4) + 128],
                scalar1=opcb[:, c:c + 1], scalar2=modfm[:, 24 + c:25 + c],
                op0=ALU.mult, op1=ALU.add), "ve" if c == 7 else None)
        set_bank_free("pmm2", "ve", v_h2); set_bank_free("pmm3", "ve", v_h2)
        if debug and j == 1:
            gp.wait("ve", v_h2)
            dma(gp, dbg["d_h2"], h2fm[:], sem="st")
        te.wait("ve", v_h2)
        if j == 0:
            te.wait("ve", v_sv)
            te.wait("pe", at_mark[7])
            te.wait("ve", v_bwt)
        pe_bank("pmm2")
        mm = None
        for g in range(4):
            for kc in range(HC):
                mm = te.emit(lambda e, g=g, kc=kc: e.matmul(
                    pmm[2][:, 128 * g:128 * (g + 1)],
                    alh[:, (kc * 4 + g) * 128:(kc * 4 + g) * 128 + 128],
                    h2fm[:, 128 * kc:128 * (kc + 1)],
                    start=(kc == 0), stop=(kc == 7)),
                    "pe" if (g == 3 and kc == 7) else None)
        ve.wait("pe", mm)
        v_mid = None
        for g in range(4):
            v_mid = ve.emit(lambda e, g=g: e.tensor_scalar(
                out=midb[:, 128 * g:128 * (g + 1)], in0=pmm[2][:, 128 * g:128 * (g + 1)],
                scalar1=sv[:, g:g + 1], scalar2=None, op0=ALU.mult), "ve" if g == 3 else None)
        set_bank_free("pmm2", "ve", v_mid)
        if debug and j == 1:
            gp.wait("ve", v_mid)
            dma(gp, dbg["d_mid"], midb[:], sem="st")
        v_gh = None
        for c in range(HC):
            v_gh = ve.emit(lambda e, c=c: e.tensor_scalar(
                out=xn2fm[:, 128 * c:128 * (c + 1)], in0=h2fm[:, 128 * c:128 * (c + 1)],
                scalar1=gbpr[:, c:c + 1], scalar2=None, op0=ALU.mult),
                "ve" if c == 7 else None)
        te.wait("ve", v_mid)
        pe_bank("pmm2"); pe_bank("pmm3")
        mm2 = None
        for hh in range(2):
            for g in range(4):
                mm2 = te.emit(lambda e, hh=hh, g=g: e.matmul(
                    pmm[2 + hh][:, :],
                    midb[:, 128 * g:128 * (g + 1)],
                    bwt[:, 1024 * g + 512 * hh:1024 * g + 512 * (hh + 1)],
                    start=(g == 0), stop=False))
        te.wait("ve", v_gh)
        for hh in range(2):
            for ci in range(4):
                c = 4 * hh + ci
                mm2 = te.emit(lambda e, hh=hh, ci=ci, c=c: e.matmul(
                    pmm[2 + hh][:, 128 * ci:128 * (ci + 1)],
                    xn2fm[:, 128 * c:128 * (c + 1)], idnb[:],
                    start=False, stop=(ci == 3)),
                    "pe" if (hh == 1 and ci == 3) else None)
        ffn_tr[j] = mm2
        ve.wait("pe", mm2)
        if debug and j in x2_dump:
            ve.wait("st", x2_dump[j])
        vv = None
        for hh in range(2):
            vv = ve.emit(lambda e, hh=hh, ts=tslot: e.tensor_tensor(
                out=xtok[ts][:, 512 * hh:512 * (hh + 1)], in0=pmm[2 + hh][:, :],
                in1=xtok[ts][:, 512 * hh:512 * (hh + 1)], op=ALU.add), "ve" if hh == 1 else None)
        set_bank_free("pmm2", "ve", vv); set_bank_free("pmm3", "ve", vv)
        gp.wait("ve", vv)
        ffn_done[j] = dma(gp, out_d[128 * j:128 * (j + 1), :], xtok[tslot][:], sem="st")

    next_comb = 0
    for ut in range(NUNIT):
        newt = [jj for jj in range(4 * ut, min(4 * ut + 5, NUT)) if jj not in tile_sc]
        sclo = newt[0] - 4 * ut
        a_last = None
        for jj in newt:
            a_last = ln1_load_stats(jj, jj - 4 * ut)
        ln1_stats_batch(sclo, newt[-1] - 4 * ut + 1, a_last)
        for jj in newt:
            ln1_norm(jj)
        build_hpad(ut)
        proj_q(ut)
        score_unit(ut)
        group = []
        while next_comb < NTOK and (512 * (ut + 1) >= 128 * next_comb + 143 or ut == NUNIT - 1):
            group.append(next_comb)
            next_comb += 1
        for B in group:
            load_xtok(B)
            combine_block(B)
            for jj in [p for p in pending_gxn if p - GXN_RING == B]:
                gxn_norm(jj)
                pending_gxn.remove(jj)
        a_last = None
        for gi, B in enumerate(group):
            a_last = ffn_stats(B, gi)
        if group:
            ffn_chain(len(group), a_last)
        for gi, B in enumerate(group):
            ffn_rest(B, gi)

    gp.wait("st", cnt["st"])
    gp.wait("ldg", cnt["ldg"])
    gp.wait("ld", cnt["ld"])
    gp.wait("xb", cnt["xb"])

    import contextlib
    with contextlib.ExitStack() as stk:
        for s in semnames:
            SEM[s] = stk.enter_context(nc.semaphore(s))
        with nc.Block() as block:
            @block.sync
            def _(e):
                for fi, f in enumerate(prog["sync"]):
                    try:
                        f(e)
                    except Exception as ex:
                        raise RuntimeError(f"sync op index {fi}: {ex}") from ex

            @block.gpsimd
            def _(e):
                for f in prog["gpsimd"]:
                    f(e)

            @block.tensor
            def _(e):
                for fi, f in enumerate(prog["tensor"]):
                    try:
                        f(e)
                    except Exception as ex:
                        raise RuntimeError(f"tensor op index {fi} line {tags['tensor'][fi]}: {ex}") from ex

            @block.vector
            def _(e):
                for f in prog["vector"]:
                    f(e)

            @block.scalar
            def _(e):
                for f in prog["scalar"]:
                    f(e)

    return nc, dbg


def make_in_maps(inputs):
    x = np.asarray(inputs["L0_attn_inputs"], dtype=np.float32)
    w1 = np.ascontiguousarray(inputs["L0_attn_w1"], dtype=np.float32)
    w2 = np.ascontiguousarray(inputs["L0_attn_w2"], dtype=np.float32)
    v = np.asarray(inputs["L0_attn_v"], dtype=np.float32).reshape(1024)
    Wr = np.asarray(inputs["L0_ffn_Wr"], dtype=np.float32)
    A = np.ascontiguousarray(np.asarray(inputs["L0_ffn_A"], dtype=np.float32))
    Bw = np.ascontiguousarray(np.asarray(inputs["L0_ffn_Bw"], dtype=np.float32))
    c = np.asarray(inputs["L0_c"], dtype=np.float32)
    wmod = np.ascontiguousarray(inputs["L0_wmod"], dtype=np.float32)

    v_r = np.ascontiguousarray(v.reshape(8, 128))
    wrt = np.ascontiguousarray(Wr.T)
    ident = np.eye(128, dtype=np.float32)
    onescol = np.ones((1, 128), dtype=np.float32)

    in_maps = []
    for core in range(8):
        b = core // 4
        seg = core % 4
        s0 = 2048 * seg
        lo = s0 - 7
        hi = s0 + 2048 + 8
        xs = np.zeros((UPAD, H), dtype=np.float32)
        a0, a1 = max(lo, 0), min(hi, 8192)
        xs[a0 - lo:a1 - lo, :] = x[b, a0:a1, :]
        mv = np.zeros((UPAD, 1), dtype=np.float32)
        mv[a0 - lo:a1 - lo, 0] = 1.0
        xg = np.ascontiguousarray(x[b, 0:9, :])
        in_maps.append({
            "x_slice": xs, "w1": w1, "w2": w2, "v_r": v_r, "wrt": wrt,
            "A": A, "Bw": Bw, "c_row": c[b:b + 1, :], "wmod": wmod,
            "mvec": mv, "xg": xg, "ident": ident, "onescol": onescol,
        })
    return in_maps


def kernel(**inputs):
    nc, _ = build_nc(debug=False)
    in_maps = make_in_maps(inputs)
    res = run_bass_kernel_spmd(nc, in_maps, core_ids=list(range(8)))
    out = np.zeros((2, 8192, 1024), dtype=np.float32)
    for core in range(8):
        b = core // 4
        s0 = 2048 * (core % 4)
        out[b, s0:s0 + 2048, :] = np.asarray(res.results[core]["out"])
    return out
